# version 08
# baseline (speedup 1.0000x reference)
import numpy as np
import ml_dtypes
from contextlib import ExitStack

import concourse.bass as bass
import concourse.mybir as mybir
from concourse.bass_utils import run_bass_kernel_spmd
from concourse.aot_env import allow_ici_all_to_all

F32 = mybir.dt.float32
BF16 = mybir.dt.bfloat16
I32 = mybir.dt.int32
AF = mybir.ActivationFunctionType
OP = mybir.AluOpType
AX = mybir.AxisListType.X

BF16NP = ml_dtypes.bfloat16

NCORES = 8
D = 2048
B = 4
S = 4096
SC = S // NCORES
TOK = B * SC
KC = D // 128


class Prog:

    def __init__(self, nc, stack):
        self.nc = nc
        self.stack = stack
        self.sems = {}
        self.cnt = {}

    def sem(self, name):
        if name not in self.sems:
            self.sems[name] = self.stack.enter_context(self.nc.semaphore(name))
            self.cnt[name] = 0
        return self.sems[name]

    def inc(self, inst, name, by=1):
        inst.then_inc(self.sem(name), by)
        self.cnt[name] += by
        return self.cnt[name]

    def wait(self, eng, name, val=None):
        eng.wait_ge(self.sem(name), self.cnt[name] if val is None else val)

    def wait_all(self, eng, names):
        for n in names:
            if self.cnt.get(n):
                eng.wait_ge(self.sems[n], self.cnt[n])

    def chain(self, eng, name, fn, *a, **k):
        if self.cnt[name]:
            eng.wait_ge(self.sems[name], self.cnt[name])
        inst = fn(*a, **k)
        return self.inc(inst, name)


def emit_ln(p, blk, x_t, out_bf, sums, mean, rstd, scratch, n=D):
    nc = p.nc
    with nc.Block() as b1:
        V0 = p.cnt["v"]
        S0 = p.cnt["s"]

        @b1.vector
        def _(v):
            p.chain(v, "v", v.reduce_sum, sums[:, 0:1], x_t[:, :], axis=AX)
            p.chain(v, "v", v.tensor_scalar, mean[:, :], sums[:, 0:1], 1.0 / n, None, op0=OP.mult)
            p.chain(v, "v", v.tensor_tensor, rstd[:, :], mean[:, :], mean[:, :], op=OP.mult)
            p.wait(v, "s", S0 + 1)
            p.chain(v, "v", v.scalar_tensor_tensor, rstd[:, :], sums[:, 1:2], 1.0 / n,
                    rstd[:, :], op0=OP.mult, op1=OP.subtract)
            p.chain(v, "v", v.tensor_scalar, rstd[:, :], rstd[:, :], 1e-5, None, op0=OP.add)
            p.chain(v, "v", v.reciprocal, rstd[:, :], rstd[:, :])
            p.chain(v, "v", v.tensor_scalar, mean[:, :], mean[:, :], -1.0, None, op0=OP.mult)
            p.wait(v, "s", S0 + 2)
            p.chain(v, "v", v.tensor_scalar, out_bf[:, :], x_t[:, :], mean[:, 0:1],
                    rstd[:, 0:1], op0=OP.add, op1=OP.mult)

        @b1.scalar
        def _(s):
            p.chain(s, "s", s.activation, scratch[:, 0:n], x_t[:, :], AF.Square,
                    accum_out=sums[:, 1:2])
            p.wait(s, "v", V0 + 6)
            p.chain(s, "s", s.activation, rstd[:, :], rstd[:, :], AF.Sqrt)


def emit_ident(p, ident, iota_i, iota_f, rowid):
    nc = p.nc
    with nc.Block() as blk:
        @blk.gpsimd
        def _(g):
            p.chain(g, "g", g.iota, iota_i[:, :], [[1, 128]], channel_multiplier=0)
            p.chain(g, "g", g.iota, iota_f[:, 0:1].bitcast(I32), [[1, 1]], channel_multiplier=1)
    with nc.Block() as blk:
        @blk.vector
        def _(v):
            p.chain(v, "v", v.tensor_copy, rowid[:, :], iota_f[:, 0:1].bitcast(I32))
            p.chain(v, "v", v.tensor_copy, iota_f[:, :], iota_i[:, :])
    with nc.Block() as blk:
        @blk.vector
        def _(v):
            p.chain(v, "v", v.tensor_scalar, ident[:, :], iota_f[:, :], rowid[:, 0:1],
                    None, op0=OP.is_equal)






def emit_lnT_store(p, ident, ln_bf, lnT, r0, ps_tr, stage):
    nc = p.nc
    with nc.Block() as blk:
        M0 = p.cnt["tr_mm"]
        E0 = p.cnt["tr_ev"]
        P0 = {0: p.cnt["dOa"], 1: p.cnt["dOb"]}

        @blk.tensor
        def _(t):
            for c in range(KC):
                if E0 + c - 1 > 0:
                    p.wait(t, "tr_ev", E0 + c - 1)
                inst = t.transpose(ps_tr[c % 2][:, :], ln_bf[:, c * 128:(c + 1) * 128],
                                   ident[:, :])
                p.inc(inst, "tr_mm")

        @blk.vector
        def _(v):
            for c in range(KC):
                p.wait(v, "tr_mm", M0 + c + 1)
                pw = P0[c % 2] + 16 * (c // 2)
                if pw > 0:
                    p.wait(v, "dOa" if c % 2 == 0 else "dOb", pw)
                inst = v.tensor_copy(stage[c % 2][:, 0:128], ps_tr[c % 2][:, :])
                p.inc(inst, "tr_ev")

        @blk.sync
        def _(sy):
            for c in range(KC):
                p.wait(sy, "tr_ev", E0 + c + 1)
                inst = sy.dma_start(lnT[c * 128:(c + 1) * 128, r0:r0 + 128],
                                    stage[c % 2][:, 0:128])
                p.inc(inst, "dOa" if c % 2 == 0 else "dOb", 16)










def emit_transpose_to_slab2(p, ident, src_bf, slab, t0, T, psB, nchunks=KC):
    nc = p.nc
    with nc.Block() as blk:
        E0 = p.cnt["tr_ev"]
        M0 = p.cnt["tr_mm"]

        @blk.tensor
        def _(t):
            for c in range(nchunks):
                if E0 + c - 1 > 0:
                    p.wait(t, "tr_ev", E0 + c - 1)
                inst = t.transpose(psB[c % 2][:, 0:128], src_bf[:, c * 128:(c + 1) * 128],
                                   ident[:, :])
                p.inc(inst, "tr_mm")

        @blk.vector
        def _(v):
            for c in range(nchunks):
                p.wait(v, "tr_mm", M0 + c + 1)
                inst = v.tensor_copy(slab[:, c * T + t0: c * T + t0 + 128], psB[c % 2][:, 0:128])
                p.inc(inst, "tr_ev")


def emit_moe(nc, p, pool, d):
    ln2T, x1s, wrTe, weT = d["ln2T"], d["x1s"], d["wrTe"], d["weT"]
    x2, lnT, c_accD = d["x2"], d["lnT"], d["c_accD"]
    brh_sb, eb_sb, ones_bf, cT_sb, c_col = d["brh_sb"], d["eb_sb"], d["ones_bf"], d["cT_sb"], d["c_col"]
    identF, ident, sums, mean, rstd = d["identF"], d["ident"], d["sums"], d["mean"], d["rstd"]
    cstage, psA, psB, psC = d["cstage"], d["psA"], d["psB"], d["psC"]
    ps_c = psC[0:1, 0:512]
    ps_ct = psC[:, 0:1]
    p.sem("es")

    pool.reset()
    ln2T_sb = pool.take(KC * TOK, BF16)
    base = pool.mark()
    lg4 = [pool.take(8 * 512, BF16) for _ in range(2)]
    tmp = pool.take(11 * 512, BF16)
    bc = pool.take(8 * 512, BF16)
    aw = pool.take(8 * 512, BF16)
    bg = pool.take(8 * 512, BF16)
    wslab = [pool.take(KC * 1024, BF16) for _ in range(2)]
    ps_cb = [psB[i].bitcast(F32)[0:1, 0:512] for i in range(4)]
    p.sem("cs")
    p.sem("gg")

    p.sem("dGa")
    p.sem("dGb")
    p.sem("dCa")
    p.sem("dCb")

    p.sem("dRa")
    p.sem("dRb")
    p.sem("dTa")
    p.sem("dTb")
    SPL = {"g": range(0, 16), "s": range(0, 0), "c": range(0, 0)}
    SLQ = {"g": ("dGa", "dGb"), "s": ("dRa", "dRb"), "c": ("dTa", "dTb")}
    G0 = {k: {b: p.cnt[SLQ[k][b]] for b in range(2)} for k in SLQ}

    def load_router_slab(eng, kind, oc):
        buf = oc % 2
        sm = SLQ[kind][buf]
        for c in SPL[kind]:
            inst = eng.dma_start(wslab[buf][:, c * 1024:(c + 1) * 1024],
                                 wrTe[c * 128:(c + 1) * 128, oc * 1024:(oc + 1) * 1024])
            p.inc(inst, sm, 16)

    def rslab_cnt(kind, oc):
        return G0[kind][oc % 2] + 16 * len(SPL[kind]) * (oc // 2 + 1)

    with nc.Block() as blk:
        @blk.sync
        def _(sy):
            p.wait_all(sy, ["dOa", "dOb", "dOx"])
            for c in range(KC):
                inst = sy.dma_start(ln2T_sb[:, c * TOK:(c + 1) * TOK],
                                    ln2T[c * 128:(c + 1) * 128, :])
                p.inc(inst, "dS", 16)
            load_router_slab(sy, "s", 0)
            p.wait(sy, "dS")

        @blk.scalar
        def _(sc):
            load_router_slab(sc, "c", 0)

        @blk.vector
        def _(v):
            p.chain(v, "v", v.memset, tmp[:, 0:1024], 0.0)

        @blk.gpsimd
        def _(g):
            load_router_slab(g, "g", 0)

    with nc.Block() as blk:
        @blk.gpsimd
        def _(g):
            inst = g.dma_start(c_accD.rearrange("a b -> (a b)").rearrange("(p f) -> p f", p=32),
                               tmp[0:32, 0:1024].bitcast(F32))
            p.inc(inst, "dG", 16)
            p.wait(g, "dG")

    NOC = 16
    NGT = 4 * NOC
    with nc.Block() as blk:
        M0 = p.cnt["mm"]
        EV0 = p.cnt["ev"]
        ES0 = p.cnt["es"]
        SG0 = p.cnt["sg"]
        CM0 = p.cnt["cm"]
        CS0 = p.cnt["cs"]
        GCa = {0: p.cnt["dCa"], 1: p.cnt["dCb"]}
        GG0 = p.cnt["gg"]
        va = {}
        cva = {}
        vtt = {}

        def evac_v_cnt(gt, e):
            return EV0 + 4 * gt + e // 2 + 1

        def evac_s_cnt(gt, e):
            return ES0 + 4 * gt + (e + 1) // 2

        @blk.sync
        def _(sy):
            if SPL["s"]:
                for oc in range(1, NOC):
                    p.wait(sy, "mm", M0 + 32 * (oc - 1))
                    load_router_slab(sy, "s", oc)


        @blk.vector
        def _(v):
            for gt in range(NGT):
                oc = gt // 4
                lgc = lg4[gt % 2]
                LG = lgc[:, 0:4096]
                BC = bc[:, 0:4096]
                AW = aw[:, 0:4096]

                def L(e_):
                    return lgc[:, e_ * 512:(e_ + 1) * 512]

                def T(i_):
                    return tmp[:, i_ * 512:(i_ + 1) * 512]

                for e in range(0, 8, 2):
                    p.wait(v, "mm", M0 + 8 * gt + e + 1)
                    if e == 0 and gt >= 2:
                        p.wait(v, "cm", CM0 + 8 * (gt - 2) + 8)
                    if p.cnt["ev"]:
                        v.wait_ge(p.sem("ev"), p.cnt["ev"])
                    inst = v.tensor_scalar(
                        L(e), psA[(8 * gt + e) % 2][:, :],
                        brh_sb[:, e * 16 + oc: e * 16 + oc + 1],
                        None, op0=OP.add)
                    p.inc(inst, "ev")
                p.wait(v, "es", evac_s_cnt(gt, 7))
                for pr in range(4):
                    p.chain(v, "v", v.tensor_tensor, T(pr), L(2 * pr), L(2 * pr + 1), op=OP.max)
                    p.chain(v, "v", v.tensor_tensor, T(4 + pr), L(2 * pr), L(2 * pr + 1), op=OP.min)
                for q in range(2):
                    p.chain(v, "v", v.tensor_tensor, T(8), T(2 * q), T(2 * q + 1), op=OP.min)
                    p.chain(v, "v", v.tensor_tensor, T(2 * q), T(2 * q), T(2 * q + 1), op=OP.max)
                    p.chain(v, "v", v.tensor_tensor, T(2 * q + 1), T(4 + 2 * q), T(5 + 2 * q), op=OP.max)
                    p.chain(v, "v", v.tensor_tensor, T(2 * q + 1), T(2 * q + 1), T(8), op=OP.max)
                p.chain(v, "v", v.tensor_tensor, T(8), T(0), T(2), op=OP.min)
                p.chain(v, "v", v.tensor_tensor, T(9), T(0), T(2), op=OP.max)
                p.chain(v, "v", v.tensor_tensor, T(10), T(1), T(3), op=OP.max)
                p.chain(v, "v", v.tensor_tensor, T(10), T(10), T(8), op=OP.max)
                p.chain(v, "v", v.tensor_tensor, T(8), T(9), T(10), op=OP.add)
                vtt[gt] = p.cnt["v"]
                p.chain(v, "v", v.tensor_copy, bc[:, 0:512], T(8))
                p.chain(v, "v", v.tensor_copy, bc[:, 512:1024], bc[:, 0:512])
                p.chain(v, "v", v.tensor_copy, bc[:, 1024:2048], bc[:, 0:1024])
                p.chain(v, "v", v.tensor_copy, bc[:, 2048:4096], bc[:, 0:2048])
                p.chain(v, "v", v.scalar_tensor_tensor, AW, LG, 2.0, BC,
                        op0=OP.mult, op1=OP.subtract)
                va[gt] = p.cnt["v"]
                p.wait(v, "gg", GG0 + 4 * (gt + 1))
                p.chain(v, "v", v.tensor_tensor, bg[:, 0:4096], LG, bg[:, 0:4096],
                        op=OP.is_ge)
                p.wait(v, "sg", SG0 + gt + 1)
                p.chain(v, "v", v.tensor_tensor, LG, bg[:, 0:4096], AW, op=OP.mult)
                cva[gt] = p.cnt["v"]

        @blk.gpsimd
        def _(g):
            def accums(tb):
                for e in range(8):
                    p.wait(g, "cs", CS0 + 8 * tb + e + 1)
                    inst = g.dma_start(
                        c_accD[e:e + 1, (tb % 4) * 512:(tb % 4 + 1) * 512],
                        cstage[e % 2][:, :], accum_op=OP.add)
                    p.inc(inst, "dCa" if e % 2 == 0 else "dCb", 16)
            for gt in range(NGT):
                oc = gt // 4
                if gt % 4 == 0 and oc + 1 < NOC:
                    p.wait(g, "mm", M0 + 32 * oc)
                    load_router_slab(g, "g", oc + 1)
                lgc = lg4[gt % 2]
                p.wait(g, "v", vtt[gt])
                if gt >= 1:
                    p.wait(g, "v", cva[gt - 1])
                p.chain(g, "gg", g.tensor_copy, bg[:, 0:512], tmp[:, 10 * 512:11 * 512])
                p.chain(g, "gg", g.tensor_copy, bg[:, 512:1024], bg[:, 0:512])
                p.chain(g, "gg", g.tensor_copy, bg[:, 1024:2048], bg[:, 0:1024])
                p.chain(g, "gg", g.tensor_copy, bg[:, 2048:4096], bg[:, 0:2048])
                if gt >= 1:
                    accums(gt - 1)
            accums(NGT - 1)

        @blk.scalar
        def _(sc):
            def cstage_evacs(tb):
                for e in range(8):
                    gdx = 8 * tb + e
                    p.wait(sc, "cm", CM0 + gdx + 1)
                    sm = "dCa" if e % 2 == 0 else "dCb"
                    tgt = GCa[e % 2] + (16 * ((gdx - 2) // 2 + 1) if gdx >= 2 else 0)
                    if tgt:
                        p.wait(sc, sm, tgt)
                    inst = sc.copy(cstage[e % 2][:, :], ps_cb[gdx % 4])
                    p.inc(inst, "cs")
            for gt in range(NGT):
                oc = gt // 4
                lgc = lg4[gt % 2]
                for e in range(1, 8, 2):
                    p.wait(sc, "mm", M0 + 8 * gt + e + 1)
                    if e == 1 and gt >= 2:
                        p.wait(sc, "cm", CM0 + 8 * (gt - 2) + 8)
                    inst = sc.add(
                        lgc[:, e * 512:(e + 1) * 512],
                        psA[(8 * gt + e) % 2][:, :],
                        brh_sb[:, e * 16 + oc: e * 16 + oc + 1])
                    p.inc(inst, "es")
                    if e == 1 and gt % 4 == 0 and oc + 1 < NOC and SPL["c"]:
                        load_router_slab(sc, "c", oc + 1)
                if gt >= 1:
                    cstage_evacs(gt - 1)
                p.wait(sc, "v", va[gt])
                inst = sc.activation(aw[:, 0:4096], aw[:, 0:4096], AF.Sigmoid)
                p.inc(inst, "sg")
            cstage_evacs(NGT - 1)

        @blk.tensor
        def _(t):
            def ones(tb):
                lgc = lg4[tb % 2]
                p.wait(t, "v", cva[tb])
                for e in range(8):
                    gdx = 8 * tb + e
                    csw = CS0 + gdx - 3 if gdx >= 4 else CS0
                    if csw > 0:
                        p.wait(t, "cs", csw)
                    inst = t.matmul(ps_cb[gdx % 4], ones_bf[:, :],
                                    lgc[:, e * 512:(e + 1) * 512],
                                    start=True, stop=True)
                    p.inc(inst, "cm")
            for gt in range(NGT):
                oc = gt // 4
                if gt % 4 == 0:
                    for kind in ("g", "s", "c"):
                        if SPL[kind]:
                            t.wait_ge(p.sem(SLQ[kind][oc % 2]), rslab_cnt(kind, oc))
                for e in range(8):
                    j8 = 8 * gt + e
                    if j8 >= 2:
                        pe = j8 - 2
                        pt, pee = pe // 8, pe % 8
                        if pee % 2 == 0:
                            p.wait(t, "ev", evac_v_cnt(pt, pee))
                        else:
                            p.wait(t, "es", evac_s_cnt(pt, pee))
                    for c in range(KC):
                        inst = t.matmul(
                            psA[j8 % 2][:, :],
                            wslab[oc % 2][:, c * 1024 + e * 128: c * 1024 + (e + 1) * 128],
                            ln2T_sb[:, c * TOK + (gt % 4) * 512: c * TOK + (gt % 4 + 1) * 512],
                            start=(c == 0), stop=(c == KC - 1))
                    p.inc(inst, "mm")
                if gt >= 1:
                    ones(gt - 1)
            ones(NGT - 1)

    pool.reset(base)
    arena = pool.take(KC * 512, F32)
    we_sb = [pool.take(KC * 512, BF16) for _ in range(2)]
    c_row = pool.take(TOK, F32)[0:1, :]
    aslot2 = [pool.take(512, F32) for _ in range(2)]
    sslot2 = [pool.take(512, F32) for _ in range(2)]

    with nc.Block() as blk:
        @blk.sync
        def _(sy):
            p.wait_all(sy, ["dCa", "dCb"])
    for e in range(8):
        with nc.Block() as blk:
            @blk.sync
            def _(sy):
                inst = sy.dma_start(c_row[:, :], c_accD[e:e + 1, :])
                p.inc(inst, "dS", 16)
                p.wait(sy, "dS")

            @blk.gpsimd
            def _(g):
                inst = g.dma_start(cT_sb[e:e + 1, :], c_accD[e:e + 1, :])
                p.inc(inst, "dG", 16)
                p.wait(g, "dG")
        with nc.Block() as blk:
            M0 = p.cnt["tr_mm"]
            E0 = p.cnt["tr_ev"]

            @blk.tensor
            def _(t):
                for tc in range(16):
                    if E0 + tc > 0:
                        p.wait(t, "tr_ev", E0 + tc)
                    inst = t.transpose(ps_ct, c_row[0:1, tc * 128:(tc + 1) * 128],
                                       identF[0:1, 0:1])
                    p.inc(inst, "tr_mm")

            @blk.vector
            def _(v):
                for tc in range(16):
                    p.wait(v, "tr_mm", M0 + tc + 1)
                    inst = v.tensor_copy(c_col[:, tc * 8 + e: tc * 8 + e + 1], ps_ct)
                    p.inc(inst, "tr_ev")

    E0Q = {k: {b: p.cnt[SLQ[k][b]] for b in range(2)} for k in SLQ}

    def load_we(eng, kind, k):
        oc4, e = k // 8, k % 8
        sm = SLQ[kind][k % 2]
        for c in SPL[kind]:
            inst = eng.dma_start(
                we_sb[k % 2][:, c * 512:(c + 1) * 512],
                weT[c * 128:(c + 1) * 128, e * D + oc4 * 512: e * D + (oc4 + 1) * 512])
            p.inc(inst, sm, 16)

    def weslab_cnt(kind, k):
        return E0Q[kind][k % 2] + 16 * len(SPL[kind]) * (k // 2 + 1)

    with nc.Block() as blk:
        @blk.gpsimd
        def _(g):
            load_we(g, "g", 0)

        @blk.sync
        def _(sy):
            load_we(sy, "s", 0)

        @blk.scalar
        def _(sc):
            load_we(sc, "c", 0)

    for oc4 in range(4):
        with nc.Block() as blk:
            M0 = p.cnt["mm"]
            E0 = p.cnt["ev"]

            @blk.tensor
            def _(t):
                for tt in range(16):
                    if E0 + tt - 1 > 0:
                        p.wait(t, "ev", E0 + tt - 1)
                    inst = t.matmul(psA[tt % 2][:, :], cT_sb[:, tt * 128:(tt + 1) * 128],
                                    eb_sb[:, oc4 * 512:(oc4 + 1) * 512],
                                    start=True, stop=True)
                    p.inc(inst, "mm")

            @blk.vector
            def _(v):
                for tt in range(16):
                    p.wait(v, "mm", M0 + tt + 1)
                    inst = v.tensor_copy(arena[:, tt * 512:(tt + 1) * 512], psA[tt % 2][:, :])
                    p.inc(inst, "ev")
        for e in range(8):
            with nc.Block() as blk:
                M0 = p.cnt["mm"]
                E0 = p.cnt["ev"]
                kslab = 8 * oc4 + e

                @blk.gpsimd
                def _(g):
                    if kslab + 1 < 32:
                        load_we(g, "g", kslab + 1)

                @blk.sync
                def _(sy):
                    if kslab + 1 < 32:
                        load_we(sy, "s", kslab + 1)

                @blk.scalar
                def _(sc):
                    if kslab + 1 < 32:
                        load_we(sc, "c", kslab + 1)

                @blk.tensor
                def _(t):
                    for kind in ("g", "s", "c"):
                        if SPL[kind]:
                            t.wait_ge(p.sem(SLQ[kind][kslab % 2]), weslab_cnt(kind, kslab))
                    for tt in range(16):
                        if E0 + tt - 1 > 0:
                            p.wait(t, "ev", E0 + tt - 1)
                        for c in range(KC):
                            inst = t.matmul(
                                psA[tt % 2][:, :],
                                ln2T_sb[:, c * TOK + tt * 128: c * TOK + (tt + 1) * 128],
                                we_sb[e % 2][:, c * 512:(c + 1) * 512],
                                start=(c == 0), stop=(c == KC - 1))
                        p.inc(inst, "mm")

                @blk.vector
                def _(v):
                    for tt in range(16):
                        p.wait(v, "mm", M0 + tt + 1)
                        inst = v.scalar_tensor_tensor(
                            arena[:, tt * 512:(tt + 1) * 512], psA[tt % 2][:, :],
                            c_col[:, tt * 8 + e: tt * 8 + e + 1],
                            arena[:, tt * 512:(tt + 1) * 512],
                            op0=OP.mult, op1=OP.add)
                        p.inc(inst, "ev")
        p.sem("dXa")
        p.sem("dXb")
        with nc.Block() as blk:
            DX0 = {0: p.cnt["dXa"], 1: p.cnt["dXb"]}
            EV0 = p.cnt["ev"]
            PA0 = {0: p.cnt["dAa"], 1: p.cnt["dAb"]}

            @blk.sync
            def _(sy):
                for tt in range(16):
                    if tt >= 2:
                        p.wait(sy, "ev", EV0 + tt - 1)
                    inst = sy.dma_start(aslot2[tt % 2][:, :],
                                        x1s[tt * 128:(tt + 1) * 128, oc4 * 512:(oc4 + 1) * 512])
                    p.inc(inst, "dXa" if tt % 2 == 0 else "dXb", 16)

            @blk.vector
            def _(v):
                for tt in range(16):
                    p.wait(v, "dXa" if tt % 2 == 0 else "dXb",
                           DX0[tt % 2] + 16 * (tt // 2 + 1))
                    pw = PA0[tt % 2] + 16 * (tt // 2)
                    if pw > 0:
                        p.wait(v, "dAa" if tt % 2 == 0 else "dAb", pw)
                    inst = v.tensor_tensor(sslot2[tt % 2][:, :],
                                           arena[:, tt * 512:(tt + 1) * 512],
                                           aslot2[tt % 2][:, :], op=OP.add)
                    p.inc(inst, "ev")

            @blk.scalar
            def _(s):
                for tt in range(16):
                    p.wait(s, "ev", EV0 + tt + 1)
                    inst = s.dma_start(x2[tt * 128:(tt + 1) * 128, oc4 * 512:(oc4 + 1) * 512],
                                       sslot2[tt % 2][:, :])
                    p.inc(inst, "dAa" if tt % 2 == 0 else "dAb", 16)

    pool.reset(base)
    x2_t = pool.take(D, F32)
    scratch = pool.take(D, F32)
    ln_bf = pool.take(D, BF16)
    stage3 = [pool.take(128, BF16) for _ in range(2)]
    with nc.Block() as blk:
        @blk.gpsimd
        def _(g):
            p.wait_all(g, ["dAa", "dAb"])
    for tt in range(TOK // 128):
        r0 = tt * 128
        with nc.Block() as blk:
            @blk.sync
            def _(sy):
                inst = sy.dma_start(x2_t[:, :], x2[r0:r0 + 128, :])
                p.inc(inst, "dS", 16)
                p.wait(sy, "dS")
        emit_ln(p, None, x2_t, ln_bf, sums, mean, rstd, scratch)
        emit_lnT_store(p, ident, ln_bf, lnT, r0, [psB[0][:, 0:128], psB[1][:, 0:128]], stage3)


def emit_l1attn(nc, p, pool, d):
    ln3T, wqkvT, ao1 = d["ln3T"], d["wqkvT"], d["ao1"]
    bdmask, ident = d["bdmask"], d["ident"]
    psA, psB, psO, psC = d["psA"], d["psB"], d["psO"], d["psC"]
    ps_s = [psA[0][:, 0:128], psA[1][:, 0:128]]
    ps_tr = [psB[0][:, 0:128], psB[1][:, 0:128]]
    ps_o = [psO[:, 0:128], psC[:, 0:128]]

    pool.reset()
    ln3_sb = pool.take(KC * TOK, BF16)
    ltmp = pool.take(TOK, BF16)
    w_sb = pool.take(KC * 768, BF16)
    qkT_sb = pool.take(5 * TOK, BF16)
    v_sb = pool.take(16 * 128, BF16)
    s_t = [pool.take(128, F32) for _ in range(2)]
    p_t = [pool.take(128, BF16) for _ in range(2)]
    pn_t = [pool.take(128, BF16) for _ in range(2)]
    pT_t = [pool.take(128, BF16) for _ in range(2)]
    negmax = pool.take(1, F32)
    rowsum = pool.take(1, F32)
    rinv = pool.take(1, F32)
    stage4 = [pool.take(128, BF16) for _ in range(2)]

    ao1p = ao1.rearrange("(b t) d -> t b d", b=B)

    for c in range(KC):
        with nc.Block() as blk:
            @blk.sync
            def _(sy):
                if c == 0:
                    p.wait_all(sy, ["dOa", "dOb", "dOx"])
                inst = sy.dma_start(ltmp[:, :], ln3T[c * 128:(c + 1) * 128, :])
                p.inc(inst, "dS", 16)
                p.wait(sy, "dS")
        with nc.Block() as blk:
            @blk.vector
            def _(v):
                p.chain(v, "v", v.tensor_copy,
                        ln3_sb[:, c * TOK:(c + 1) * TOK].rearrange("p (t b) -> p t b", b=B),
                        ltmp[:, :].rearrange("p (b t) -> p t b", b=B))

    def rhs_jb(c, t4):
        return ln3_sb[:, c * TOK + t4 * 512: c * TOK + (t4 + 1) * 512]

    def lhs_jb(c, tt):
        return ln3_sb[:, c * TOK + tt * 128: c * TOK + (tt + 1) * 128]

    SCALE = 1.0 / np.sqrt(128.0)
    for g_ in range(4):
        with nc.Block() as blk:
            @blk.gpsimd
            def _(g):
                for c in range(KC):
                    inst = g.dma_start(w_sb[:, c * 768:(c + 1) * 768],
                                       wqkvT[c * 128:(c + 1) * 128, g_ * 768:(g_ + 1) * 768])
                    p.inc(inst, "dG", 16)
                p.wait(g, "dG")
        with nc.Block() as blk:
            M0 = p.cnt["mm"]
            E0 = p.cnt["ev"]

            @blk.tensor
            def _(t):
                for j, (ch, t4) in enumerate([(c_, t_) for c_ in range(5) for t_ in range(4)]):
                    if E0 + j - 1 > 0:
                        p.wait(t, "ev", E0 + j - 1)
                    for c in range(KC):
                        inst = t.matmul(
                            psA[j % 2][:, :],
                            w_sb[:, c * 768 + ch * 128: c * 768 + (ch + 1) * 128],
                            rhs_jb(c, t4),
                            start=(c == 0), stop=(c == KC - 1))
                    p.inc(inst, "mm")

            @blk.vector
            def _(v):
                for j, (ch, t4) in enumerate([(c_, t_) for c_ in range(5) for t_ in range(4)]):
                    p.wait(v, "mm", M0 + j + 1)
                    inst = v.tensor_copy(
                        qkT_sb[:, ch * TOK + t4 * 512: ch * TOK + (t4 + 1) * 512],
                        psA[j % 2][:, :])
                    p.inc(inst, "ev")
        with nc.Block() as blk:
            M0 = p.cnt["mm"]
            E0 = p.cnt["ev"]

            @blk.tensor
            def _(t):
                for tt in range(16):
                    if E0 + tt - 1 > 0:
                        p.wait(t, "ev", E0 + tt - 1)
                    for c in range(KC):
                        inst = t.matmul(
                            psA[tt % 2][:, 0:128],
                            lhs_jb(c, tt),
                            w_sb[:, c * 768 + 640: c * 768 + 768],
                            start=(c == 0), stop=(c == KC - 1))
                    p.inc(inst, "mm")

            @blk.vector
            def _(v):
                for tt in range(16):
                    p.wait(v, "mm", M0 + tt + 1)
                    inst = v.tensor_copy(v_sb[:, tt * 128:(tt + 1) * 128],
                                         psA[tt % 2][:, 0:128])
                    p.inc(inst, "ev")
        for tt in range(16):
            with nc.Block() as blk:
                V0 = p.cnt["v"]
                S0 = p.cnt["s"]
                T0 = p.cnt["t"]
                P0E = {0: p.cnt["dOa"], 1: p.cnt["dOb"]}

                @blk.tensor
                def _(t):
                    for hh in range(4):
                        if hh >= 2:
                            p.wait(t, "v", V0 + 6 * (hh - 2) + 1)
                        inst = t.matmul(ps_s[hh % 2],
                                        qkT_sb[:, hh * TOK + tt * 128: hh * TOK + (tt + 1) * 128],
                                        qkT_sb[:, 4 * TOK + tt * 128: 4 * TOK + (tt + 1) * 128],
                                        start=True, stop=True)
                        p.inc(inst, "t")
                        p.wait(t, "v", V0 + 6 * hh + 4)
                        if hh >= 2:
                            p.wait(t, "v", V0 + 6 * (hh - 2) + 5)
                        inst = t.transpose(ps_tr[hh % 2], pn_t[hh % 2][:, :], ident[:, :])
                        p.inc(inst, "t")
                        p.wait(t, "v", V0 + 6 * hh + 5)
                        if hh >= 2:
                            p.wait(t, "v", V0 + 6 * (hh - 2) + 6)
                        inst = t.matmul(ps_o[hh % 2], pT_t[hh % 2][:, :],
                                        v_sb[:, tt * 128:(tt + 1) * 128],
                                        start=True, stop=True)
                        p.inc(inst, "t")

                @blk.vector
                def _(v):
                    for hh in range(4):
                        p.wait(v, "t", T0 + 3 * hh + 1)
                        if p.cnt["v"]:
                            v.wait_ge(p.sem("v"), p.cnt["v"])
                        inst = v.scalar_tensor_tensor(s_t[hh % 2][:, :], ps_s[hh % 2],
                                                      SCALE, bdmask[:, :],
                                                      op0=OP.mult, op1=OP.add)
                        p.inc(inst, "v")
                        if hh >= 1:
                            p.wait(v, "s", S0 + hh)
                        v.wait_ge(p.sem("v"), p.cnt["v"])
                        inst = v.reduce_max(negmax[:, :], s_t[hh % 2][:, :], axis=AX,
                                            negate=True)
                        p.inc(inst, "v")
                        p.wait(v, "s", S0 + hh + 1)
                        v.wait_ge(p.sem("v"), p.cnt["v"])
                        inst = v.reciprocal(rinv[:, :], rowsum[:, :])
                        p.inc(inst, "v")
                        v.wait_ge(p.sem("v"), p.cnt["v"])
                        inst = v.tensor_scalar(pn_t[hh % 2][:, :], p_t[hh % 2][:, :],
                                               rinv[:, 0:1], None, op0=OP.mult)
                        p.inc(inst, "v")
                        p.wait(v, "t", T0 + 3 * hh + 2)
                        v.wait_ge(p.sem("v"), p.cnt["v"])
                        inst = v.tensor_copy(pT_t[hh % 2][:, :], ps_tr[hh % 2])
                        p.inc(inst, "v")
                        p.wait(v, "t", T0 + 3 * hh + 3)
                        pw = P0E[hh % 2] + 16 * (hh // 2)
                        if pw > 0:
                            p.wait(v, "dOa" if hh % 2 == 0 else "dOb", pw)
                        v.wait_ge(p.sem("v"), p.cnt["v"])
                        inst = v.tensor_copy(stage4[hh % 2][:, :], ps_o[hh % 2])
                        p.inc(inst, "v")

                @blk.scalar
                def _(s):
                    for hh in range(4):
                        p.wait(s, "v", V0 + 6 * hh + 2)
                        if hh >= 2:
                            p.wait(s, "v", V0 + 6 * (hh - 2) + 4)
                        inst = s.activation(p_t[hh % 2][:, :], s_t[hh % 2][:, :], AF.Exp,
                                            bias=negmax[:, 0:1], accum_out=rowsum[:, :])
                        p.inc(inst, "s")

                @blk.sync
                def _(sy):
                    for hh in range(4):
                        p.wait(sy, "v", V0 + 6 * hh + 6)
                        h = 4 * g_ + hh
                        inst = sy.dma_start(
                            ao1p[tt * 32:(tt + 1) * 32, :, h * 128:(h + 1) * 128],
                            stage4[hh % 2][:, :])
                        p.inc(inst, "dOa" if hh % 2 == 0 else "dOb", 16)


def emit_l1ffn(nc, p, pool, d):
    ln4T, x3s, wgG, wuG, wdG, x4 = (d["ln4T"], d["x3s"], d["wgG"], d["wuG"],
                                    d["wdG"], d["x4"])
    psA, psB = d["psA"], d["psB"]
    psH = [psA[0][:, :], psA[1][:, :],
           psB[2][:, :].bitcast(F32), psB[3][:, :].bitcast(F32)]
    TH = 1024
    pool.reset()
    ln4h = pool.take(KC * TH, BF16)
    y_sb = pool.take(8 * D, F32)
    gu_sb = [pool.take(KC * 512, BF16) for _ in range(2)]
    wd_sb = pool.take(4 * D, BF16)
    g_sb = pool.take(4 * TH, F32)
    u_sb = pool.take(4 * TH, F32)
    tmp = pool.take(4 * TH, F32)
    inter = u_sb[:, 0:2 * TH].bitcast(BF16)

    p.sem("es")
    bank_last = [None] * 4

    p.sem("dSg")
    p.sem("dSu")
    p.sem("dSw")

    def load_gu(sy, wsrc, nb, buf):
        sm = "dSg" if buf == 0 else "dSu"
        rb = (nb // 4) * D
        nl0 = (nb % 4) * 512
        for c in range(KC):
            inst = sy.dma_start(gu_sb[buf][:, c * 512:(c + 1) * 512],
                                wsrc[rb + c * 128:rb + (c + 1) * 128, nl0:nl0 + 512])
            p.inc(inst, sm, 16)
        return p.cnt[sm]

    def load_wd(sy, nb):
        n0 = nb * 512
        for nc4 in range(4):
            inst = sy.dma_start(wd_sb[:, nc4 * D:(nc4 + 1) * D],
                                wdG[n0 + nc4 * 128:n0 + (nc4 + 1) * 128, :])
            p.inc(inst, "dSw", 16)
        return p.cnt["dSw"]

    with nc.Block() as blk:
        @blk.gpsimd
        def _(g):
            p.wait(g, "cc", 3)
            p.wait_all(g, ["dOa", "dOb", "dOx"])

    gate_dS = [None]
    for th in range(2):
        t0 = th * TH
        with nc.Block() as blk:
            @blk.sync
            def _(sy):
                for c in range(KC):
                    inst = sy.dma_start(ln4h[:, c * TH:(c + 1) * TH],
                                        ln4T[c * 128:(c + 1) * 128, t0:t0 + TH])
                    p.inc(inst, "dS", 16)
                if th == 0:
                    gate_dS[0] = load_gu(sy, wgG, 0, 0)
                p.wait(sy, "dS")
        for nb in range(32):
            jobs8 = [(a, b_) for a in range(4) for b_ in range(2)]
            for i in range(4):
                bank_last[i] = None
            with nc.Block() as blk:
                M0 = p.cnt["mm"]
                EV0 = p.cnt["ev"]
                ES0 = p.cnt["es"]
                G0 = p.cnt["g"]
                up_dS = [None]

                @blk.sync
                def _(sy):
                    up_dS[0] = load_gu(sy, wuG, nb, 1)

                @blk.tensor
                def _(t):
                    t.wait_ge(p.sem("dSg"), gate_dS[0])
                    for j, (nc4, t4) in enumerate(jobs8):
                        if bank_last[j % 4] is not None:
                            p.wait(t, *bank_last[j % 4])
                        for c in range(KC):
                            inst = t.matmul(
                                psH[j % 4],
                                gu_sb[0][:, c * 512 + nc4 * 128: c * 512 + (nc4 + 1) * 128],
                                ln4h[:, c * TH + t4 * 512: c * TH + (t4 + 1) * 512],
                                start=(c == 0), stop=(c == KC - 1))
                        p.inc(inst, "mm")
                        bank_last[j % 4] = ("ev" if j % 2 == 0 else "es",
                                            (EV0 if j % 2 == 0 else ES0) + j // 2 + 1)

                @blk.vector
                def _(v):
                    for j, (nc4, t4) in enumerate(jobs8):
                        if j % 2 == 0:
                            p.wait(v, "mm", M0 + j + 1)
                            inst = v.tensor_copy(
                                g_sb[:, nc4 * TH + t4 * 512: nc4 * TH + (t4 + 1) * 512],
                                psH[j % 4])
                            p.inc(inst, "ev")
                    p.wait(v, "g", G0 + 2)
                    p.chain(v, "v", v.tensor_tensor, tmp[:, :], tmp[:, :], g_sb[:, :], op=OP.mult)

                @blk.scalar
                def _(sc):
                    for j, (nc4, t4) in enumerate(jobs8):
                        if j % 2 == 1:
                            p.wait(sc, "mm", M0 + j + 1)
                            inst = sc.copy(
                                g_sb[:, nc4 * TH + t4 * 512: nc4 * TH + (t4 + 1) * 512],
                                psH[j % 4])
                            p.inc(inst, "es")

                @blk.gpsimd
                def _(g):
                    p.wait(g, "ev", EV0 + 4)
                    p.wait(g, "es", ES0 + 4)
                    p.chain(g, "g", g.tensor_tensor, tmp[:, :], g_sb[:, :], g_sb[:, :], op=OP.mult)
                    p.chain(g, "g", g.tensor_scalar, tmp[:, :], tmp[:, :], 0.044715, 1.0,
                            op0=OP.mult, op1=OP.add)
            for i in range(4):
                bank_last[i] = None
            with nc.Block() as blk:
                M0 = p.cnt["mm"]
                EV0 = p.cnt["ev"]
                ES0 = p.cnt["es"]
                V0 = p.cnt["v"]
                S0 = p.cnt["s"]
                upl = up_dS[0]
                wd_dS = [None]
                jobs32 = [(a, b_) for a in range(8) for b_ in range(4)]
                sched = {}
                nev, nes = 0, 0
                for j in range(8):
                    if j % 2 == 0:
                        nev += 1
                        sched[j] = ("ev", EV0 + nev)
                    else:
                        nes += 1
                        sched[j] = ("es", ES0 + nes)
                for j32 in range(32):
                    nev += 1
                    sched[8 + j32] = ("ev", EV0 + nev)

                @blk.sync
                def _(sy):
                    wd_dS[0] = load_wd(sy, nb)
                    if nb + 1 < 32:
                        gate_dS[0] = load_gu(sy, wgG, nb + 1, 0)
                    elif th == 0:
                        gate_dS[0] = load_gu(sy, wgG, 0, 0)

                @blk.scalar
                def _(sc):
                    inst = sc.activation(tmp[:, :], tmp[:, :], AF.Tanh,
                                         scale=0.7978845608028654)
                    p.inc(inst, "s")
                    p.wait(sc, "s", S0 + 1)
                    inst = sc.add(tmp[:, :], tmp[:, :], 1.0)
                    p.inc(inst, "s")
                    for j, (nc4, t4) in enumerate(jobs8):
                        if j % 2 == 1:
                            p.wait(sc, "mm", M0 + j + 1)
                            inst = sc.copy(
                                u_sb[:, nc4 * TH + t4 * 512: nc4 * TH + (t4 + 1) * 512],
                                psH[j % 4])
                            p.inc(inst, "es")

                @blk.tensor
                def _(t):
                    t.wait_ge(p.sem("dSu"), upl)
                    for j, (nc4, t4) in enumerate(jobs8):
                        if bank_last[j % 4] is not None:
                            p.wait(t, *bank_last[j % 4])
                        for c in range(KC):
                            inst = t.matmul(
                                psH[j % 4],
                                gu_sb[1][:, c * 512 + nc4 * 128: c * 512 + (nc4 + 1) * 128],
                                ln4h[:, c * TH + t4 * 512: c * TH + (t4 + 1) * 512],
                                start=(c == 0), stop=(c == KC - 1))
                        p.inc(inst, "mm")
                        bank_last[j % 4] = sched[j]
                    p.wait(t, "v", V0 + 2)
                    t.wait_ge(p.sem("dSw"), wd_dS[0])
                    for j32, (tt, hc) in enumerate(jobs32):
                        J = 8 + j32
                        if bank_last[J % 4] is not None:
                            p.wait(t, *bank_last[J % 4])
                        for nc4 in range(4):
                            inst = t.matmul(
                                psH[J % 4],
                                inter[:, nc4 * TH + tt * 128: nc4 * TH + (tt + 1) * 128],
                                wd_sb[:, nc4 * D + hc * 512: nc4 * D + (hc + 1) * 512],
                                start=(nc4 == 0), stop=(nc4 == 3))
                        p.inc(inst, "mm")
                        bank_last[J % 4] = sched[J]

                @blk.vector
                def _(v):
                    for j, (nc4, t4) in enumerate(jobs8):
                        if j % 2 == 0:
                            p.wait(v, "mm", M0 + j + 1)
                            inst = v.tensor_copy(
                                u_sb[:, nc4 * TH + t4 * 512: nc4 * TH + (t4 + 1) * 512],
                                psH[j % 4])
                            p.inc(inst, "ev")
                    p.wait(v, "es", ES0 + 4)
                    p.chain(v, "v", v.tensor_tensor, g_sb[:, :], g_sb[:, :], u_sb[:, :], op=OP.mult)
                    p.wait(v, "s", S0 + 2)
                    p.chain(v, "v", v.tensor_tensor, inter[:, :], g_sb[:, :], tmp[:, :], op=OP.mult)
                    for j32, (tt, hc) in enumerate(jobs32):
                        J = 8 + j32
                        ysl = y_sb[:, tt * D + hc * 512: tt * D + (hc + 1) * 512]
                        p.wait(v, "mm", M0 + J + 1)
                        if nb == 0:
                            inst = v.tensor_copy(ysl, psH[J % 4])
                        else:
                            inst = v.tensor_tensor(ysl, ysl, psH[J % 4], op=OP.add)
                        p.inc(inst, "ev")
        for tt in range(8):
            r0 = t0 + tt * 128
            with nc.Block() as blk:
                @blk.sync
                def _(sy):
                    inst = sy.dma_start(tmp[:, 0:2048], x3s[r0:r0 + 128, :])
                    p.inc(inst, "dS", 16)
                    p.wait(sy, "dS")
            with nc.Block() as blk:
                @blk.vector
                def _(v):
                    p.chain(v, "v", v.tensor_tensor, g_sb[:, 0:2048],
                            y_sb[:, tt * D:(tt + 1) * D], tmp[:, 0:2048], op=OP.add)
            with nc.Block() as blk:
                @blk.sync
                def _(sy):
                    inst = sy.dma_start(x4[r0:r0 + 128, :], g_sb[:, 0:2048])
                    p.inc(inst, "dO", 16)
                    p.wait(sy, "dO")


def emit_attn_phase(nc, p, pool, d, npairs=8):
    qkR, vR, aoS = d["qkR"], d["vR"], d["aoS"]
    ident, rowid = d["ident"], d["rowid"]
    psA, psB, psO, psC = d["psA"], d["psB"], d["psO"], d["psC"]
    pool.reset()
    MW = 4800
    band = pool.take(MW, F32)
    bandw = pool.take(MW, F32)
    k_sb = [pool.take(S, BF16) for _ in range(2)]
    v_sb = [pool.take(S, BF16) for _ in range(2)]
    q_sb = [pool.take(S, BF16) for _ in range(2)]
    s_sb = [pool.take(S, F32) for _ in range(2)]
    p_sb = [pool.take(S, BF16) for _ in range(2)]
    pT_sb = [pool.take(S, BF16) for _ in range(2)]
    negmax = [pool.take(1, F32) for _ in range(2)]
    rowsum = [pool.take(1, F32) for _ in range(2)]
    rinv = [pool.take(1, F32) for _ in range(2)]
    stage1 = [pool.take(128, BF16) for _ in range(2)]
    p.sem("trs")

    with nc.Block() as blk:
        @blk.gpsimd
        def _(g):
            p.chain(g, "g", g.iota, band[:, :].bitcast(I32), [[1, MW]], channel_multiplier=0)
    with nc.Block() as blk:
        @blk.vector
        def _(v):
            p.chain(v, "v", v.tensor_copy, bandw[:, :], band[:, :].bitcast(I32))
    with nc.Block() as blk:
        @blk.vector
        def _(v):
            p.chain(v, "v", v.tensor_scalar, bandw[:, :], bandw[:, :], rowid[:, 0:1],
                    -4096.0, op0=OP.subtract, op1=OP.add)
    with nc.Block() as blk:
        @blk.vector
        def _(v):
            p.chain(v, "v", v.tensor_scalar, bandw[:, :], bandw[:, :], 0.0, None, op0=OP.is_gt)
    with nc.Block() as blk:
        @blk.vector
        def _(v):
            p.chain(v, "v", v.tensor_scalar, band[:, :], bandw[:, :], -1e9, None, op0=OP.mult)

    SCALE = 1.0 / np.sqrt(128.0)
    NKB = [qt // 4 + 1 for qt in range(32)]
    NSUB = [n * 4 for n in NKB]
    CKB = [sum(NKB[:q]) for q in range(33)]
    CSUB = [sum(NSUB[:q]) for q in range(33)]

    p.sem("dSa")
    p.sem("dSb")

    def load_pair(sy, pj, buf):
        sm = "dSa" if buf == 0 else "dSb"
        inst = sy.dma_start(
            k_sb[buf][:, :].rearrange("p (r t) -> p r t", r=8),
            bass.AP(qkR.tensor, (pj * 256 + 128) * 512,
                    [[512, 128], [2048 * 512, 8], [1, 512]]))
        p.inc(inst, sm, 16)
        inst = sy.dma_start(
            q_sb[buf][:, :].rearrange("p (r t) -> p r t", r=8),
            bass.AP(qkR.tensor, (pj * 256) * 512,
                    [[512, 128], [2048 * 512, 8], [1, 512]]))
        p.inc(inst, sm, 16)
        for r in range(8):
            inst = sy.dma_start(
                v_sb[buf][:, r * 512:(r + 1) * 512].rearrange("p (j d) -> p j d", j=4),
                bass.AP(vR.tensor, (r * 4096 + pj * 512) * 128,
                        [[128, 128], [128 * 128, 4], [1, 128]]))
            p.inc(inst, sm, 16)
        return p.cnt[sm]

    pair_dS = [None, None]
    with nc.Block() as blk:
        @blk.sync
        def _(sy):
            p.wait(sy, "cc", 5)
            pair_dS[0] = load_pair(sy, 0, 0)

    for pj in range(npairs):
        kb_buf = pj % 2
        with nc.Block() as blk:
            SS0 = p.cnt["sS"]
            SE0 = p.cnt["sE"]
            V0 = p.cnt["v"]
            S0 = p.cnt["s"]
            TRM0 = p.cnt["tr_mm"]
            TRE0 = p.cnt["tr_ev"]
            TRS0 = p.cnt["trs"]
            OM0 = p.cnt["o_mm"]
            OE0 = p.cnt["o_ev"]
            P0 = {0: p.cnt["dOa"], 1: p.cnt["dOb"]}
            my_dS = pair_dS[pj % 2]
            ps_o = [psO[:, 0:128], psC[:, 0:128]]

            @blk.sync
            def _(sy):
                if pj + 1 < npairs:
                    pair_dS[(pj + 1) % 2] = load_pair(sy, pj + 1, (pj + 1) % 2)
                for qt in range(32):
                    p.wait(sy, "o_ev", OE0 + qt + 1)
                    r0o = (qt // 4) * 4096 + pj * 512 + (qt % 4) * 128
                    inst = sy.dma_start(aoS[r0o:r0o + 128, :], stage1[qt % 2][:, :])
                    p.inc(inst, "dOa" if qt % 2 == 0 else "dOb", 16)

            @blk.tensor
            def _(t):
                bankB = {}
                t.wait_ge(p.sem("dSa" if kb_buf == 0 else "dSb"), my_dS)

                def t1(qt):
                    for kb in range(NKB[qt]):
                        g_ = CKB[qt] + kb
                        if SE0 + g_ - 1 > SE0 - 2 and g_ >= 2:
                            p.wait(t, "sE", SE0 + g_ - 1)
                        inst = t.matmul(psA[g_ % 2][:, :],
                                        q_sb[kb_buf][:, qt * 128:(qt + 1) * 128],
                                        k_sb[kb_buf][:, kb * 512:(kb + 1) * 512],
                                        start=True, stop=True)
                        p.inc(inst, "sS")

                def t2(qt):
                    nsub = NSUB[qt]

                    def emit_tr(j):
                        gs = CSUB[qt] + j
                        if j == 0:
                            p.wait(t, "s", S0 + qt + 1)
                        if bankB.get(gs % 4) is not None:
                            p.wait(t, *bankB[gs % 4])
                        inst = t.transpose(psB[gs % 4][:, 0:128],
                                           p_sb[qt % 2][:, j * 128:(j + 1) * 128],
                                           ident[:, :])
                        p.inc(inst, "tr_mm")
                        if j % 2 == 0:
                            bankB[gs % 4] = ("tr_ev", TRE0 + CSUB[qt] // 2 + j // 2 + 1)
                        else:
                            bankB[gs % 4] = ("trs", TRS0 + CSUB[qt] // 2 + (j + 1) // 2)

                    def emit_mm(j):
                        if j % 2 == 0:
                            p.wait(t, "tr_ev", TRE0 + CSUB[qt] // 2 + j // 2 + 1)
                        else:
                            p.wait(t, "trs", TRS0 + CSUB[qt] // 2 + (j + 1) // 2)
                        if j == 0 and OE0 + qt - 1 > 0:
                            p.wait(t, "o_ev", OE0 + qt - 1)
                        inst = t.matmul(ps_o[qt % 2], pT_sb[qt % 2][:, j * 128:(j + 1) * 128],
                                        v_sb[kb_buf][:, j * 128:(j + 1) * 128],
                                        start=(j == 0), stop=(j == nsub - 1))
                        if j == nsub - 1:
                            p.inc(inst, "o_mm")

                    for j in range(nsub):
                        emit_tr(j)
                        if j >= 3:
                            emit_mm(j - 3)
                    for j in range(max(0, nsub - 3), nsub):
                        emit_mm(j)

                for qt in range(32):
                    t1(qt)
                    if qt >= 1:
                        t2(qt - 1)
                t2(31)

            @blk.vector
            def _(v):
                for it in range(33):
                    if it < 32:
                        qt = it
                        if qt >= 2:
                            p.wait(v, "s", S0 + qt - 1)
                        for kb in range(NKB[qt]):
                            g_ = CKB[qt] + kb
                            p.wait(v, "sS", SS0 + g_ + 1)
                            mstart = 4096 - qt * 128 + kb * 512
                            inst = v.scalar_tensor_tensor(
                                s_sb[qt % 2][:, kb * 512:(kb + 1) * 512], psA[g_ % 2][:, :],
                                SCALE, band[:, mstart:mstart + 512],
                                op0=OP.mult, op1=OP.add)
                            p.inc(inst, "sE")
                        p.wait(v, "sE", SE0 + CKB[qt] + NKB[qt])
                        if p.cnt["v"]:
                            v.wait_ge(p.sem("v"), p.cnt["v"])
                        inst = v.reduce_max(negmax[qt % 2][:, :],
                                            s_sb[qt % 2][:, 0:NKB[qt] * 512],
                                            axis=AX, negate=True)
                        p.inc(inst, "v")
                    if it >= 1:
                        qt = it - 1
                        for j in range(NSUB[qt]):
                            if j % 2 == 0:
                                p.wait(v, "tr_mm", TRM0 + CSUB[qt] + j + 1)
                                inst = v.tensor_copy(
                                    pT_sb[qt % 2][:, j * 128:(j + 1) * 128],
                                    psB[(CSUB[qt] + j) % 4][:, 0:128])
                                p.inc(inst, "tr_ev")
                        p.wait(v, "s", S0 + qt + 1)
                        v.wait_ge(p.sem("v"), p.cnt["v"])
                        inst = v.reciprocal(rinv[qt % 2][:, :], rowsum[qt % 2][:, :])
                        p.inc(inst, "v")
                        p.wait(v, "o_mm", OM0 + qt + 1)
                        pw = P0[qt % 2] + 16 * (qt // 2)
                        if pw > 0:
                            p.wait(v, "dOa" if qt % 2 == 0 else "dOb", pw)
                        v.wait_ge(p.sem("v"), p.cnt["v"])
                        inst = v.tensor_scalar(stage1[qt % 2][:, :], ps_o[qt % 2],
                                               rinv[qt % 2][:, 0:1], None, op0=OP.mult)
                        p.inc(inst, "o_ev")

            @blk.scalar
            def _(sc):
                for it in range(33):
                    if it < 32:
                        qt = it
                        p.wait(sc, "v", V0 + (1 if qt == 0 else 2 * qt))
                        p.wait(sc, "sE", SE0 + CKB[qt] + NKB[qt])
                        if qt >= 2:
                            p.wait(sc, "tr_mm", TRM0 + CSUB[qt - 1])
                        inst = sc.activation(p_sb[qt % 2][:, 0:NKB[qt] * 512],
                                             s_sb[qt % 2][:, 0:NKB[qt] * 512], AF.Exp,
                                             bias=negmax[qt % 2][:, 0:1],
                                             accum_out=rowsum[qt % 2][:, :])
                        p.inc(inst, "s")
                    if it >= 1:
                        qt = it - 1
                        for j in range(NSUB[qt]):
                            if j % 2 == 1:
                                p.wait(sc, "tr_mm", TRM0 + CSUB[qt] + j + 1)
                                inst = sc.copy(
                                    pT_sb[qt % 2][:, j * 128:(j + 1) * 128],
                                    psB[(CSUB[qt] + j) % 4][:, 0:128])
                                p.inc(inst, "trs")




def emit_dequant(nc, p, pool, d):
    gWp, gSc, uWp, uSc, dWp, dSc = d["gWp"], d["gSc"], d["uWp"], d["uSc"], d["dWp"], d["dSc"]
    wgL, wuL, wdL = d["wgL"], d["wuL"], d["wdL"]
    ident, psB = d["ident"], d["psB"]
    pool.reset()
    wp_sb = [pool.take(1024, I32) for _ in range(2)]
    sc_sb = [pool.take(64, F32) for _ in range(2)]
    nib = [pool.take(2048, I32) for _ in range(2)]
    m7 = [pool.take(2048, I32) for _ in range(2)]
    bits = [pool.take(2048, I32) for _ in range(2)]
    sgn = [pool.take(2048, I32) for _ in range(2)]
    alt = [pool.take(2048, I32) for _ in range(2)]
    msk = [pool.take(2048, I32) for _ in range(2)]
    outi = [pool.take(2048, I32) for _ in range(2)]
    deq = [pool.take(2048, BF16) for _ in range(2)]
    stg = [pool.take(128, BF16) for _ in range(2)]
    p.sem("dq0")
    p.sem("dq1")
    p.sem("dSq0")
    p.sem("dSq1")
    p.sem("dQa")
    p.sem("dQb")
    p.sem("trs")

    gWp_, gSc_, uWp_, uSc_, dWp_, dSc_ = gWp, gSc, uWp, uSc, dWp, dSc
    tiles = [(nm, rt) for nm in range(3) for rt in range(16)]
    mats = [(gWp_, gSc_, wgL, 0x3F000000, 252), (uWp_, uSc_, wuL, 0x3F000000, 252),
            (dWp_, dSc_, wdL, 0x3E800000, 250)]

    def chain_front(v, b, ALT, OFF, dS_val):
        c = lambda fn, *a, **k: p.chain(v, "dq0", fn, *a, **k)
        v.wait_ge(p.sem("dSq0" if b == 0 else "dSq1"), dS_val)
        c(v.tensor_scalar, nib[b][:, 0:1024], wp_sb[b][:, :], 15, None, op0=OP.bitwise_and)
        c(v.tensor_scalar, nib[b][:, 1024:2048], wp_sb[b][:, :], 4, 15,
          op0=OP.logical_shift_right, op1=OP.bitwise_and)
        c(v.tensor_scalar, m7[b][:, :], nib[b][:, :], 7, None, op0=OP.bitwise_and)
        c(v.tensor_scalar, bits[b][:, :], m7[b][:, :], OFF, None, op0=OP.add)
        c(v.tensor_scalar, bits[b][:, :], bits[b][:, :], 22, None, op0=OP.logical_shift_left)
        c(v.tensor_scalar, sgn[b][:, :], nib[b][:, :], 8, 28,
          op0=OP.bitwise_and, op1=OP.logical_shift_left)
        c(v.tensor_tensor, bits[b][:, :], bits[b][:, :], sgn[b][:, :], op=OP.bitwise_or)
        c(v.tensor_scalar, alt[b][:, :], nib[b][:, :], 1, None, op0=OP.bitwise_and)
        c(v.tensor_scalar, alt[b][:, :], alt[b][:, :], ALT, None, op0=OP.mult)
        c(v.tensor_tensor, alt[b][:, :], alt[b][:, :], sgn[b][:, :], op=OP.bitwise_or)
        c(v.tensor_scalar, msk[b][:, :], m7[b][:, :], 2, None, op0=OP.is_ge)

    def chain_tail(g, b, front_cnt, tr_need):
        c = lambda fn, *a, **k: p.chain(g, "dq1", fn, *a, **k)
        g.wait_ge(p.sem("dq0"), front_cnt)
        if tr_need > 0:
            g.wait_ge(p.sem("tr_mm"), tr_need)
        c(g.tensor_tensor, outi[b][:, :], bits[b][:, :], alt[b][:, :], op=OP.subtract)
        c(g.tensor_tensor, outi[b][:, :], outi[b][:, :], msk[b][:, :], op=OP.mult)
        c(g.tensor_tensor, outi[b][:, :], outi[b][:, :], alt[b][:, :], op=OP.add)
        of = outi[b][:, :].bitcast(F32)
        for par in range(2):
            src = of[:, par * 1024:(par + 1) * 1024].rearrange("p (g j) -> p g j", j=16)
            dst = deq[b][:, :].rearrange("p (g j) -> p g j", j=32)[:, :, par::2]
            scb = sc_sb[b][:, :].unsqueeze(-1).broadcast_to([128, 64, 16])
            c(g.tensor_tensor, dst, src, scb, op=OP.mult)

    NFR = 11
    NTL = 5
    ld_dS = []
    ld_dS = []
    with nc.Block() as blk:
        CH0v = p.cnt["dq0"]
        CH1 = p.cnt["dq1"]
        TRM0 = p.cnt["tr_mm"]
        TRS0 = p.cnt["trs"]
        P0 = {0: p.cnt["dQa"], 1: p.cnt["dQb"]}

        @blk.sync
        def _(sy):
            for ti, (nm, rt) in enumerate(tiles):
                wp_d, sc_d = mats[nm][0], mats[nm][1]
                b = ti % 2
                if ti >= 2:
                    p.wait(sy, "dq1", CH1 + NTL * (ti - 1))
                smq = "dSq0" if b == 0 else "dSq1"
                inst = sy.dma_start(wp_sb[b][:, :], wp_d[rt * 128:(rt + 1) * 128, :])
                p.inc(inst, smq, 16)
                inst = sy.dma_start(sc_sb[b][:, :], sc_d[rt * 128:(rt + 1) * 128, :])
                p.inc(inst, smq, 16)
                ld_dS.append(p.cnt[smq])

        @blk.vector
        def _(v):
            for ti, (nm, rt) in enumerate(tiles):
                ALT, OFF = mats[nm][3], mats[nm][4]
                if ti >= 2:
                    p.wait(v, "dq1", CH1 + NTL * (ti - 1))
                chain_front(v, ti % 2, ALT, OFF, ld_dS[ti])

        @blk.gpsimd
        def _(g):
            for ti, (nm, rt) in enumerate(tiles):
                tr_need = TRM0 + 16 * (ti - 1) if ti >= 2 else 0
                chain_tail(g, ti % 2, CH0v + NFR * (ti + 1), tr_need)

        @blk.tensor
        def _(t):
            for ti, (nm, rt) in enumerate(tiles):
                b = ti % 2
                t.wait_ge(p.sem("dq1"), CH1 + NTL * (ti + 1))
                for cc_ in range(16):
                    gs = 16 * ti + cc_
                    if gs >= 4:
                        p.wait(t, "trs", TRS0 + gs - 3)
                    inst = t.transpose(psB[gs % 4][:, 0:128],
                                       deq[b][:, cc_ * 128:(cc_ + 1) * 128], ident[:, :])
                    p.inc(inst, "tr_mm")

        @blk.scalar
        def _(sc):
            for ti, (nm, rt) in enumerate(tiles):
                out_d = mats[nm][2]
                for cc_ in range(16):
                    gs = 16 * ti + cc_
                    p.wait(sc, "tr_mm", TRM0 + gs + 1)
                    pw = P0[gs % 2] + 16 * (gs // 2)
                    if pw > 0:
                        p.wait(sc, "dQa" if gs % 2 == 0 else "dQb", pw)
                    inst = sc.copy(stg[gs % 2][:, :], psB[gs % 4][:, 0:128])
                    p.inc(inst, "trs")
                    p.wait(sc, "trs", TRS0 + gs + 1)
                    inst = sc.dma_start(
                        out_d[cc_ * 128:(cc_ + 1) * 128, rt * 128:(rt + 1) * 128],
                        stg[gs % 2][:, :])
                    p.inc(inst, "dQa" if gs % 2 == 0 else "dQb", 16)



class Pool:

    def __init__(self, ap, words):
        self.ap = ap
        self.words = words
        self.off = 0

    def take(self, nelem, dt):
        esz = {F32: 4, I32: 4, BF16: 2}[dt]
        w = (nelem * esz + 31) // 32 * 8
        assert self.off + w <= self.words, f"pool overflow: {self.off}+{w} > {self.words}"
        view = self.ap[:, self.off:self.off + w]
        self.off += w
        if dt is not F32:
            view = view.bitcast(dt)
        return view[:, 0:nelem]

    def mark(self):
        return self.off

    def reset(self, off=0):
        self.off = off


def build_fused(dev_out=(), phases=None):
    MLP = 16384
    nc = bass.Bass("TRN2", target_bir_lowering=False)
    ein = lambda n, sh, dt: nc.dram_tensor(n, sh, dt, kind="ExternalInput").ap()
    xs = ein("xs", [TOK, D], F32)
    wpT = ein("wpT", [D, 3 * D], F32)
    woT = ein("woT", [D, D], F32)
    wrTe = ein("wrTe", [D, 8 * D], F32)
    brh = ein("brh", [128, 128], F32)
    weT = ein("weT", [D, 8 * D], F32)
    ebb = ein("ebb", [8, D], F32)
    wqkvT = ein("wqkvT", [D, 3072], F32)
    wprojT = ein("wprojT", [D, D], F32)
    gWp = ein("gWp", [2048, 1024], I32)
    gSc = ein("gSc", [2048, 64], F32)
    uWp = ein("uWp", [2048, 1024], I32)
    uSc = ein("uSc", [2048, 64], F32)
    dWp = ein("dWp", [2048, 1024], I32)
    dSc = ein("dSc", [2048, 64], F32)
    x4 = nc.dram_tensor("x4", [TOK, D], F32, kind="ExternalOutput").ap()

    def internal(n, sh, dt, shared=False):
        kind = "ExternalOutput" if n in dev_out else "Internal"
        kw = dict(addr_space="Shared") if (shared and n not in dev_out) else {}
        return nc.dram_tensor(n, sh, dt, kind=kind, **kw).ap()

    qkS = internal("qkS", [8 * 2048, 512], BF16)
    qkR = internal("qkR", [8 * 2048, 512], BF16)
    vS = internal("vS", [8 * 4096, 128], BF16)
    vR = internal("vR", [8 * 4096, 128], BF16)
    aoS = internal("aoS", [8 * 4096, 128], BF16)
    aoR = internal("aoR", [8 * 4096, 128], BF16)
    wgL = internal("wgL", [D, 2048], BF16)
    wuL = internal("wuL", [D, 2048], BF16)
    wdL = internal("wdL", [2048, D], BF16)
    wgG = internal("wgG", [8 * D, 2048], BF16, shared=True)
    wuG = internal("wuG", [8 * D, 2048], BF16, shared=True)
    wdG = internal("wdG", [MLP, D], BF16, shared=True)
    x1D = internal("x1D", [TOK, D], F32)
    x2D = internal("x2D", [TOK, D], F32)
    x3D = internal("x3D", [TOK, D], F32)
    ln2TD = internal("ln2TD", [D, TOK], BF16)
    ln3TD = internal("ln3TD", [D, TOK], BF16)
    ln4TD = internal("ln4TD", [D, TOK], BF16)
    ao1D = internal("ao1D", [TOK, D], BF16)
    c_accD = internal("c_accD", [8, TOK], F32)

    if phases is None:
        phases = {"dequant", "A", "attn", "projC", "moe", "l1attn", "projF", "l1ffn",
                  "cc"}
    with ExitStack() as st:
        p = Prog(nc, st)
        sb = lambda name, shape, dt: st.enter_context(nc.sbuf_tensor(name, shape, dt))
        psa = lambda name, shape, dt: st.enter_context(nc.psum_tensor(name, shape, dt))

        ident = sb("ident", [128, 128], BF16)
        identF = sb("identF", [128, 128], F32)
        iota_i = sb("iota_i", [128, 128], I32)
        iota_f = sb("iota_f", [128, 128], F32)
        rowid = sb("rowid", [128, 1], F32)
        bdmask = sb("bdmask", [128, 128], F32)
        ones_bf = sb("ones_bf", [128, 1], BF16)
        brh_sb = sb("brh_sb", [128, 128], F32)
        eb_sb = sb("eb_sb", [8, D], BF16)
        cT_sb = sb("cT_sb", [8, TOK], BF16)
        c_col = sb("c_col", [128, 128], F32)
        sums = sb("sums", [128, 2], F32)
        mean = sb("mean", [128, 1], F32)
        rstd = sb("rstd", [128, 1], F32)
        cstage = [sb(f"cstage{i}", [1, 512], F32) for i in range(2)]
        PW = 49152
        pool = Pool(sb("pool", [128, PW], F32), PW)

        psA = [psa(f"psA{i}", [128, 512], F32) for i in range(2)]
        psB = [psa(f"psB{i}", [128, 1024], BF16) for i in range(4)]
        psO = psa("psO", [128, 512], F32)
        psC = psa("psC", [128, 512], F32)

        for s_ in ["v", "s", "t", "g", "dS", "dG", "dO", "dA", "dOa", "dOb", "dOx",
                   "dAa", "dAb", "tr_mm", "tr_ev",
                   "mm", "ev", "cm", "cv", "sg", "cc", "o_mm", "o_ev", "sS", "sE"]:
            p.sem(s_)

        with nc.Block() as blk:
            @blk.sync
            def _(sy):
                inst = sy.dma_start(brh_sb[:, :], brh)
                p.inc(inst, "dS", 16)
                p.wait(sy, "dS")

            @blk.gpsimd
            def _(g):
                inst = g.dma_start(eb_sb[:, :], ebb)
                p.inc(inst, "dG", 16)
                p.wait(g, "dG")
        emit_ident(p, ident, iota_i, iota_f, rowid)
        with nc.Block() as blk:
            @blk.vector
            def _(v):
                p.chain(v, "v", v.tensor_scalar, identF[:, :], iota_f[:, :], rowid[:, 0:1],
                        None, op0=OP.is_equal)
                p.chain(v, "v", v.memset, ones_bf[:, :], 1.0)
        with nc.Block() as blk:
            @blk.vector
            def _(v):
                p.chain(v, "v", v.tensor_scalar, bdmask[:, :].bitcast(I32), iota_i[:, :],
                        2, None, op0=OP.logical_shift_right)
                p.chain(v, "v", v.tensor_copy, sums[:, 0:1].bitcast(I32),
                        iota_i[:, 0:1])
        with nc.Block() as blk:
            @blk.gpsimd
            def _(g):
                p.chain(g, "g", g.iota, sums[:, 0:1].bitcast(I32), [[1, 1]], channel_multiplier=1)
        with nc.Block() as blk:
            @blk.vector
            def _(v):
                p.chain(v, "v", v.tensor_scalar, sums[:, 0:1].bitcast(I32),
                        sums[:, 0:1].bitcast(I32), 2, None, op0=OP.logical_shift_right)
        with nc.Block() as blk:
            @blk.vector
            def _(v):
                p.chain(v, "v", v.tensor_copy, mean[:, 0:1], sums[:, 0:1].bitcast(I32))
                p.chain(v, "v", v.tensor_copy, rstd[:, :].bitcast(I32)[:, 0:1],
                        sums[:, 0:1].bitcast(I32))
        with nc.Block() as blk:
            @blk.vector
            def _(v):
                p.chain(v, "v", v.tensor_copy, identF[:, :], bdmask[:, :].bitcast(I32))
        with nc.Block() as blk:
            @blk.vector
            def _(v):
                p.chain(v, "v", v.tensor_scalar, bdmask[:, :], identF[:, :], mean[:, 0:1],
                        None, op0=OP.is_equal)
        with nc.Block() as blk:
            @blk.vector
            def _(v):
                p.chain(v, "v", v.tensor_scalar, bdmask[:, :], bdmask[:, :], -1.0, 1e9,
                        op0=OP.add, op1=OP.mult)
        with nc.Block() as blk:
            @blk.vector
            def _(v):
                p.chain(v, "v", v.tensor_scalar, identF[:, :], iota_f[:, :], rowid[:, 0:1],
                        None, op0=OP.is_equal)

        emit_dequant(nc, p, pool, dict(
            gWp=gWp, gSc=gSc, uWp=uWp, uSc=uSc, dWp=dWp, dSc=dSc,
            wgL=wgL, wuL=wuL, wdL=wdL, ident=ident, psB=psB))
        with nc.Block() as blk:
            @blk.gpsimd
            def _(g):
                p.wait_all(g, ["dQa", "dQb"])
                for src, dst in [(wgL, wgG), (wuL, wuG), (wdL, wdG)]:
                    inst = g.collective_compute(
                        "AllGather", OP.bypass,
                        replica_groups=[list(range(NCORES))],
                        ins=[src.opt()], outs=[dst.opt()])
                    p.inc(inst, "cc")

        pool.reset()
        x_t = pool.take(D, F32)
        scratch = pool.take(D, F32)
        ln_bf = pool.take(D, BF16)
        ln1T = pool.take(KC * TOK, BF16)
        wslab = pool.take(KC * 512, BF16)
        stage5 = [pool.take(512, BF16) for _ in range(2)]

        for tt in range(TOK // 128):
            with nc.Block() as blk:
                @blk.sync
                def _(sy):
                    inst = sy.dma_start(x_t[:, :], xs[tt * 128:(tt + 1) * 128, :])
                    p.inc(inst, "dS", 16)
                    p.wait(sy, "dS")
            emit_ln(p, None, x_t, ln_bf, sums, mean, rstd, scratch)
            emit_transpose_to_slab2(p, ident, ln_bf, ln1T, tt * 128, TOK, psB)

        def mm_evac_store(jobs):
            with nc.Block() as blk:
                M0 = p.cnt["mm"]
                E0 = p.cnt["ev"]
                P0 = {0: p.cnt["dOa"], 1: p.cnt["dOb"]}

                @blk.tensor
                def _(t):
                    for j, (lhs_l, rhs_l, _o) in enumerate(jobs):
                        if E0 + j - 1 > 0:
                            p.wait(t, "ev", E0 + j - 1)
                        for c in range(len(lhs_l)):
                            inst = t.matmul(psA[j % 2][:, :], lhs_l[c], rhs_l[c],
                                            start=(c == 0), stop=(c == len(lhs_l) - 1))
                        p.inc(inst, "mm")

                @blk.vector
                def _(v):
                    for j in range(len(jobs)):
                        p.wait(v, "mm", M0 + j + 1)
                        pw = P0[j % 2] + 16 * (j // 2)
                        if pw > 0:
                            p.wait(v, "dOa" if j % 2 == 0 else "dOb", pw)
                        inst = v.tensor_copy(stage5[j % 2][:, :], psA[j % 2][:, :])
                        p.inc(inst, "ev")

                @blk.sync
                def _(sy):
                    for j, (_l, _r, o_ap) in enumerate(jobs):
                        p.wait(sy, "ev", E0 + j + 1)
                        inst = sy.dma_start(o_ap, stage5[j % 2][:, :])
                        p.inc(inst, "dOa" if j % 2 == 0 else "dOb", 16)

        def load_wslab(src_ap, c0, col0, ncols=512):
            with nc.Block() as blk:
                @blk.gpsimd
                def _(g):
                    for c in range(KC):
                        inst = g.dma_start(wslab[:, c * 512:c * 512 + ncols],
                                           src_ap[c * 128:(c + 1) * 128, col0:col0 + ncols])
                        p.inc(inst, "dG", 16)
                    p.wait(g, "dG")

        for mb in range(8):
            load_wslab(wpT, 0, mb * 512)
            jobs = []
            for m4 in range(4):
                row0 = mb * 512 + m4 * 128
                is_k = row0 >= D
                h = (row0 - D) // 128 if is_k else row0 // 128
                for t4 in range(4):
                    b = t4
                    j = b * 2 + h // 8
                    r0 = j * 2048 + (h % 8) * 256 + (128 if is_k else 0)
                    lhs = [wslab[:, c * 512 + m4 * 128: c * 512 + (m4 + 1) * 128] for c in range(KC)]
                    rhs = [ln1T[:, c * TOK + t4 * 512: c * TOK + (t4 + 1) * 512] for c in range(KC)]
                    jobs.append((lhs, rhs, qkS[r0:r0 + 128, :]))
            mm_evac_store(jobs)
        for vb in range(4):
            load_wslab(wpT, 0, 2 * D + vb * 512)
            h0 = vb * 4
            jv = (h0 // 8)
            jobs = []
            for tt in range(TOK // 128):
                b = tt // 4
                j = b * 2 + jv
                lhs = [ln1T[:, c * TOK + tt * 128: c * TOK + (tt + 1) * 128] for c in range(KC)]
                rhs = [wslab[:, c * 512:(c + 1) * 512] for c in range(KC)]
                off = (j * 4096 + (h0 % 8) * 512 + (tt % 4) * 128) * 128
                dst = bass.AP(vS.tensor, off, [[128, 128], [512 * 128, 4], [1, 128]])
                jobs.append((lhs, rhs, dst))
            mm_evac_store(jobs)
        with nc.Block() as blk:
            @blk.gpsimd
            def _(g):
                p.wait_all(g, ["dOa", "dOb"])
                with allow_ici_all_to_all():
                    inst = g.collective_compute(
                        "AllToAll", OP.bypass,
                        replica_groups=[list(range(NCORES))],
                        ins=[qkS.opt()], outs=[qkR.opt()])
                    p.inc(inst, "cc")
                    inst = g.collective_compute(
                        "AllToAll", OP.bypass,
                        replica_groups=[list(range(NCORES))],
                        ins=[vS.opt()], outs=[vR.opt()])
                    p.inc(inst, "cc")

        emit_attn_phase(nc, p, pool, dict(
            qkR=qkR, vR=vR, aoS=aoS, ident=ident, rowid=rowid,
            psA=psA, psB=psB, psO=psO, psC=psC))

        with nc.Block() as blk:
            @blk.gpsimd
            def _(g):
                p.wait_all(g, ["dOa", "dOb"])
                with allow_ici_all_to_all():
                    inst = g.collective_compute(
                        "AllToAll", OP.bypass,
                        replica_groups=[list(range(NCORES))],
                        ins=[aoS.opt()], outs=[aoR.opt()])
                    p.inc(inst, "cc")

        def emit_proj(xs_in, ao_rd, wT_in, x1_out, lnT_out, wait_cc=None):
            pool.reset()
            w_sb = pool.take(KC * D, BF16)
            ao_t = pool.take(D, BF16)
            aoT_t = pool.take(KC * 128, BF16)
            x_t = pool.take(D, F32)
            x1_t = pool.take(D, F32)
            scratch = pool.take(D, F32)
            ln_bf = pool.take(D, BF16)
            stage2 = [pool.take(128, BF16) for _ in range(2)]
            with nc.Block() as blk:
                @blk.gpsimd
                def _(g):
                    if wait_cc is not None:
                        p.wait(g, "cc", wait_cc)
                    p.wait_all(g, ["dOa", "dOb", "dOx", "dAa", "dAb"])
                    for c in range(KC):
                        inst = g.dma_start(w_sb[:, c * D:(c + 1) * D], wT_in[c * 128:(c + 1) * 128, :])
                        p.inc(inst, "dG", 16)
                    p.wait(g, "dG")
            for tt in range(TOK // 128):
                r0 = tt * 128
                with nc.Block() as blk:
                    @blk.sync
                    def _(sy):
                        inst = sy.dma_start(x_t[:, :], xs_in[r0:r0 + 128, :])
                        p.inc(inst, "dS", 16)

                    @blk.gpsimd
                    def _(g):
                        ao_rd(g, tt, ao_t)
                        p.wait(g, "dG")
                with nc.Block() as blk:
                    @blk.sync
                    def _(sy):
                        p.wait(sy, "dS")
                emit_transpose_to_slab2(p, ident, ao_t, aoT_t, 0, 128, psB)
                with nc.Block() as blk:
                    M0 = p.cnt["mm"]
                    E0 = p.cnt["ev"]

                    @blk.tensor
                    def _(t):
                        for j in range(4):
                            if E0 + j - 1 > 0:
                                p.wait(t, "ev", E0 + j - 1)
                            for c in range(KC):
                                inst = t.matmul(psA[j % 2][:, :],
                                                aoT_t[:, c * 128:(c + 1) * 128],
                                                w_sb[:, c * D + j * 512: c * D + (j + 1) * 512],
                                                start=(c == 0), stop=(c == KC - 1))
                            p.inc(inst, "mm")

                    @blk.vector
                    def _(v):
                        for j in range(4):
                            p.wait(v, "mm", M0 + j + 1)
                            inst = v.tensor_tensor(x1_t[:, j * 512:(j + 1) * 512],
                                                   psA[j % 2][:, :],
                                                   x_t[:, j * 512:(j + 1) * 512], op=OP.add)
                            p.inc(inst, "ev")
                with nc.Block() as blk:
                    @blk.sync
                    def _(sy):
                        inst = sy.dma_start(x1_out[r0:r0 + 128, :], x1_t[:, :])
                        p.inc(inst, "dO", 16)
                emit_ln(p, None, x1_t, ln_bf, sums, mean, rstd, scratch)
                emit_lnT_store(p, ident, ln_bf, lnT_out, r0, [psB[0][:, 0:128], psB[1][:, 0:128]],
                               stage2)

        def ao_from_aoR(g, tt, ao_t):
            b = tt // 4
            for hb in range(2):
                off = ((b * 2 + hb) * 4096 + (tt % 4) * 128) * 128
                inst = g.dma_start(
                    ao_t[:, hb * 1024:(hb + 1) * 1024].rearrange("p (h d) -> p h d", h=8),
                    bass.AP(aoR.tensor, off, [[128, 128], [512 * 128, 8], [1, 128]]))
                p.inc(inst, "dG", 16)

        emit_proj(xs, ao_from_aoR, woT, x1D, ln2TD, wait_cc=6)

        emit_moe(nc, p, pool, dict(
            ln2T=ln2TD, x1s=x1D, wrTe=wrTe, weT=weT, x2=x2D, lnT=ln3TD, c_accD=c_accD,
            brh_sb=brh_sb, eb_sb=eb_sb, ones_bf=ones_bf, cT_sb=cT_sb, c_col=c_col,
            identF=identF, ident=ident, sums=sums, mean=mean, rstd=rstd, cstage=cstage,
            psA=psA, psB=psB, psC=psC))

        emit_l1attn(nc, p, pool, dict(
            ln3T=ln3TD, wqkvT=wqkvT, ao1=ao1D, bdmask=bdmask, ident=ident,
            psA=psA, psB=psB, psO=psO, psC=psC))

        def ao_from_ao1(g, tt, ao_t):
            inst = g.dma_start(ao_t[:, :], ao1D[tt * 128:(tt + 1) * 128, :])
            p.inc(inst, "dG", 16)

        emit_proj(x2D, ao_from_ao1, wprojT, x3D, ln4TD)

        emit_l1ffn(nc, p, pool, dict(
            ln4T=ln4TD, x3s=x3D, wgG=wgG, wuG=wuG, wdG=wdG, x4=x4, psA=psA, psB=psB))

        with nc.Block() as blk:
            @blk.sync
            def _(sy):
                p.wait_all(sy, ["dO", "dOa", "dOb", "dOx", "dAa", "dAb", "dS", "dG", "dA",
                                "dSa", "dSb", "dSg", "dSu", "dSw", "dGa", "dGb",
                                "dCa", "dCb", "dQa", "dQb", "dSq0", "dSq1", "dXa", "dXb",
                                "dRa", "dRb", "dTa", "dTb"])
    return nc


def shard_tokens(full):
    out = []
    for i in range(NCORES):
        sl = full[:, i * SC:(i + 1) * SC]
        out.append(np.ascontiguousarray(sl.reshape((TOK,) + full.shape[2:])))
    return out


def unshard_tokens(shards):
    rest = shards[0].shape[1:]
    full = np.empty((B, S) + rest, dtype=shards[0].dtype)
    for i, sh in enumerate(shards):
        full[:, i * SC:(i + 1) * SC] = sh.reshape((B, SC) + rest)
    return full


def fused_maps(L0_attn_x, L0_attn_w_project, L0_attn_w_out,
               L0_ffn_router_w, L0_ffn_router_b, L0_ffn_expert_w, L0_ffn_expert_b,
               L1_attn_w_qkv, L1_attn_w_proj,
               L1_ffn_gate_W_packed, L1_ffn_gate_scales,
               L1_ffn_up_W_packed, L1_ffn_up_scales,
               L1_ffn_down_W_packed, L1_ffn_down_scales):
    x = np.asarray(L0_attn_x, dtype=np.float32)
    xs = shard_tokens(x)
    wpT = np.ascontiguousarray(np.asarray(L0_attn_w_project, dtype=np.float32).T)
    woT = np.ascontiguousarray(np.asarray(L0_attn_w_out, dtype=np.float32).T)
    rw = np.asarray(L0_ffn_router_w, dtype=np.float32)
    rb = np.asarray(L0_ffn_router_b, dtype=np.float32)
    ew = np.asarray(L0_ffn_expert_w, dtype=np.float32)
    ebias = np.ascontiguousarray(np.asarray(L0_ffn_expert_b, dtype=np.float32))
    wrTe = np.ascontiguousarray(
        rw.reshape(16, 128, 8, D).transpose(0, 2, 1, 3).reshape(8 * D, D).T)
    brh = np.ascontiguousarray(rb.reshape(D, 8).T.reshape(8 * D).reshape(128, 128).T)
    weT = np.ascontiguousarray(ew.transpose(2, 0, 1).reshape(D, 8 * D))
    wqkvT = np.ascontiguousarray(np.asarray(L1_attn_w_qkv, dtype=np.float32).T)
    wprojT = np.ascontiguousarray(np.asarray(L1_attn_w_proj, dtype=np.float32).T)
    gWp = np.asarray(L1_ffn_gate_W_packed)
    gSc = np.asarray(L1_ffn_gate_scales, dtype=np.float32)
    uWp = np.asarray(L1_ffn_up_W_packed)
    uSc = np.asarray(L1_ffn_up_scales, dtype=np.float32)
    dWp = np.asarray(L1_ffn_down_W_packed)
    dSc = np.asarray(L1_ffn_down_scales, dtype=np.float32)
    maps = []
    for i in range(NCORES):
        maps.append(dict(
            xs=xs[i], wpT=wpT, woT=woT, wrTe=wrTe, brh=brh, weT=weT, ebb=ebias,
            wqkvT=wqkvT, wprojT=wprojT,
            gWp=np.ascontiguousarray(gWp[i * 2048:(i + 1) * 2048]),
            gSc=np.ascontiguousarray(gSc[i * 2048:(i + 1) * 2048]),
            uWp=np.ascontiguousarray(uWp[i * 2048:(i + 1) * 2048]),
            uSc=np.ascontiguousarray(uSc[i * 2048:(i + 1) * 2048]),
            dWp=np.ascontiguousarray(dWp[:, i * 1024:(i + 1) * 1024]),
            dSc=np.ascontiguousarray(dSc[:, i * 64:(i + 1) * 64])))
    return maps


def kernel(L0_attn_x, L0_attn_w_project, L0_attn_w_out,
           L0_ffn_router_w, L0_ffn_router_b, L0_ffn_expert_w, L0_ffn_expert_b,
           L1_attn_w_qkv, L1_attn_w_proj,
           L1_ffn_gate_W_packed, L1_ffn_gate_scales,
           L1_ffn_up_W_packed, L1_ffn_up_scales,
           L1_ffn_down_W_packed, L1_ffn_down_scales):
    run = lambda nc, maps: run_bass_kernel_spmd(nc, maps, core_ids=list(range(NCORES))).results

    maps = fused_maps(L0_attn_x, L0_attn_w_project, L0_attn_w_out,
                      L0_ffn_router_w, L0_ffn_router_b, L0_ffn_expert_w, L0_ffn_expert_b,
                      L1_attn_w_qkv, L1_attn_w_proj,
                      L1_ffn_gate_W_packed, L1_ffn_gate_scales,
                      L1_ffn_up_W_packed, L1_ffn_up_scales,
                      L1_ffn_down_W_packed, L1_ffn_down_scales)
    res = run(build_fused(), maps)
    x4 = unshard_tokens([np.asarray(r["x4"]) for r in res])
    return np.ascontiguousarray(x4, dtype=np.float32)




# version 09
# speedup vs baseline: 1.0371x; 1.0371x over previous
import numpy as np
import ml_dtypes
from contextlib import ExitStack

import concourse.bass as bass
import concourse.mybir as mybir
from concourse.bass_utils import run_bass_kernel_spmd
from concourse.aot_env import allow_ici_all_to_all

F32 = mybir.dt.float32
BF16 = mybir.dt.bfloat16
I32 = mybir.dt.int32
AF = mybir.ActivationFunctionType
OP = mybir.AluOpType
AX = mybir.AxisListType.X

BF16NP = ml_dtypes.bfloat16

NCORES = 8
D = 2048
B = 4
S = 4096
SC = S // NCORES
TOK = B * SC
KC = D // 128


class Prog:

    def __init__(self, nc, stack):
        self.nc = nc
        self.stack = stack
        self.sems = {}
        self.cnt = {}

    def sem(self, name):
        if name not in self.sems:
            self.sems[name] = self.stack.enter_context(self.nc.semaphore(name))
            self.cnt[name] = 0
        return self.sems[name]

    def inc(self, inst, name, by=1):
        inst.then_inc(self.sem(name), by)
        self.cnt[name] += by
        return self.cnt[name]

    def wait(self, eng, name, val=None):
        eng.wait_ge(self.sem(name), self.cnt[name] if val is None else val)

    def wait_all(self, eng, names):
        for n in names:
            if self.cnt.get(n):
                eng.wait_ge(self.sems[n], self.cnt[n])

    def chain(self, eng, name, fn, *a, **k):
        if self.cnt[name]:
            eng.wait_ge(self.sems[name], self.cnt[name])
        inst = fn(*a, **k)
        return self.inc(inst, name)


def emit_ln(p, blk, x_t, out_bf, sums, mean, rstd, scratch, n=D):
    nc = p.nc
    with nc.Block() as b1:
        V0 = p.cnt["v"]
        S0 = p.cnt["s"]

        @b1.vector
        def _(v):
            p.chain(v, "v", v.reduce_sum, sums[:, 0:1], x_t[:, :], axis=AX)
            p.chain(v, "v", v.tensor_scalar, mean[:, :], sums[:, 0:1], 1.0 / n, None, op0=OP.mult)
            p.chain(v, "v", v.tensor_tensor, rstd[:, :], mean[:, :], mean[:, :], op=OP.mult)
            p.wait(v, "s", S0 + 1)
            p.chain(v, "v", v.scalar_tensor_tensor, rstd[:, :], sums[:, 1:2], 1.0 / n,
                    rstd[:, :], op0=OP.mult, op1=OP.subtract)
            p.chain(v, "v", v.tensor_scalar, rstd[:, :], rstd[:, :], 1e-5, None, op0=OP.add)
            p.chain(v, "v", v.reciprocal, rstd[:, :], rstd[:, :])
            p.chain(v, "v", v.tensor_scalar, mean[:, :], mean[:, :], -1.0, None, op0=OP.mult)
            p.wait(v, "s", S0 + 2)
            p.chain(v, "v", v.tensor_scalar, out_bf[:, :], x_t[:, :], mean[:, 0:1],
                    rstd[:, 0:1], op0=OP.add, op1=OP.mult)

        @b1.scalar
        def _(s):
            p.chain(s, "s", s.activation, scratch[:, 0:n], x_t[:, :], AF.Square,
                    accum_out=sums[:, 1:2])
            p.wait(s, "v", V0 + 6)
            p.chain(s, "s", s.activation, rstd[:, :], rstd[:, :], AF.Sqrt)


def emit_ident(p, ident, iota_i, iota_f, rowid):
    nc = p.nc
    with nc.Block() as blk:
        @blk.gpsimd
        def _(g):
            p.chain(g, "g", g.iota, iota_i[:, :], [[1, 128]], channel_multiplier=0)
            p.chain(g, "g", g.iota, iota_f[:, 0:1].bitcast(I32), [[1, 1]], channel_multiplier=1)
    with nc.Block() as blk:
        @blk.vector
        def _(v):
            p.chain(v, "v", v.tensor_copy, rowid[:, :], iota_f[:, 0:1].bitcast(I32))
            p.chain(v, "v", v.tensor_copy, iota_f[:, :], iota_i[:, :])
    with nc.Block() as blk:
        @blk.vector
        def _(v):
            p.chain(v, "v", v.tensor_scalar, ident[:, :], iota_f[:, :], rowid[:, 0:1],
                    None, op0=OP.is_equal)






def emit_lnT_store(p, ident, ln_bf, lnT, r0, ps_tr, stage):
    nc = p.nc
    with nc.Block() as blk:
        M0 = p.cnt["tr_mm"]
        E0 = p.cnt["tr_ev"]
        P0 = {0: p.cnt["dOa"], 1: p.cnt["dOb"]}

        @blk.tensor
        def _(t):
            for c in range(KC):
                if E0 + c - 1 > 0:
                    p.wait(t, "tr_ev", E0 + c - 1)
                inst = t.transpose(ps_tr[c % 2][:, :], ln_bf[:, c * 128:(c + 1) * 128],
                                   ident[:, :])
                p.inc(inst, "tr_mm")

        @blk.vector
        def _(v):
            for c in range(KC):
                p.wait(v, "tr_mm", M0 + c + 1)
                pw = P0[c % 2] + 16 * (c // 2)
                if pw > 0:
                    p.wait(v, "dOa" if c % 2 == 0 else "dOb", pw)
                inst = v.tensor_copy(stage[c % 2][:, 0:128], ps_tr[c % 2][:, :])
                p.inc(inst, "tr_ev")

        @blk.sync
        def _(sy):
            for c in range(KC):
                p.wait(sy, "tr_ev", E0 + c + 1)
                inst = sy.dma_start(lnT[c * 128:(c + 1) * 128, r0:r0 + 128],
                                    stage[c % 2][:, 0:128])
                p.inc(inst, "dOa" if c % 2 == 0 else "dOb", 16)










def emit_transpose_to_slab2(p, ident, src_bf, slab, t0, T, psB, nchunks=KC):
    nc = p.nc
    with nc.Block() as blk:
        E0 = p.cnt["tr_ev"]
        M0 = p.cnt["tr_mm"]

        @blk.tensor
        def _(t):
            for c in range(nchunks):
                if E0 + c - 1 > 0:
                    p.wait(t, "tr_ev", E0 + c - 1)
                inst = t.transpose(psB[c % 2][:, 0:128], src_bf[:, c * 128:(c + 1) * 128],
                                   ident[:, :])
                p.inc(inst, "tr_mm")

        @blk.vector
        def _(v):
            for c in range(nchunks):
                p.wait(v, "tr_mm", M0 + c + 1)
                inst = v.tensor_copy(slab[:, c * T + t0: c * T + t0 + 128], psB[c % 2][:, 0:128])
                p.inc(inst, "tr_ev")


def emit_moe(nc, p, pool, d):
    ln2T, x1s, wrTe, weT = d["ln2T"], d["x1s"], d["wrTe"], d["weT"]
    x2, lnT, c_accD = d["x2"], d["lnT"], d["c_accD"]
    brh_sb, eb_sb, ones_bf, cT_sb, c_col = d["brh_sb"], d["eb_sb"], d["ones_bf"], d["cT_sb"], d["c_col"]
    identF, ident, sums, mean, rstd = d["identF"], d["ident"], d["sums"], d["mean"], d["rstd"]
    cstage, psA, psB, psC = d["cstage"], d["psA"], d["psB"], d["psC"]
    ps_c = psC[0:1, 0:512]
    ps_ct = psC[:, 0:1]
    p.sem("es")

    pool.reset()
    ln2T_sb = pool.take(KC * TOK, BF16)
    base = pool.mark()
    lg4 = [pool.take(8 * 512, BF16) for _ in range(2)]
    tmp = pool.take(11 * 512, BF16)
    bc = pool.take(8 * 512, BF16)
    aw = pool.take(8 * 512, BF16)
    bg = pool.take(8 * 512, BF16)
    wslab = [pool.take(KC * 1024, BF16) for _ in range(2)]
    ps_cb = [psB[i].bitcast(F32)[0:1, 0:512] for i in range(4)]
    p.sem("cs")
    p.sem("gg")

    p.sem("dGa")
    p.sem("dGb")
    p.sem("dCa")
    p.sem("dCb")

    p.sem("dRa")
    p.sem("dRb")
    p.sem("dTa")
    p.sem("dTb")
    SPL = {"g": range(0, 16), "s": range(0, 0), "c": range(0, 0)}
    SLQ = {"g": ("dGa", "dGb"), "s": ("dRa", "dRb"), "c": ("dTa", "dTb")}
    G0 = {k: {b: p.cnt[SLQ[k][b]] for b in range(2)} for k in SLQ}

    def load_router_slab(eng, kind, oc):
        buf = oc % 2
        sm = SLQ[kind][buf]
        for c in SPL[kind]:
            inst = eng.dma_start(wslab[buf][:, c * 1024:(c + 1) * 1024],
                                 wrTe[c * 128:(c + 1) * 128, oc * 1024:(oc + 1) * 1024])
            p.inc(inst, sm, 16)

    def rslab_cnt(kind, oc):
        return G0[kind][oc % 2] + 16 * len(SPL[kind]) * (oc // 2 + 1)

    with nc.Block() as blk:
        @blk.sync
        def _(sy):
            p.wait_all(sy, ["dOa", "dOb", "dOx"])
            for c in range(KC):
                inst = sy.dma_start(ln2T_sb[:, c * TOK:(c + 1) * TOK],
                                    ln2T[c * 128:(c + 1) * 128, :])
                p.inc(inst, "dS", 16)
            load_router_slab(sy, "s", 0)
            p.wait(sy, "dS")

        @blk.scalar
        def _(sc):
            load_router_slab(sc, "c", 0)

        @blk.vector
        def _(v):
            p.chain(v, "v", v.memset, tmp[:, 0:1024], 0.0)

        @blk.gpsimd
        def _(g):
            load_router_slab(g, "g", 0)

    with nc.Block() as blk:
        @blk.gpsimd
        def _(g):
            inst = g.dma_start(c_accD.rearrange("a b -> (a b)").rearrange("(p f) -> p f", p=32),
                               tmp[0:32, 0:1024].bitcast(F32))
            p.inc(inst, "dG", 16)
            p.wait(g, "dG")

    NOC = 16
    NGT = 4 * NOC
    with nc.Block() as blk:
        M0 = p.cnt["mm"]
        EV0 = p.cnt["ev"]
        ES0 = p.cnt["es"]
        SG0 = p.cnt["sg"]
        CM0 = p.cnt["cm"]
        CS0 = p.cnt["cs"]
        GCa = {0: p.cnt["dCa"], 1: p.cnt["dCb"]}
        GG0 = p.cnt["gg"]
        va = {}
        cva = {}
        vtt = {}

        def evac_v_cnt(gt, e):
            return EV0 + 4 * gt + e // 2 + 1

        def evac_s_cnt(gt, e):
            return ES0 + 4 * gt + (e + 1) // 2

        @blk.sync
        def _(sy):
            if SPL["s"]:
                for oc in range(1, NOC):
                    p.wait(sy, "mm", M0 + 32 * (oc - 1))
                    load_router_slab(sy, "s", oc)


        @blk.vector
        def _(v):
            for gt in range(NGT):
                oc = gt // 4
                lgc = lg4[gt % 2]
                LG = lgc[:, 0:4096]
                BC = bc[:, 0:4096]
                AW = aw[:, 0:4096]

                def L(e_):
                    return lgc[:, e_ * 512:(e_ + 1) * 512]

                def T(i_):
                    return tmp[:, i_ * 512:(i_ + 1) * 512]

                for e in range(0, 8, 2):
                    p.wait(v, "mm", M0 + 8 * gt + e + 1)
                    if e == 0 and gt >= 2:
                        p.wait(v, "cm", CM0 + 8 * (gt - 2) + 8)
                    if p.cnt["ev"]:
                        v.wait_ge(p.sem("ev"), p.cnt["ev"])
                    inst = v.tensor_scalar(
                        L(e), psA[(8 * gt + e) % 2][:, :],
                        brh_sb[:, e * 16 + oc: e * 16 + oc + 1],
                        None, op0=OP.add)
                    p.inc(inst, "ev")
                p.wait(v, "es", evac_s_cnt(gt, 7))
                for pr in range(4):
                    p.chain(v, "v", v.tensor_tensor, T(pr), L(2 * pr), L(2 * pr + 1), op=OP.max)
                    p.chain(v, "v", v.tensor_tensor, T(4 + pr), L(2 * pr), L(2 * pr + 1), op=OP.min)
                for q in range(2):
                    p.chain(v, "v", v.tensor_tensor, T(8), T(2 * q), T(2 * q + 1), op=OP.min)
                    p.chain(v, "v", v.tensor_tensor, T(2 * q), T(2 * q), T(2 * q + 1), op=OP.max)
                    p.chain(v, "v", v.tensor_tensor, T(2 * q + 1), T(4 + 2 * q), T(5 + 2 * q), op=OP.max)
                    p.chain(v, "v", v.tensor_tensor, T(2 * q + 1), T(2 * q + 1), T(8), op=OP.max)
                p.chain(v, "v", v.tensor_tensor, T(8), T(0), T(2), op=OP.min)
                p.chain(v, "v", v.tensor_tensor, T(9), T(0), T(2), op=OP.max)
                p.chain(v, "v", v.tensor_tensor, T(10), T(1), T(3), op=OP.max)
                p.chain(v, "v", v.tensor_tensor, T(10), T(10), T(8), op=OP.max)
                p.chain(v, "v", v.tensor_tensor, T(8), T(9), T(10), op=OP.add)
                vtt[gt] = p.cnt["v"]
                p.chain(v, "v", v.tensor_copy, bc[:, 0:512], T(8))
                p.chain(v, "v", v.tensor_copy, bc[:, 512:1024], bc[:, 0:512])
                p.chain(v, "v", v.tensor_copy, bc[:, 1024:2048], bc[:, 0:1024])
                p.chain(v, "v", v.tensor_copy, bc[:, 2048:4096], bc[:, 0:2048])
                p.chain(v, "v", v.scalar_tensor_tensor, AW, LG, 2.0, BC,
                        op0=OP.mult, op1=OP.subtract)
                va[gt] = p.cnt["v"]
                p.chain(v, "v", v.tensor_copy, bg[:, 0:512], T(10))
                p.chain(v, "v", v.tensor_copy, bg[:, 512:1024], bg[:, 0:512])
                p.chain(v, "v", v.tensor_copy, bg[:, 1024:2048], bg[:, 0:1024])
                p.chain(v, "v", v.tensor_copy, bg[:, 2048:4096], bg[:, 0:2048])
                p.chain(v, "v", v.tensor_tensor, bg[:, 0:4096], LG, bg[:, 0:4096],
                        op=OP.is_ge)
                p.wait(v, "sg", SG0 + gt + 1)
                p.chain(v, "v", v.tensor_tensor, LG, bg[:, 0:4096], AW, op=OP.mult)
                cva[gt] = p.cnt["v"]

        @blk.gpsimd
        def _(g):
            def accums(tb):
                for e in range(8):
                    p.wait(g, "cs", CS0 + 8 * tb + e + 1)
                    inst = g.dma_start(
                        c_accD[e:e + 1, (tb % 4) * 512:(tb % 4 + 1) * 512],
                        cstage[e % 2][:, :], accum_op=OP.add)
                    p.inc(inst, "dCa" if e % 2 == 0 else "dCb", 16)
            for gt in range(NGT):
                oc = gt // 4
                if gt % 4 == 0 and oc + 1 < NOC:
                    p.wait(g, "mm", M0 + 32 * oc)
                    load_router_slab(g, "g", oc + 1)
                if gt >= 1:
                    accums(gt - 1)
            accums(NGT - 1)

        @blk.scalar
        def _(sc):
            def cstage_evacs(tb):
                for e in range(8):
                    gdx = 8 * tb + e
                    p.wait(sc, "cm", CM0 + gdx + 1)
                    sm = "dCa" if e % 2 == 0 else "dCb"
                    tgt = GCa[e % 2] + (16 * ((gdx - 2) // 2 + 1) if gdx >= 2 else 0)
                    if tgt:
                        p.wait(sc, sm, tgt)
                    inst = sc.copy(cstage[e % 2][:, :], ps_cb[gdx % 4])
                    p.inc(inst, "cs")
            for gt in range(NGT):
                oc = gt // 4
                lgc = lg4[gt % 2]
                for e in range(1, 8, 2):
                    p.wait(sc, "mm", M0 + 8 * gt + e + 1)
                    if e == 1 and gt >= 2:
                        p.wait(sc, "cm", CM0 + 8 * (gt - 2) + 8)
                    inst = sc.add(
                        lgc[:, e * 512:(e + 1) * 512],
                        psA[(8 * gt + e) % 2][:, :],
                        brh_sb[:, e * 16 + oc: e * 16 + oc + 1])
                    p.inc(inst, "es")
                    if e == 1 and gt % 4 == 0 and oc + 1 < NOC and SPL["c"]:
                        load_router_slab(sc, "c", oc + 1)
                if gt >= 1:
                    cstage_evacs(gt - 1)
                p.wait(sc, "v", va[gt])
                inst = sc.activation(aw[:, 0:4096], aw[:, 0:4096], AF.Sigmoid)
                p.inc(inst, "sg")
            cstage_evacs(NGT - 1)

        @blk.tensor
        def _(t):
            def ones(tb):
                lgc = lg4[tb % 2]
                p.wait(t, "v", cva[tb])
                for e in range(8):
                    gdx = 8 * tb + e
                    csw = CS0 + gdx - 3 if gdx >= 4 else CS0
                    if csw > 0:
                        p.wait(t, "cs", csw)
                    inst = t.matmul(ps_cb[gdx % 4], ones_bf[:, :],
                                    lgc[:, e * 512:(e + 1) * 512],
                                    start=True, stop=True)
                    p.inc(inst, "cm")
            for gt in range(NGT):
                oc = gt // 4
                if gt % 4 == 0:
                    for kind in ("g", "s", "c"):
                        if SPL[kind]:
                            t.wait_ge(p.sem(SLQ[kind][oc % 2]), rslab_cnt(kind, oc))
                for e in range(8):
                    j8 = 8 * gt + e
                    if j8 >= 2:
                        pe = j8 - 2
                        pt, pee = pe // 8, pe % 8
                        if pee % 2 == 0:
                            p.wait(t, "ev", evac_v_cnt(pt, pee))
                        else:
                            p.wait(t, "es", evac_s_cnt(pt, pee))
                    for c in range(KC):
                        inst = t.matmul(
                            psA[j8 % 2][:, :],
                            wslab[oc % 2][:, c * 1024 + e * 128: c * 1024 + (e + 1) * 128],
                            ln2T_sb[:, c * TOK + (gt % 4) * 512: c * TOK + (gt % 4 + 1) * 512],
                            start=(c == 0), stop=(c == KC - 1))
                    p.inc(inst, "mm")
                if gt >= 1:
                    ones(gt - 1)
            ones(NGT - 1)

    pool.reset(base)
    arena = pool.take(KC * 512, F32)
    we_sb = [pool.take(KC * 512, BF16) for _ in range(2)]
    c_row = pool.take(TOK, F32)[0:1, :]
    aslot2 = [pool.take(512, F32) for _ in range(2)]
    sslot2 = [pool.take(512, F32) for _ in range(2)]

    with nc.Block() as blk:
        @blk.sync
        def _(sy):
            p.wait_all(sy, ["dCa", "dCb"])
    for e in range(8):
        with nc.Block() as blk:
            @blk.sync
            def _(sy):
                inst = sy.dma_start(c_row[:, :], c_accD[e:e + 1, :])
                p.inc(inst, "dS", 16)
                p.wait(sy, "dS")

            @blk.gpsimd
            def _(g):
                inst = g.dma_start(cT_sb[e:e + 1, :], c_accD[e:e + 1, :])
                p.inc(inst, "dG", 16)
                p.wait(g, "dG")
        with nc.Block() as blk:
            M0 = p.cnt["tr_mm"]
            E0 = p.cnt["tr_ev"]

            @blk.tensor
            def _(t):
                for tc in range(16):
                    if E0 + tc > 0:
                        p.wait(t, "tr_ev", E0 + tc)
                    inst = t.transpose(ps_ct, c_row[0:1, tc * 128:(tc + 1) * 128],
                                       identF[0:1, 0:1])
                    p.inc(inst, "tr_mm")

            @blk.vector
            def _(v):
                for tc in range(16):
                    p.wait(v, "tr_mm", M0 + tc + 1)
                    inst = v.tensor_copy(c_col[:, tc * 8 + e: tc * 8 + e + 1], ps_ct)
                    p.inc(inst, "tr_ev")

    E0Q = {k: {b: p.cnt[SLQ[k][b]] for b in range(2)} for k in SLQ}

    def load_we(eng, kind, k):
        oc4, e = k // 8, k % 8
        sm = SLQ[kind][k % 2]
        for c in SPL[kind]:
            inst = eng.dma_start(
                we_sb[k % 2][:, c * 512:(c + 1) * 512],
                weT[c * 128:(c + 1) * 128, e * D + oc4 * 512: e * D + (oc4 + 1) * 512])
            p.inc(inst, sm, 16)

    def weslab_cnt(kind, k):
        return E0Q[kind][k % 2] + 16 * len(SPL[kind]) * (k // 2 + 1)

    with nc.Block() as blk:
        @blk.gpsimd
        def _(g):
            load_we(g, "g", 0)

        @blk.sync
        def _(sy):
            load_we(sy, "s", 0)

        @blk.scalar
        def _(sc):
            load_we(sc, "c", 0)

    for oc4 in range(4):
        with nc.Block() as blk:
            M0 = p.cnt["mm"]
            E0 = p.cnt["ev"]

            @blk.tensor
            def _(t):
                for tt in range(16):
                    if E0 + tt - 1 > 0:
                        p.wait(t, "ev", E0 + tt - 1)
                    inst = t.matmul(psA[tt % 2][:, :], cT_sb[:, tt * 128:(tt + 1) * 128],
                                    eb_sb[:, oc4 * 512:(oc4 + 1) * 512],
                                    start=True, stop=True)
                    p.inc(inst, "mm")

            @blk.vector
            def _(v):
                for tt in range(16):
                    p.wait(v, "mm", M0 + tt + 1)
                    inst = v.tensor_copy(arena[:, tt * 512:(tt + 1) * 512], psA[tt % 2][:, :])
                    p.inc(inst, "ev")
        for e in range(8):
            with nc.Block() as blk:
                M0 = p.cnt["mm"]
                E0 = p.cnt["ev"]
                kslab = 8 * oc4 + e

                @blk.gpsimd
                def _(g):
                    if kslab + 1 < 32:
                        load_we(g, "g", kslab + 1)

                @blk.sync
                def _(sy):
                    if kslab + 1 < 32:
                        load_we(sy, "s", kslab + 1)

                @blk.scalar
                def _(sc):
                    if kslab + 1 < 32:
                        load_we(sc, "c", kslab + 1)

                @blk.tensor
                def _(t):
                    for kind in ("g", "s", "c"):
                        if SPL[kind]:
                            t.wait_ge(p.sem(SLQ[kind][kslab % 2]), weslab_cnt(kind, kslab))
                    for tt in range(16):
                        if E0 + tt - 1 > 0:
                            p.wait(t, "ev", E0 + tt - 1)
                        for c in range(KC):
                            inst = t.matmul(
                                psA[tt % 2][:, :],
                                ln2T_sb[:, c * TOK + tt * 128: c * TOK + (tt + 1) * 128],
                                we_sb[e % 2][:, c * 512:(c + 1) * 512],
                                start=(c == 0), stop=(c == KC - 1))
                        p.inc(inst, "mm")

                @blk.vector
                def _(v):
                    for tt in range(16):
                        p.wait(v, "mm", M0 + tt + 1)
                        inst = v.scalar_tensor_tensor(
                            arena[:, tt * 512:(tt + 1) * 512], psA[tt % 2][:, :],
                            c_col[:, tt * 8 + e: tt * 8 + e + 1],
                            arena[:, tt * 512:(tt + 1) * 512],
                            op0=OP.mult, op1=OP.add)
                        p.inc(inst, "ev")
        p.sem("dXa")
        p.sem("dXb")
        with nc.Block() as blk:
            DX0 = {0: p.cnt["dXa"], 1: p.cnt["dXb"]}
            EV0 = p.cnt["ev"]
            PA0 = {0: p.cnt["dAa"], 1: p.cnt["dAb"]}

            @blk.sync
            def _(sy):
                for tt in range(16):
                    if tt >= 2:
                        p.wait(sy, "ev", EV0 + tt - 1)
                    inst = sy.dma_start(aslot2[tt % 2][:, :],
                                        x1s[tt * 128:(tt + 1) * 128, oc4 * 512:(oc4 + 1) * 512])
                    p.inc(inst, "dXa" if tt % 2 == 0 else "dXb", 16)

            @blk.vector
            def _(v):
                for tt in range(16):
                    p.wait(v, "dXa" if tt % 2 == 0 else "dXb",
                           DX0[tt % 2] + 16 * (tt // 2 + 1))
                    pw = PA0[tt % 2] + 16 * (tt // 2)
                    if pw > 0:
                        p.wait(v, "dAa" if tt % 2 == 0 else "dAb", pw)
                    inst = v.tensor_tensor(sslot2[tt % 2][:, :],
                                           arena[:, tt * 512:(tt + 1) * 512],
                                           aslot2[tt % 2][:, :], op=OP.add)
                    p.inc(inst, "ev")

            @blk.scalar
            def _(s):
                for tt in range(16):
                    p.wait(s, "ev", EV0 + tt + 1)
                    inst = s.dma_start(x2[tt * 128:(tt + 1) * 128, oc4 * 512:(oc4 + 1) * 512],
                                       sslot2[tt % 2][:, :])
                    p.inc(inst, "dAa" if tt % 2 == 0 else "dAb", 16)

    pool.reset(base)
    x2_t = pool.take(D, F32)
    scratch = pool.take(D, F32)
    ln_bf = pool.take(D, BF16)
    stage3 = [pool.take(128, BF16) for _ in range(2)]
    with nc.Block() as blk:
        @blk.gpsimd
        def _(g):
            p.wait_all(g, ["dAa", "dAb"])
    for tt in range(TOK // 128):
        r0 = tt * 128
        with nc.Block() as blk:
            @blk.sync
            def _(sy):
                inst = sy.dma_start(x2_t[:, :], x2[r0:r0 + 128, :])
                p.inc(inst, "dS", 16)
                p.wait(sy, "dS")
        emit_ln(p, None, x2_t, ln_bf, sums, mean, rstd, scratch)
        emit_lnT_store(p, ident, ln_bf, lnT, r0, [psB[0][:, 0:128], psB[1][:, 0:128]], stage3)


def emit_l1attn(nc, p, pool, d):
    ln3T, wqkvT, ao1 = d["ln3T"], d["wqkvT"], d["ao1"]
    bdmask, ident = d["bdmask"], d["ident"]
    psA, psB, psO, psC = d["psA"], d["psB"], d["psO"], d["psC"]
    ps_s = [psA[0][:, 0:128], psA[1][:, 0:128]]
    ps_tr = [psB[0][:, 0:128], psB[1][:, 0:128]]
    ps_o = [psO[:, 0:128], psC[:, 0:128]]

    pool.reset()
    ln3_sb = pool.take(KC * TOK, BF16)
    ltmp = pool.take(TOK, BF16)
    w_sb = pool.take(KC * 768, BF16)
    qkT_sb = pool.take(5 * TOK, BF16)
    v_sb = pool.take(16 * 128, BF16)
    s_t = [pool.take(128, F32) for _ in range(2)]
    p_t = [pool.take(128, BF16) for _ in range(2)]
    pn_t = [pool.take(128, BF16) for _ in range(2)]
    pT_t = [pool.take(128, BF16) for _ in range(2)]
    negmax = pool.take(1, F32)
    rowsum = pool.take(1, F32)
    rinv = pool.take(1, F32)
    stage4 = [pool.take(128, BF16) for _ in range(2)]

    ao1p = ao1.rearrange("(b t) d -> t b d", b=B)

    for c in range(KC):
        with nc.Block() as blk:
            @blk.sync
            def _(sy):
                if c == 0:
                    p.wait_all(sy, ["dOa", "dOb", "dOx"])
                inst = sy.dma_start(ltmp[:, :], ln3T[c * 128:(c + 1) * 128, :])
                p.inc(inst, "dS", 16)
                p.wait(sy, "dS")
        with nc.Block() as blk:
            @blk.vector
            def _(v):
                p.chain(v, "v", v.tensor_copy,
                        ln3_sb[:, c * TOK:(c + 1) * TOK].rearrange("p (t b) -> p t b", b=B),
                        ltmp[:, :].rearrange("p (b t) -> p t b", b=B))

    def rhs_jb(c, t4):
        return ln3_sb[:, c * TOK + t4 * 512: c * TOK + (t4 + 1) * 512]

    def lhs_jb(c, tt):
        return ln3_sb[:, c * TOK + tt * 128: c * TOK + (tt + 1) * 128]

    SCALE = 1.0 / np.sqrt(128.0)
    for g_ in range(4):
        with nc.Block() as blk:
            @blk.gpsimd
            def _(g):
                for c in range(KC):
                    inst = g.dma_start(w_sb[:, c * 768:(c + 1) * 768],
                                       wqkvT[c * 128:(c + 1) * 128, g_ * 768:(g_ + 1) * 768])
                    p.inc(inst, "dG", 16)
                p.wait(g, "dG")
        with nc.Block() as blk:
            M0 = p.cnt["mm"]
            E0 = p.cnt["ev"]

            @blk.tensor
            def _(t):
                for j, (ch, t4) in enumerate([(c_, t_) for c_ in range(5) for t_ in range(4)]):
                    if E0 + j - 1 > 0:
                        p.wait(t, "ev", E0 + j - 1)
                    for c in range(KC):
                        inst = t.matmul(
                            psA[j % 2][:, :],
                            w_sb[:, c * 768 + ch * 128: c * 768 + (ch + 1) * 128],
                            rhs_jb(c, t4),
                            start=(c == 0), stop=(c == KC - 1))
                    p.inc(inst, "mm")

            @blk.vector
            def _(v):
                for j, (ch, t4) in enumerate([(c_, t_) for c_ in range(5) for t_ in range(4)]):
                    p.wait(v, "mm", M0 + j + 1)
                    inst = v.tensor_copy(
                        qkT_sb[:, ch * TOK + t4 * 512: ch * TOK + (t4 + 1) * 512],
                        psA[j % 2][:, :])
                    p.inc(inst, "ev")
        with nc.Block() as blk:
            M0 = p.cnt["mm"]
            E0 = p.cnt["ev"]

            @blk.tensor
            def _(t):
                for tt in range(16):
                    if E0 + tt - 1 > 0:
                        p.wait(t, "ev", E0 + tt - 1)
                    for c in range(KC):
                        inst = t.matmul(
                            psA[tt % 2][:, 0:128],
                            lhs_jb(c, tt),
                            w_sb[:, c * 768 + 640: c * 768 + 768],
                            start=(c == 0), stop=(c == KC - 1))
                    p.inc(inst, "mm")

            @blk.vector
            def _(v):
                for tt in range(16):
                    p.wait(v, "mm", M0 + tt + 1)
                    inst = v.tensor_copy(v_sb[:, tt * 128:(tt + 1) * 128],
                                         psA[tt % 2][:, 0:128])
                    p.inc(inst, "ev")
        for tt in range(16):
            with nc.Block() as blk:
                V0 = p.cnt["v"]
                S0 = p.cnt["s"]
                T0 = p.cnt["t"]
                P0E = {0: p.cnt["dOa"], 1: p.cnt["dOb"]}

                @blk.tensor
                def _(t):
                    for hh in range(4):
                        if hh >= 2:
                            p.wait(t, "v", V0 + 6 * (hh - 2) + 1)
                        inst = t.matmul(ps_s[hh % 2],
                                        qkT_sb[:, hh * TOK + tt * 128: hh * TOK + (tt + 1) * 128],
                                        qkT_sb[:, 4 * TOK + tt * 128: 4 * TOK + (tt + 1) * 128],
                                        start=True, stop=True)
                        p.inc(inst, "t")
                        p.wait(t, "v", V0 + 6 * hh + 4)
                        if hh >= 2:
                            p.wait(t, "v", V0 + 6 * (hh - 2) + 5)
                        inst = t.transpose(ps_tr[hh % 2], pn_t[hh % 2][:, :], ident[:, :])
                        p.inc(inst, "t")
                        p.wait(t, "v", V0 + 6 * hh + 5)
                        if hh >= 2:
                            p.wait(t, "v", V0 + 6 * (hh - 2) + 6)
                        inst = t.matmul(ps_o[hh % 2], pT_t[hh % 2][:, :],
                                        v_sb[:, tt * 128:(tt + 1) * 128],
                                        start=True, stop=True)
                        p.inc(inst, "t")

                @blk.vector
                def _(v):
                    for hh in range(4):
                        p.wait(v, "t", T0 + 3 * hh + 1)
                        if p.cnt["v"]:
                            v.wait_ge(p.sem("v"), p.cnt["v"])
                        inst = v.scalar_tensor_tensor(s_t[hh % 2][:, :], ps_s[hh % 2],
                                                      SCALE, bdmask[:, :],
                                                      op0=OP.mult, op1=OP.add)
                        p.inc(inst, "v")
                        if hh >= 1:
                            p.wait(v, "s", S0 + hh)
                        v.wait_ge(p.sem("v"), p.cnt["v"])
                        inst = v.reduce_max(negmax[:, :], s_t[hh % 2][:, :], axis=AX,
                                            negate=True)
                        p.inc(inst, "v")
                        p.wait(v, "s", S0 + hh + 1)
                        v.wait_ge(p.sem("v"), p.cnt["v"])
                        inst = v.reciprocal(rinv[:, :], rowsum[:, :])
                        p.inc(inst, "v")
                        v.wait_ge(p.sem("v"), p.cnt["v"])
                        inst = v.tensor_scalar(pn_t[hh % 2][:, :], p_t[hh % 2][:, :],
                                               rinv[:, 0:1], None, op0=OP.mult)
                        p.inc(inst, "v")
                        p.wait(v, "t", T0 + 3 * hh + 2)
                        v.wait_ge(p.sem("v"), p.cnt["v"])
                        inst = v.tensor_copy(pT_t[hh % 2][:, :], ps_tr[hh % 2])
                        p.inc(inst, "v")
                        p.wait(v, "t", T0 + 3 * hh + 3)
                        pw = P0E[hh % 2] + 16 * (hh // 2)
                        if pw > 0:
                            p.wait(v, "dOa" if hh % 2 == 0 else "dOb", pw)
                        v.wait_ge(p.sem("v"), p.cnt["v"])
                        inst = v.tensor_copy(stage4[hh % 2][:, :], ps_o[hh % 2])
                        p.inc(inst, "v")

                @blk.scalar
                def _(s):
                    for hh in range(4):
                        p.wait(s, "v", V0 + 6 * hh + 2)
                        if hh >= 2:
                            p.wait(s, "v", V0 + 6 * (hh - 2) + 4)
                        inst = s.activation(p_t[hh % 2][:, :], s_t[hh % 2][:, :], AF.Exp,
                                            bias=negmax[:, 0:1], accum_out=rowsum[:, :])
                        p.inc(inst, "s")

                @blk.sync
                def _(sy):
                    for hh in range(4):
                        p.wait(sy, "v", V0 + 6 * hh + 6)
                        h = 4 * g_ + hh
                        inst = sy.dma_start(
                            ao1p[tt * 32:(tt + 1) * 32, :, h * 128:(h + 1) * 128],
                            stage4[hh % 2][:, :])
                        p.inc(inst, "dOa" if hh % 2 == 0 else "dOb", 16)


def emit_l1ffn(nc, p, pool, d):
    ln4T, x3s, wgG, wuG, wdG, x4 = (d["ln4T"], d["x3s"], d["wgG"], d["wuG"],
                                    d["wdG"], d["x4"])
    psA, psB = d["psA"], d["psB"]
    psH = [psA[0][:, :], psA[1][:, :],
           psB[2][:, :].bitcast(F32), psB[3][:, :].bitcast(F32)]
    TH = 1024
    pool.reset()
    ln4h = pool.take(KC * TH, BF16)
    y_sb = pool.take(8 * D, F32)
    gu_sb = [pool.take(KC * 512, BF16) for _ in range(2)]
    wd_sb = pool.take(4 * D, BF16)
    g_sb = pool.take(4 * TH, F32)
    u_sb = pool.take(4 * TH, F32)
    tmp = pool.take(4 * TH, F32)
    inter = u_sb[:, 0:2 * TH].bitcast(BF16)

    p.sem("es")
    bank_last = [None] * 4

    p.sem("dSg")
    p.sem("dSu")
    p.sem("dSw")

    def load_gu(sy, wsrc, nb, buf):
        sm = "dSg" if buf == 0 else "dSu"
        rb = (nb // 4) * D
        nl0 = (nb % 4) * 512
        for c in range(KC):
            inst = sy.dma_start(gu_sb[buf][:, c * 512:(c + 1) * 512],
                                wsrc[rb + c * 128:rb + (c + 1) * 128, nl0:nl0 + 512])
            p.inc(inst, sm, 16)
        return p.cnt[sm]

    def load_wd(sy, nb):
        n0 = nb * 512
        for nc4 in range(4):
            inst = sy.dma_start(wd_sb[:, nc4 * D:(nc4 + 1) * D],
                                wdG[n0 + nc4 * 128:n0 + (nc4 + 1) * 128, :])
            p.inc(inst, "dSw", 16)
        return p.cnt["dSw"]

    with nc.Block() as blk:
        @blk.gpsimd
        def _(g):
            p.wait(g, "cc", 3)
            p.wait_all(g, ["dOa", "dOb", "dOx"])

    gate_dS = [None]
    for th in range(2):
        t0 = th * TH
        with nc.Block() as blk:
            @blk.sync
            def _(sy):
                for c in range(KC):
                    inst = sy.dma_start(ln4h[:, c * TH:(c + 1) * TH],
                                        ln4T[c * 128:(c + 1) * 128, t0:t0 + TH])
                    p.inc(inst, "dS", 16)
                if th == 0:
                    gate_dS[0] = load_gu(sy, wgG, 0, 0)
                p.wait(sy, "dS")
        for nb in range(32):
            jobs8 = [(a, b_) for a in range(4) for b_ in range(2)]
            for i in range(4):
                bank_last[i] = None
            with nc.Block() as blk:
                M0 = p.cnt["mm"]
                EV0 = p.cnt["ev"]
                ES0 = p.cnt["es"]
                G0 = p.cnt["g"]
                up_dS = [None]

                @blk.sync
                def _(sy):
                    up_dS[0] = load_gu(sy, wuG, nb, 1)

                @blk.tensor
                def _(t):
                    t.wait_ge(p.sem("dSg"), gate_dS[0])
                    for j, (nc4, t4) in enumerate(jobs8):
                        if bank_last[j % 4] is not None:
                            p.wait(t, *bank_last[j % 4])
                        for c in range(KC):
                            inst = t.matmul(
                                psH[j % 4],
                                gu_sb[0][:, c * 512 + nc4 * 128: c * 512 + (nc4 + 1) * 128],
                                ln4h[:, c * TH + t4 * 512: c * TH + (t4 + 1) * 512],
                                start=(c == 0), stop=(c == KC - 1))
                        p.inc(inst, "mm")
                        bank_last[j % 4] = ("ev" if j % 2 == 0 else "es",
                                            (EV0 if j % 2 == 0 else ES0) + j // 2 + 1)

                @blk.vector
                def _(v):
                    for j, (nc4, t4) in enumerate(jobs8):
                        if j % 2 == 0:
                            p.wait(v, "mm", M0 + j + 1)
                            inst = v.tensor_copy(
                                g_sb[:, nc4 * TH + t4 * 512: nc4 * TH + (t4 + 1) * 512],
                                psH[j % 4])
                            p.inc(inst, "ev")
                    p.wait(v, "g", G0 + 2)
                    p.chain(v, "v", v.tensor_tensor, tmp[:, :], tmp[:, :], g_sb[:, :], op=OP.mult)

                @blk.scalar
                def _(sc):
                    for j, (nc4, t4) in enumerate(jobs8):
                        if j % 2 == 1:
                            p.wait(sc, "mm", M0 + j + 1)
                            inst = sc.copy(
                                g_sb[:, nc4 * TH + t4 * 512: nc4 * TH + (t4 + 1) * 512],
                                psH[j % 4])
                            p.inc(inst, "es")

                @blk.gpsimd
                def _(g):
                    p.wait(g, "ev", EV0 + 4)
                    p.wait(g, "es", ES0 + 4)
                    p.chain(g, "g", g.tensor_tensor, tmp[:, :], g_sb[:, :], g_sb[:, :], op=OP.mult)
                    p.chain(g, "g", g.tensor_scalar, tmp[:, :], tmp[:, :], 0.044715, 1.0,
                            op0=OP.mult, op1=OP.add)
            for i in range(4):
                bank_last[i] = None
            with nc.Block() as blk:
                M0 = p.cnt["mm"]
                EV0 = p.cnt["ev"]
                ES0 = p.cnt["es"]
                V0 = p.cnt["v"]
                S0 = p.cnt["s"]
                upl = up_dS[0]
                wd_dS = [None]
                jobs32 = [(a, b_) for a in range(8) for b_ in range(4)]
                sched = {}
                nev, nes = 0, 0
                for j in range(8):
                    if j % 2 == 0:
                        nev += 1
                        sched[j] = ("ev", EV0 + nev)
                    else:
                        nes += 1
                        sched[j] = ("es", ES0 + nes)
                for j32 in range(32):
                    nev += 1
                    sched[8 + j32] = ("ev", EV0 + nev)

                @blk.sync
                def _(sy):
                    wd_dS[0] = load_wd(sy, nb)
                    if nb + 1 < 32:
                        gate_dS[0] = load_gu(sy, wgG, nb + 1, 0)
                    elif th == 0:
                        gate_dS[0] = load_gu(sy, wgG, 0, 0)

                @blk.scalar
                def _(sc):
                    inst = sc.activation(tmp[:, :], tmp[:, :], AF.Tanh,
                                         scale=0.7978845608028654)
                    p.inc(inst, "s")
                    p.wait(sc, "s", S0 + 1)
                    inst = sc.add(tmp[:, :], tmp[:, :], 1.0)
                    p.inc(inst, "s")
                    for j, (nc4, t4) in enumerate(jobs8):
                        if j % 2 == 1:
                            p.wait(sc, "mm", M0 + j + 1)
                            inst = sc.copy(
                                u_sb[:, nc4 * TH + t4 * 512: nc4 * TH + (t4 + 1) * 512],
                                psH[j % 4])
                            p.inc(inst, "es")

                @blk.tensor
                def _(t):
                    t.wait_ge(p.sem("dSu"), upl)
                    for j, (nc4, t4) in enumerate(jobs8):
                        if bank_last[j % 4] is not None:
                            p.wait(t, *bank_last[j % 4])
                        for c in range(KC):
                            inst = t.matmul(
                                psH[j % 4],
                                gu_sb[1][:, c * 512 + nc4 * 128: c * 512 + (nc4 + 1) * 128],
                                ln4h[:, c * TH + t4 * 512: c * TH + (t4 + 1) * 512],
                                start=(c == 0), stop=(c == KC - 1))
                        p.inc(inst, "mm")
                        bank_last[j % 4] = sched[j]
                    p.wait(t, "v", V0 + 2)
                    t.wait_ge(p.sem("dSw"), wd_dS[0])
                    for j32, (tt, hc) in enumerate(jobs32):
                        J = 8 + j32
                        if bank_last[J % 4] is not None:
                            p.wait(t, *bank_last[J % 4])
                        for nc4 in range(4):
                            inst = t.matmul(
                                psH[J % 4],
                                inter[:, nc4 * TH + tt * 128: nc4 * TH + (tt + 1) * 128],
                                wd_sb[:, nc4 * D + hc * 512: nc4 * D + (hc + 1) * 512],
                                start=(nc4 == 0), stop=(nc4 == 3))
                        p.inc(inst, "mm")
                        bank_last[J % 4] = sched[J]

                @blk.vector
                def _(v):
                    for j, (nc4, t4) in enumerate(jobs8):
                        if j % 2 == 0:
                            p.wait(v, "mm", M0 + j + 1)
                            inst = v.tensor_copy(
                                u_sb[:, nc4 * TH + t4 * 512: nc4 * TH + (t4 + 1) * 512],
                                psH[j % 4])
                            p.inc(inst, "ev")
                    p.wait(v, "es", ES0 + 4)
                    p.chain(v, "v", v.tensor_tensor, g_sb[:, :], g_sb[:, :], u_sb[:, :], op=OP.mult)
                    p.wait(v, "s", S0 + 2)
                    p.chain(v, "v", v.tensor_tensor, inter[:, :], g_sb[:, :], tmp[:, :], op=OP.mult)
                    for j32, (tt, hc) in enumerate(jobs32):
                        J = 8 + j32
                        ysl = y_sb[:, tt * D + hc * 512: tt * D + (hc + 1) * 512]
                        p.wait(v, "mm", M0 + J + 1)
                        if nb == 0:
                            inst = v.tensor_copy(ysl, psH[J % 4])
                        else:
                            inst = v.tensor_tensor(ysl, ysl, psH[J % 4], op=OP.add)
                        p.inc(inst, "ev")
        for tt in range(8):
            r0 = t0 + tt * 128
            with nc.Block() as blk:
                @blk.sync
                def _(sy):
                    inst = sy.dma_start(tmp[:, 0:2048], x3s[r0:r0 + 128, :])
                    p.inc(inst, "dS", 16)
                    p.wait(sy, "dS")
            with nc.Block() as blk:
                @blk.vector
                def _(v):
                    p.chain(v, "v", v.tensor_tensor, g_sb[:, 0:2048],
                            y_sb[:, tt * D:(tt + 1) * D], tmp[:, 0:2048], op=OP.add)
            with nc.Block() as blk:
                @blk.sync
                def _(sy):
                    inst = sy.dma_start(x4[r0:r0 + 128, :], g_sb[:, 0:2048])
                    p.inc(inst, "dO", 16)
                    p.wait(sy, "dO")


def emit_attn_phase(nc, p, pool, d, npairs=8):
    qkR, vR, aoS = d["qkR"], d["vR"], d["aoS"]
    ident, rowid = d["ident"], d["rowid"]
    psA, psB, psO, psC = d["psA"], d["psB"], d["psO"], d["psC"]
    pool.reset()
    MW = 4800
    band = pool.take(MW, F32)
    bandw = pool.take(MW, F32)
    k_sb = [pool.take(S, BF16) for _ in range(2)]
    v_sb = [pool.take(S, BF16) for _ in range(2)]
    q_sb = [pool.take(S, BF16) for _ in range(2)]
    s_sb = [pool.take(S, F32) for _ in range(2)]
    p_sb = [pool.take(S, BF16) for _ in range(2)]
    pT_sb = [pool.take(S, BF16) for _ in range(2)]
    negmax = [pool.take(1, F32) for _ in range(2)]
    rowsum = [pool.take(1, F32) for _ in range(2)]
    rinv = [pool.take(1, F32) for _ in range(2)]
    stage1 = [pool.take(128, BF16) for _ in range(2)]
    p.sem("trs")

    with nc.Block() as blk:
        @blk.gpsimd
        def _(g):
            p.chain(g, "g", g.iota, band[:, :].bitcast(I32), [[1, MW]], channel_multiplier=0)
    with nc.Block() as blk:
        @blk.vector
        def _(v):
            p.chain(v, "v", v.tensor_copy, bandw[:, :], band[:, :].bitcast(I32))
    with nc.Block() as blk:
        @blk.vector
        def _(v):
            p.chain(v, "v", v.tensor_scalar, bandw[:, :], bandw[:, :], rowid[:, 0:1],
                    -4096.0, op0=OP.subtract, op1=OP.add)
    with nc.Block() as blk:
        @blk.vector
        def _(v):
            p.chain(v, "v", v.tensor_scalar, bandw[:, :], bandw[:, :], 0.0, None, op0=OP.is_gt)
    with nc.Block() as blk:
        @blk.vector
        def _(v):
            p.chain(v, "v", v.tensor_scalar, band[:, :], bandw[:, :], -1e9, None, op0=OP.mult)

    SCALE = 1.0 / np.sqrt(128.0)
    NKB = [qt // 4 + 1 for qt in range(32)]
    NSUB = [n * 4 for n in NKB]
    CKB = [sum(NKB[:q]) for q in range(33)]
    CSUB = [sum(NSUB[:q]) for q in range(33)]

    p.sem("dSa")
    p.sem("dSb")

    def load_pair(sy, pj, buf):
        sm = "dSa" if buf == 0 else "dSb"
        inst = sy.dma_start(
            k_sb[buf][:, :].rearrange("p (r t) -> p r t", r=8),
            bass.AP(qkR.tensor, (pj * 256 + 128) * 512,
                    [[512, 128], [2048 * 512, 8], [1, 512]]))
        p.inc(inst, sm, 16)
        inst = sy.dma_start(
            q_sb[buf][:, :].rearrange("p (r t) -> p r t", r=8),
            bass.AP(qkR.tensor, (pj * 256) * 512,
                    [[512, 128], [2048 * 512, 8], [1, 512]]))
        p.inc(inst, sm, 16)
        for r in range(8):
            inst = sy.dma_start(
                v_sb[buf][:, r * 512:(r + 1) * 512].rearrange("p (j d) -> p j d", j=4),
                bass.AP(vR.tensor, (r * 4096 + pj * 512) * 128,
                        [[128, 128], [128 * 128, 4], [1, 128]]))
            p.inc(inst, sm, 16)
        return p.cnt[sm]

    pair_dS = [None, None]
    with nc.Block() as blk:
        @blk.sync
        def _(sy):
            p.wait(sy, "cc", 5)
            pair_dS[0] = load_pair(sy, 0, 0)

    for pj in range(npairs):
        kb_buf = pj % 2
        with nc.Block() as blk:
            SS0 = p.cnt["sS"]
            SE0 = p.cnt["sE"]
            V0 = p.cnt["v"]
            S0 = p.cnt["s"]
            TRM0 = p.cnt["tr_mm"]
            TRE0 = p.cnt["tr_ev"]
            TRS0 = p.cnt["trs"]
            OM0 = p.cnt["o_mm"]
            OE0 = p.cnt["o_ev"]
            P0 = {0: p.cnt["dOa"], 1: p.cnt["dOb"]}
            my_dS = pair_dS[pj % 2]
            ps_o = [psO[:, 0:128], psC[:, 0:128]]

            @blk.sync
            def _(sy):
                if pj + 1 < npairs:
                    pair_dS[(pj + 1) % 2] = load_pair(sy, pj + 1, (pj + 1) % 2)
                for qt in range(32):
                    p.wait(sy, "o_ev", OE0 + qt + 1)
                    r0o = (qt // 4) * 4096 + pj * 512 + (qt % 4) * 128
                    inst = sy.dma_start(aoS[r0o:r0o + 128, :], stage1[qt % 2][:, :])
                    p.inc(inst, "dOa" if qt % 2 == 0 else "dOb", 16)

            @blk.tensor
            def _(t):
                bankB = {}
                t.wait_ge(p.sem("dSa" if kb_buf == 0 else "dSb"), my_dS)

                def t1(qt):
                    for kb in range(NKB[qt]):
                        g_ = CKB[qt] + kb
                        if SE0 + g_ - 1 > SE0 - 2 and g_ >= 2:
                            p.wait(t, "sE", SE0 + g_ - 1)
                        inst = t.matmul(psA[g_ % 2][:, :],
                                        q_sb[kb_buf][:, qt * 128:(qt + 1) * 128],
                                        k_sb[kb_buf][:, kb * 512:(kb + 1) * 512],
                                        start=True, stop=True)
                        p.inc(inst, "sS")

                def t2(qt):
                    nsub = NSUB[qt]

                    def emit_tr(j):
                        gs = CSUB[qt] + j
                        if j == 0:
                            p.wait(t, "s", S0 + qt + 1)
                        if bankB.get(gs % 4) is not None:
                            p.wait(t, *bankB[gs % 4])
                        inst = t.transpose(psB[gs % 4][:, 0:128],
                                           p_sb[qt % 2][:, j * 128:(j + 1) * 128],
                                           ident[:, :])
                        p.inc(inst, "tr_mm")
                        if j % 2 == 0:
                            bankB[gs % 4] = ("tr_ev", TRE0 + CSUB[qt] // 2 + j // 2 + 1)
                        else:
                            bankB[gs % 4] = ("trs", TRS0 + CSUB[qt] // 2 + (j + 1) // 2)

                    def emit_mm(j):
                        if j % 2 == 0:
                            p.wait(t, "tr_ev", TRE0 + CSUB[qt] // 2 + j // 2 + 1)
                        else:
                            p.wait(t, "trs", TRS0 + CSUB[qt] // 2 + (j + 1) // 2)
                        if j == 0 and OE0 + qt - 1 > 0:
                            p.wait(t, "o_ev", OE0 + qt - 1)
                        inst = t.matmul(ps_o[qt % 2], pT_sb[qt % 2][:, j * 128:(j + 1) * 128],
                                        v_sb[kb_buf][:, j * 128:(j + 1) * 128],
                                        start=(j == 0), stop=(j == nsub - 1))
                        if j == nsub - 1:
                            p.inc(inst, "o_mm")

                    for j in range(nsub):
                        emit_tr(j)
                        if j >= 3:
                            emit_mm(j - 3)
                    for j in range(max(0, nsub - 3), nsub):
                        emit_mm(j)

                for qt in range(32):
                    t1(qt)
                    if qt >= 1:
                        t2(qt - 1)
                t2(31)

            @blk.vector
            def _(v):
                for it in range(33):
                    if it < 32:
                        qt = it
                        if qt >= 2:
                            p.wait(v, "s", S0 + qt - 1)
                        for kb in range(NKB[qt]):
                            g_ = CKB[qt] + kb
                            p.wait(v, "sS", SS0 + g_ + 1)
                            mstart = 4096 - qt * 128 + kb * 512
                            inst = v.scalar_tensor_tensor(
                                s_sb[qt % 2][:, kb * 512:(kb + 1) * 512], psA[g_ % 2][:, :],
                                SCALE, band[:, mstart:mstart + 512],
                                op0=OP.mult, op1=OP.add)
                            p.inc(inst, "sE")
                        p.wait(v, "sE", SE0 + CKB[qt] + NKB[qt])
                        if p.cnt["v"]:
                            v.wait_ge(p.sem("v"), p.cnt["v"])
                        inst = v.reduce_max(negmax[qt % 2][:, :],
                                            s_sb[qt % 2][:, 0:NKB[qt] * 512],
                                            axis=AX, negate=True)
                        p.inc(inst, "v")
                    if it >= 1:
                        qt = it - 1
                        for j in range(NSUB[qt]):
                            if j % 2 == 0:
                                p.wait(v, "tr_mm", TRM0 + CSUB[qt] + j + 1)
                                inst = v.tensor_copy(
                                    pT_sb[qt % 2][:, j * 128:(j + 1) * 128],
                                    psB[(CSUB[qt] + j) % 4][:, 0:128])
                                p.inc(inst, "tr_ev")
                        p.wait(v, "s", S0 + qt + 1)
                        v.wait_ge(p.sem("v"), p.cnt["v"])
                        inst = v.reciprocal(rinv[qt % 2][:, :], rowsum[qt % 2][:, :])
                        p.inc(inst, "v")
                        p.wait(v, "o_mm", OM0 + qt + 1)
                        pw = P0[qt % 2] + 16 * (qt // 2)
                        if pw > 0:
                            p.wait(v, "dOa" if qt % 2 == 0 else "dOb", pw)
                        v.wait_ge(p.sem("v"), p.cnt["v"])
                        inst = v.tensor_scalar(stage1[qt % 2][:, :], ps_o[qt % 2],
                                               rinv[qt % 2][:, 0:1], None, op0=OP.mult)
                        p.inc(inst, "o_ev")

            @blk.scalar
            def _(sc):
                for it in range(33):
                    if it < 32:
                        qt = it
                        p.wait(sc, "v", V0 + (1 if qt == 0 else 2 * qt))
                        p.wait(sc, "sE", SE0 + CKB[qt] + NKB[qt])
                        if qt >= 2:
                            p.wait(sc, "tr_mm", TRM0 + CSUB[qt - 1])
                        inst = sc.activation(p_sb[qt % 2][:, 0:NKB[qt] * 512],
                                             s_sb[qt % 2][:, 0:NKB[qt] * 512], AF.Exp,
                                             bias=negmax[qt % 2][:, 0:1],
                                             accum_out=rowsum[qt % 2][:, :])
                        p.inc(inst, "s")
                    if it >= 1:
                        qt = it - 1
                        for j in range(NSUB[qt]):
                            if j % 2 == 1:
                                p.wait(sc, "tr_mm", TRM0 + CSUB[qt] + j + 1)
                                inst = sc.copy(
                                    pT_sb[qt % 2][:, j * 128:(j + 1) * 128],
                                    psB[(CSUB[qt] + j) % 4][:, 0:128])
                                p.inc(inst, "trs")




def emit_dequant(nc, p, pool, d):
    gWp, gSc, uWp, uSc, dWp, dSc = d["gWp"], d["gSc"], d["uWp"], d["uSc"], d["dWp"], d["dSc"]
    wgL, wuL, wdL = d["wgL"], d["wuL"], d["wdL"]
    ident, psB = d["ident"], d["psB"]
    pool.reset()
    wp_sb = [pool.take(1024, I32) for _ in range(2)]
    sc_sb = [pool.take(64, F32) for _ in range(2)]
    nib = [pool.take(2048, I32) for _ in range(2)]
    m7 = [pool.take(2048, I32) for _ in range(2)]
    bits = [pool.take(2048, I32) for _ in range(2)]
    sgn = [pool.take(2048, I32) for _ in range(2)]
    alt = [pool.take(2048, I32) for _ in range(2)]
    msk = [pool.take(2048, I32) for _ in range(2)]
    outi = [pool.take(2048, I32) for _ in range(2)]
    deq = [pool.take(2048, BF16) for _ in range(2)]
    stg = [pool.take(128, BF16) for _ in range(2)]
    p.sem("dq0")
    p.sem("dq1")
    p.sem("dSq0")
    p.sem("dSq1")
    p.sem("dQa")
    p.sem("dQb")
    p.sem("trs")

    gWp_, gSc_, uWp_, uSc_, dWp_, dSc_ = gWp, gSc, uWp, uSc, dWp, dSc
    tiles = [(nm, rt) for nm in range(3) for rt in range(16)]
    mats = [(gWp_, gSc_, wgL, 0x3F000000, 252), (uWp_, uSc_, wuL, 0x3F000000, 252),
            (dWp_, dSc_, wdL, 0x3E800000, 250)]

    def chain_front(v, b, ALT, OFF, dS_val):
        c = lambda fn, *a, **k: p.chain(v, "dq0", fn, *a, **k)
        v.wait_ge(p.sem("dSq0" if b == 0 else "dSq1"), dS_val)
        c(v.tensor_scalar, nib[b][:, 0:1024], wp_sb[b][:, :], 15, None, op0=OP.bitwise_and)
        c(v.tensor_scalar, nib[b][:, 1024:2048], wp_sb[b][:, :], 4, 15,
          op0=OP.logical_shift_right, op1=OP.bitwise_and)
        c(v.tensor_scalar, m7[b][:, :], nib[b][:, :], 7, None, op0=OP.bitwise_and)
        c(v.tensor_scalar, bits[b][:, :], m7[b][:, :], OFF, None, op0=OP.add)
        c(v.tensor_scalar, bits[b][:, :], bits[b][:, :], 22, None, op0=OP.logical_shift_left)
        c(v.tensor_scalar, sgn[b][:, :], nib[b][:, :], 8, 28,
          op0=OP.bitwise_and, op1=OP.logical_shift_left)
        c(v.tensor_tensor, bits[b][:, :], bits[b][:, :], sgn[b][:, :], op=OP.bitwise_or)
        c(v.tensor_scalar, alt[b][:, :], nib[b][:, :], 1, None, op0=OP.bitwise_and)
        c(v.tensor_scalar, alt[b][:, :], alt[b][:, :], ALT, None, op0=OP.mult)
        c(v.tensor_tensor, alt[b][:, :], alt[b][:, :], sgn[b][:, :], op=OP.bitwise_or)
        c(v.tensor_scalar, msk[b][:, :], m7[b][:, :], 2, None, op0=OP.is_ge)

    def chain_tail(g, b, front_cnt, tr_need):
        c = lambda fn, *a, **k: p.chain(g, "dq1", fn, *a, **k)
        g.wait_ge(p.sem("dq0"), front_cnt)
        if tr_need > 0:
            g.wait_ge(p.sem("tr_mm"), tr_need)
        c(g.tensor_tensor, outi[b][:, :], bits[b][:, :], alt[b][:, :], op=OP.subtract)
        c(g.tensor_tensor, outi[b][:, :], outi[b][:, :], msk[b][:, :], op=OP.mult)
        c(g.tensor_tensor, outi[b][:, :], outi[b][:, :], alt[b][:, :], op=OP.add)
        of = outi[b][:, :].bitcast(F32)
        for par in range(2):
            src = of[:, par * 1024:(par + 1) * 1024].rearrange("p (g j) -> p g j", j=16)
            dst = deq[b][:, :].rearrange("p (g j) -> p g j", j=32)[:, :, par::2]
            scb = sc_sb[b][:, :].unsqueeze(-1).broadcast_to([128, 64, 16])
            c(g.tensor_tensor, dst, src, scb, op=OP.mult)

    NFR = 11
    NTL = 5
    ld_dS = []
    ld_dS = []
    with nc.Block() as blk:
        CH0v = p.cnt["dq0"]
        CH1 = p.cnt["dq1"]
        TRM0 = p.cnt["tr_mm"]
        TRS0 = p.cnt["trs"]
        P0 = {0: p.cnt["dQa"], 1: p.cnt["dQb"]}

        @blk.sync
        def _(sy):
            for ti, (nm, rt) in enumerate(tiles):
                wp_d, sc_d = mats[nm][0], mats[nm][1]
                b = ti % 2
                if ti >= 2:
                    p.wait(sy, "dq1", CH1 + NTL * (ti - 1))
                smq = "dSq0" if b == 0 else "dSq1"
                inst = sy.dma_start(wp_sb[b][:, :], wp_d[rt * 128:(rt + 1) * 128, :])
                p.inc(inst, smq, 16)
                inst = sy.dma_start(sc_sb[b][:, :], sc_d[rt * 128:(rt + 1) * 128, :])
                p.inc(inst, smq, 16)
                ld_dS.append(p.cnt[smq])

        @blk.vector
        def _(v):
            for ti, (nm, rt) in enumerate(tiles):
                ALT, OFF = mats[nm][3], mats[nm][4]
                if ti >= 2:
                    p.wait(v, "dq1", CH1 + NTL * (ti - 1))
                chain_front(v, ti % 2, ALT, OFF, ld_dS[ti])

        @blk.gpsimd
        def _(g):
            for ti, (nm, rt) in enumerate(tiles):
                tr_need = TRM0 + 16 * (ti - 1) if ti >= 2 else 0
                chain_tail(g, ti % 2, CH0v + NFR * (ti + 1), tr_need)

        @blk.tensor
        def _(t):
            for ti, (nm, rt) in enumerate(tiles):
                b = ti % 2
                t.wait_ge(p.sem("dq1"), CH1 + NTL * (ti + 1))
                for cc_ in range(16):
                    gs = 16 * ti + cc_
                    if gs >= 4:
                        p.wait(t, "trs", TRS0 + gs - 3)
                    inst = t.transpose(psB[gs % 4][:, 0:128],
                                       deq[b][:, cc_ * 128:(cc_ + 1) * 128], ident[:, :])
                    p.inc(inst, "tr_mm")

        @blk.scalar
        def _(sc):
            for ti, (nm, rt) in enumerate(tiles):
                out_d = mats[nm][2]
                for cc_ in range(16):
                    gs = 16 * ti + cc_
                    p.wait(sc, "tr_mm", TRM0 + gs + 1)
                    pw = P0[gs % 2] + 16 * (gs // 2)
                    if pw > 0:
                        p.wait(sc, "dQa" if gs % 2 == 0 else "dQb", pw)
                    inst = sc.copy(stg[gs % 2][:, :], psB[gs % 4][:, 0:128])
                    p.inc(inst, "trs")
                    p.wait(sc, "trs", TRS0 + gs + 1)
                    inst = sc.dma_start(
                        out_d[cc_ * 128:(cc_ + 1) * 128, rt * 128:(rt + 1) * 128],
                        stg[gs % 2][:, :])
                    p.inc(inst, "dQa" if gs % 2 == 0 else "dQb", 16)



class Pool:

    def __init__(self, ap, words):
        self.ap = ap
        self.words = words
        self.off = 0

    def take(self, nelem, dt):
        esz = {F32: 4, I32: 4, BF16: 2}[dt]
        w = (nelem * esz + 31) // 32 * 8
        assert self.off + w <= self.words, f"pool overflow: {self.off}+{w} > {self.words}"
        view = self.ap[:, self.off:self.off + w]
        self.off += w
        if dt is not F32:
            view = view.bitcast(dt)
        return view[:, 0:nelem]

    def mark(self):
        return self.off

    def reset(self, off=0):
        self.off = off


def build_fused(dev_out=(), phases=None):
    MLP = 16384
    nc = bass.Bass("TRN2", target_bir_lowering=False)
    ein = lambda n, sh, dt: nc.dram_tensor(n, sh, dt, kind="ExternalInput").ap()
    xs = ein("xs", [TOK, D], F32)
    wpT = ein("wpT", [D, 3 * D], F32)
    woT = ein("woT", [D, D], F32)
    wrTe = ein("wrTe", [D, 8 * D], F32)
    brh = ein("brh", [128, 128], F32)
    weT = ein("weT", [D, 8 * D], F32)
    ebb = ein("ebb", [8, D], F32)
    wqkvT = ein("wqkvT", [D, 3072], F32)
    wprojT = ein("wprojT", [D, D], F32)
    gWp = ein("gWp", [2048, 1024], I32)
    gSc = ein("gSc", [2048, 64], F32)
    uWp = ein("uWp", [2048, 1024], I32)
    uSc = ein("uSc", [2048, 64], F32)
    dWp = ein("dWp", [2048, 1024], I32)
    dSc = ein("dSc", [2048, 64], F32)
    x4 = nc.dram_tensor("x4", [TOK, D], F32, kind="ExternalOutput").ap()

    def internal(n, sh, dt, shared=False):
        kind = "ExternalOutput" if n in dev_out else "Internal"
        kw = dict(addr_space="Shared") if (shared and n not in dev_out) else {}
        return nc.dram_tensor(n, sh, dt, kind=kind, **kw).ap()

    qkS = internal("qkS", [8 * 2048, 512], BF16)
    qkR = internal("qkR", [8 * 2048, 512], BF16)
    vS = internal("vS", [8 * 4096, 128], BF16)
    vR = internal("vR", [8 * 4096, 128], BF16)
    aoS = internal("aoS", [8 * 4096, 128], BF16)
    aoR = internal("aoR", [8 * 4096, 128], BF16)
    wgL = internal("wgL", [D, 2048], BF16)
    wuL = internal("wuL", [D, 2048], BF16)
    wdL = internal("wdL", [2048, D], BF16)
    wgG = internal("wgG", [8 * D, 2048], BF16, shared=True)
    wuG = internal("wuG", [8 * D, 2048], BF16, shared=True)
    wdG = internal("wdG", [MLP, D], BF16, shared=True)
    x1D = internal("x1D", [TOK, D], F32)
    x2D = internal("x2D", [TOK, D], F32)
    x3D = internal("x3D", [TOK, D], F32)
    ln2TD = internal("ln2TD", [D, TOK], BF16)
    ln3TD = internal("ln3TD", [D, TOK], BF16)
    ln4TD = internal("ln4TD", [D, TOK], BF16)
    ao1D = internal("ao1D", [TOK, D], BF16)
    c_accD = internal("c_accD", [8, TOK], F32)

    if phases is None:
        phases = {"dequant", "A", "attn", "projC", "moe", "l1attn", "projF", "l1ffn",
                  "cc"}
    with ExitStack() as st:
        p = Prog(nc, st)
        sb = lambda name, shape, dt: st.enter_context(nc.sbuf_tensor(name, shape, dt))
        psa = lambda name, shape, dt: st.enter_context(nc.psum_tensor(name, shape, dt))

        ident = sb("ident", [128, 128], BF16)
        identF = sb("identF", [128, 128], F32)
        iota_i = sb("iota_i", [128, 128], I32)
        iota_f = sb("iota_f", [128, 128], F32)
        rowid = sb("rowid", [128, 1], F32)
        bdmask = sb("bdmask", [128, 128], F32)
        ones_bf = sb("ones_bf", [128, 1], BF16)
        brh_sb = sb("brh_sb", [128, 128], F32)
        eb_sb = sb("eb_sb", [8, D], BF16)
        cT_sb = sb("cT_sb", [8, TOK], BF16)
        c_col = sb("c_col", [128, 128], F32)
        sums = sb("sums", [128, 2], F32)
        mean = sb("mean", [128, 1], F32)
        rstd = sb("rstd", [128, 1], F32)
        cstage = [sb(f"cstage{i}", [1, 512], F32) for i in range(2)]
        PW = 49152
        pool = Pool(sb("pool", [128, PW], F32), PW)

        psA = [psa(f"psA{i}", [128, 512], F32) for i in range(2)]
        psB = [psa(f"psB{i}", [128, 1024], BF16) for i in range(4)]
        psO = psa("psO", [128, 512], F32)
        psC = psa("psC", [128, 512], F32)

        for s_ in ["v", "s", "t", "g", "dS", "dG", "dO", "dA", "dOa", "dOb", "dOx",
                   "dAa", "dAb", "tr_mm", "tr_ev",
                   "mm", "ev", "cm", "cv", "sg", "cc", "o_mm", "o_ev", "sS", "sE"]:
            p.sem(s_)

        with nc.Block() as blk:
            @blk.sync
            def _(sy):
                inst = sy.dma_start(brh_sb[:, :], brh)
                p.inc(inst, "dS", 16)
                p.wait(sy, "dS")

            @blk.gpsimd
            def _(g):
                inst = g.dma_start(eb_sb[:, :], ebb)
                p.inc(inst, "dG", 16)
                p.wait(g, "dG")
        emit_ident(p, ident, iota_i, iota_f, rowid)
        with nc.Block() as blk:
            @blk.vector
            def _(v):
                p.chain(v, "v", v.tensor_scalar, identF[:, :], iota_f[:, :], rowid[:, 0:1],
                        None, op0=OP.is_equal)
                p.chain(v, "v", v.memset, ones_bf[:, :], 1.0)
        with nc.Block() as blk:
            @blk.vector
            def _(v):
                p.chain(v, "v", v.tensor_scalar, bdmask[:, :].bitcast(I32), iota_i[:, :],
                        2, None, op0=OP.logical_shift_right)
                p.chain(v, "v", v.tensor_copy, sums[:, 0:1].bitcast(I32),
                        iota_i[:, 0:1])
        with nc.Block() as blk:
            @blk.gpsimd
            def _(g):
                p.chain(g, "g", g.iota, sums[:, 0:1].bitcast(I32), [[1, 1]], channel_multiplier=1)
        with nc.Block() as blk:
            @blk.vector
            def _(v):
                p.chain(v, "v", v.tensor_scalar, sums[:, 0:1].bitcast(I32),
                        sums[:, 0:1].bitcast(I32), 2, None, op0=OP.logical_shift_right)
        with nc.Block() as blk:
            @blk.vector
            def _(v):
                p.chain(v, "v", v.tensor_copy, mean[:, 0:1], sums[:, 0:1].bitcast(I32))
                p.chain(v, "v", v.tensor_copy, rstd[:, :].bitcast(I32)[:, 0:1],
                        sums[:, 0:1].bitcast(I32))
        with nc.Block() as blk:
            @blk.vector
            def _(v):
                p.chain(v, "v", v.tensor_copy, identF[:, :], bdmask[:, :].bitcast(I32))
        with nc.Block() as blk:
            @blk.vector
            def _(v):
                p.chain(v, "v", v.tensor_scalar, bdmask[:, :], identF[:, :], mean[:, 0:1],
                        None, op0=OP.is_equal)
        with nc.Block() as blk:
            @blk.vector
            def _(v):
                p.chain(v, "v", v.tensor_scalar, bdmask[:, :], bdmask[:, :], -1.0, 1e9,
                        op0=OP.add, op1=OP.mult)
        with nc.Block() as blk:
            @blk.vector
            def _(v):
                p.chain(v, "v", v.tensor_scalar, identF[:, :], iota_f[:, :], rowid[:, 0:1],
                        None, op0=OP.is_equal)

        emit_dequant(nc, p, pool, dict(
            gWp=gWp, gSc=gSc, uWp=uWp, uSc=uSc, dWp=dWp, dSc=dSc,
            wgL=wgL, wuL=wuL, wdL=wdL, ident=ident, psB=psB))
        with nc.Block() as blk:
            @blk.gpsimd
            def _(g):
                p.wait_all(g, ["dQa", "dQb"])
                for src, dst in [(wgL, wgG), (wuL, wuG), (wdL, wdG)]:
                    inst = g.collective_compute(
                        "AllGather", OP.bypass,
                        replica_groups=[list(range(NCORES))],
                        ins=[src.opt()], outs=[dst.opt()])
                    p.inc(inst, "cc")

        pool.reset()
        x_t = pool.take(D, F32)
        scratch = pool.take(D, F32)
        ln_bf = pool.take(D, BF16)
        ln1T = pool.take(KC * TOK, BF16)
        wslab = pool.take(KC * 512, BF16)
        stage5 = [pool.take(512, BF16) for _ in range(2)]

        for tt in range(TOK // 128):
            with nc.Block() as blk:
                @blk.sync
                def _(sy):
                    inst = sy.dma_start(x_t[:, :], xs[tt * 128:(tt + 1) * 128, :])
                    p.inc(inst, "dS", 16)
                    p.wait(sy, "dS")
            emit_ln(p, None, x_t, ln_bf, sums, mean, rstd, scratch)
            emit_transpose_to_slab2(p, ident, ln_bf, ln1T, tt * 128, TOK, psB)

        def mm_evac_store(jobs):
            with nc.Block() as blk:
                M0 = p.cnt["mm"]
                E0 = p.cnt["ev"]
                P0 = {0: p.cnt["dOa"], 1: p.cnt["dOb"]}

                @blk.tensor
                def _(t):
                    for j, (lhs_l, rhs_l, _o) in enumerate(jobs):
                        if E0 + j - 1 > 0:
                            p.wait(t, "ev", E0 + j - 1)
                        for c in range(len(lhs_l)):
                            inst = t.matmul(psA[j % 2][:, :], lhs_l[c], rhs_l[c],
                                            start=(c == 0), stop=(c == len(lhs_l) - 1))
                        p.inc(inst, "mm")

                @blk.vector
                def _(v):
                    for j in range(len(jobs)):
                        p.wait(v, "mm", M0 + j + 1)
                        pw = P0[j % 2] + 16 * (j // 2)
                        if pw > 0:
                            p.wait(v, "dOa" if j % 2 == 0 else "dOb", pw)
                        inst = v.tensor_copy(stage5[j % 2][:, :], psA[j % 2][:, :])
                        p.inc(inst, "ev")

                @blk.sync
                def _(sy):
                    for j, (_l, _r, o_ap) in enumerate(jobs):
                        p.wait(sy, "ev", E0 + j + 1)
                        inst = sy.dma_start(o_ap, stage5[j % 2][:, :])
                        p.inc(inst, "dOa" if j % 2 == 0 else "dOb", 16)

        def load_wslab(src_ap, c0, col0, ncols=512):
            with nc.Block() as blk:
                @blk.gpsimd
                def _(g):
                    for c in range(KC):
                        inst = g.dma_start(wslab[:, c * 512:c * 512 + ncols],
                                           src_ap[c * 128:(c + 1) * 128, col0:col0 + ncols])
                        p.inc(inst, "dG", 16)
                    p.wait(g, "dG")

        for mb in range(8):
            load_wslab(wpT, 0, mb * 512)
            jobs = []
            for m4 in range(4):
                row0 = mb * 512 + m4 * 128
                is_k = row0 >= D
                h = (row0 - D) // 128 if is_k else row0 // 128
                for t4 in range(4):
                    b = t4
                    j = b * 2 + h // 8
                    r0 = j * 2048 + (h % 8) * 256 + (128 if is_k else 0)
                    lhs = [wslab[:, c * 512 + m4 * 128: c * 512 + (m4 + 1) * 128] for c in range(KC)]
                    rhs = [ln1T[:, c * TOK + t4 * 512: c * TOK + (t4 + 1) * 512] for c in range(KC)]
                    jobs.append((lhs, rhs, qkS[r0:r0 + 128, :]))
            mm_evac_store(jobs)
        for vb in range(4):
            load_wslab(wpT, 0, 2 * D + vb * 512)
            h0 = vb * 4
            jv = (h0 // 8)
            jobs = []
            for tt in range(TOK // 128):
                b = tt // 4
                j = b * 2 + jv
                lhs = [ln1T[:, c * TOK + tt * 128: c * TOK + (tt + 1) * 128] for c in range(KC)]
                rhs = [wslab[:, c * 512:(c + 1) * 512] for c in range(KC)]
                off = (j * 4096 + (h0 % 8) * 512 + (tt % 4) * 128) * 128
                dst = bass.AP(vS.tensor, off, [[128, 128], [512 * 128, 4], [1, 128]])
                jobs.append((lhs, rhs, dst))
            mm_evac_store(jobs)
        with nc.Block() as blk:
            @blk.gpsimd
            def _(g):
                p.wait_all(g, ["dOa", "dOb"])
                with allow_ici_all_to_all():
                    inst = g.collective_compute(
                        "AllToAll", OP.bypass,
                        replica_groups=[list(range(NCORES))],
                        ins=[qkS.opt()], outs=[qkR.opt()])
                    p.inc(inst, "cc")
                    inst = g.collective_compute(
                        "AllToAll", OP.bypass,
                        replica_groups=[list(range(NCORES))],
                        ins=[vS.opt()], outs=[vR.opt()])
                    p.inc(inst, "cc")

        emit_attn_phase(nc, p, pool, dict(
            qkR=qkR, vR=vR, aoS=aoS, ident=ident, rowid=rowid,
            psA=psA, psB=psB, psO=psO, psC=psC))

        with nc.Block() as blk:
            @blk.gpsimd
            def _(g):
                p.wait_all(g, ["dOa", "dOb"])
                with allow_ici_all_to_all():
                    inst = g.collective_compute(
                        "AllToAll", OP.bypass,
                        replica_groups=[list(range(NCORES))],
                        ins=[aoS.opt()], outs=[aoR.opt()])
                    p.inc(inst, "cc")

        def emit_proj(xs_in, ao_rd, wT_in, x1_out, lnT_out, wait_cc=None):
            pool.reset()
            w_sb = pool.take(KC * D, BF16)
            ao_t = pool.take(D, BF16)
            aoT_t = pool.take(KC * 128, BF16)
            x_t = pool.take(D, F32)
            x1_t = pool.take(D, F32)
            scratch = pool.take(D, F32)
            ln_bf = pool.take(D, BF16)
            stage2 = [pool.take(128, BF16) for _ in range(2)]
            with nc.Block() as blk:
                @blk.gpsimd
                def _(g):
                    if wait_cc is not None:
                        p.wait(g, "cc", wait_cc)
                    p.wait_all(g, ["dOa", "dOb", "dOx", "dAa", "dAb"])
                    for c in range(KC):
                        inst = g.dma_start(w_sb[:, c * D:(c + 1) * D], wT_in[c * 128:(c + 1) * 128, :])
                        p.inc(inst, "dG", 16)
                    p.wait(g, "dG")
            for tt in range(TOK // 128):
                r0 = tt * 128
                with nc.Block() as blk:
                    @blk.sync
                    def _(sy):
                        inst = sy.dma_start(x_t[:, :], xs_in[r0:r0 + 128, :])
                        p.inc(inst, "dS", 16)

                    @blk.gpsimd
                    def _(g):
                        ao_rd(g, tt, ao_t)
                        p.wait(g, "dG")
                with nc.Block() as blk:
                    @blk.sync
                    def _(sy):
                        p.wait(sy, "dS")
                emit_transpose_to_slab2(p, ident, ao_t, aoT_t, 0, 128, psB)
                with nc.Block() as blk:
                    M0 = p.cnt["mm"]
                    E0 = p.cnt["ev"]

                    @blk.tensor
                    def _(t):
                        for j in range(4):
                            if E0 + j - 1 > 0:
                                p.wait(t, "ev", E0 + j - 1)
                            for c in range(KC):
                                inst = t.matmul(psA[j % 2][:, :],
                                                aoT_t[:, c * 128:(c + 1) * 128],
                                                w_sb[:, c * D + j * 512: c * D + (j + 1) * 512],
                                                start=(c == 0), stop=(c == KC - 1))
                            p.inc(inst, "mm")

                    @blk.vector
                    def _(v):
                        for j in range(4):
                            p.wait(v, "mm", M0 + j + 1)
                            inst = v.tensor_tensor(x1_t[:, j * 512:(j + 1) * 512],
                                                   psA[j % 2][:, :],
                                                   x_t[:, j * 512:(j + 1) * 512], op=OP.add)
                            p.inc(inst, "ev")
                with nc.Block() as blk:
                    @blk.sync
                    def _(sy):
                        inst = sy.dma_start(x1_out[r0:r0 + 128, :], x1_t[:, :])
                        p.inc(inst, "dO", 16)
                emit_ln(p, None, x1_t, ln_bf, sums, mean, rstd, scratch)
                emit_lnT_store(p, ident, ln_bf, lnT_out, r0, [psB[0][:, 0:128], psB[1][:, 0:128]],
                               stage2)

        def ao_from_aoR(g, tt, ao_t):
            b = tt // 4
            for hb in range(2):
                off = ((b * 2 + hb) * 4096 + (tt % 4) * 128) * 128
                inst = g.dma_start(
                    ao_t[:, hb * 1024:(hb + 1) * 1024].rearrange("p (h d) -> p h d", h=8),
                    bass.AP(aoR.tensor, off, [[128, 128], [512 * 128, 8], [1, 128]]))
                p.inc(inst, "dG", 16)

        emit_proj(xs, ao_from_aoR, woT, x1D, ln2TD, wait_cc=6)

        emit_moe(nc, p, pool, dict(
            ln2T=ln2TD, x1s=x1D, wrTe=wrTe, weT=weT, x2=x2D, lnT=ln3TD, c_accD=c_accD,
            brh_sb=brh_sb, eb_sb=eb_sb, ones_bf=ones_bf, cT_sb=cT_sb, c_col=c_col,
            identF=identF, ident=ident, sums=sums, mean=mean, rstd=rstd, cstage=cstage,
            psA=psA, psB=psB, psC=psC))

        emit_l1attn(nc, p, pool, dict(
            ln3T=ln3TD, wqkvT=wqkvT, ao1=ao1D, bdmask=bdmask, ident=ident,
            psA=psA, psB=psB, psO=psO, psC=psC))

        def ao_from_ao1(g, tt, ao_t):
            inst = g.dma_start(ao_t[:, :], ao1D[tt * 128:(tt + 1) * 128, :])
            p.inc(inst, "dG", 16)

        emit_proj(x2D, ao_from_ao1, wprojT, x3D, ln4TD)

        emit_l1ffn(nc, p, pool, dict(
            ln4T=ln4TD, x3s=x3D, wgG=wgG, wuG=wuG, wdG=wdG, x4=x4, psA=psA, psB=psB))

        with nc.Block() as blk:
            @blk.sync
            def _(sy):
                p.wait_all(sy, ["dO", "dOa", "dOb", "dOx", "dAa", "dAb", "dS", "dG", "dA",
                                "dSa", "dSb", "dSg", "dSu", "dSw", "dGa", "dGb",
                                "dCa", "dCb", "dQa", "dQb", "dSq0", "dSq1", "dXa", "dXb",
                                "dRa", "dRb", "dTa", "dTb"])
    return nc


def shard_tokens(full):
    out = []
    for i in range(NCORES):
        sl = full[:, i * SC:(i + 1) * SC]
        out.append(np.ascontiguousarray(sl.reshape((TOK,) + full.shape[2:])))
    return out


def unshard_tokens(shards):
    rest = shards[0].shape[1:]
    full = np.empty((B, S) + rest, dtype=shards[0].dtype)
    for i, sh in enumerate(shards):
        full[:, i * SC:(i + 1) * SC] = sh.reshape((B, SC) + rest)
    return full


def fused_maps(L0_attn_x, L0_attn_w_project, L0_attn_w_out,
               L0_ffn_router_w, L0_ffn_router_b, L0_ffn_expert_w, L0_ffn_expert_b,
               L1_attn_w_qkv, L1_attn_w_proj,
               L1_ffn_gate_W_packed, L1_ffn_gate_scales,
               L1_ffn_up_W_packed, L1_ffn_up_scales,
               L1_ffn_down_W_packed, L1_ffn_down_scales):
    x = np.asarray(L0_attn_x, dtype=np.float32)
    xs = shard_tokens(x)
    wpT = np.ascontiguousarray(np.asarray(L0_attn_w_project, dtype=np.float32).T)
    woT = np.ascontiguousarray(np.asarray(L0_attn_w_out, dtype=np.float32).T)
    rw = np.asarray(L0_ffn_router_w, dtype=np.float32)
    rb = np.asarray(L0_ffn_router_b, dtype=np.float32)
    ew = np.asarray(L0_ffn_expert_w, dtype=np.float32)
    ebias = np.ascontiguousarray(np.asarray(L0_ffn_expert_b, dtype=np.float32))
    wrTe = np.ascontiguousarray(
        rw.reshape(16, 128, 8, D).transpose(0, 2, 1, 3).reshape(8 * D, D).T)
    brh = np.ascontiguousarray(rb.reshape(D, 8).T.reshape(8 * D).reshape(128, 128).T)
    weT = np.ascontiguousarray(ew.transpose(2, 0, 1).reshape(D, 8 * D))
    wqkvT = np.ascontiguousarray(np.asarray(L1_attn_w_qkv, dtype=np.float32).T)
    wprojT = np.ascontiguousarray(np.asarray(L1_attn_w_proj, dtype=np.float32).T)
    gWp = np.asarray(L1_ffn_gate_W_packed)
    gSc = np.asarray(L1_ffn_gate_scales, dtype=np.float32)
    uWp = np.asarray(L1_ffn_up_W_packed)
    uSc = np.asarray(L1_ffn_up_scales, dtype=np.float32)
    dWp = np.asarray(L1_ffn_down_W_packed)
    dSc = np.asarray(L1_ffn_down_scales, dtype=np.float32)
    maps = []
    for i in range(NCORES):
        maps.append(dict(
            xs=xs[i], wpT=wpT, woT=woT, wrTe=wrTe, brh=brh, weT=weT, ebb=ebias,
            wqkvT=wqkvT, wprojT=wprojT,
            gWp=np.ascontiguousarray(gWp[i * 2048:(i + 1) * 2048]),
            gSc=np.ascontiguousarray(gSc[i * 2048:(i + 1) * 2048]),
            uWp=np.ascontiguousarray(uWp[i * 2048:(i + 1) * 2048]),
            uSc=np.ascontiguousarray(uSc[i * 2048:(i + 1) * 2048]),
            dWp=np.ascontiguousarray(dWp[:, i * 1024:(i + 1) * 1024]),
            dSc=np.ascontiguousarray(dSc[:, i * 64:(i + 1) * 64])))
    return maps


def kernel(L0_attn_x, L0_attn_w_project, L0_attn_w_out,
           L0_ffn_router_w, L0_ffn_router_b, L0_ffn_expert_w, L0_ffn_expert_b,
           L1_attn_w_qkv, L1_attn_w_proj,
           L1_ffn_gate_W_packed, L1_ffn_gate_scales,
           L1_ffn_up_W_packed, L1_ffn_up_scales,
           L1_ffn_down_W_packed, L1_ffn_down_scales):
    run = lambda nc, maps: run_bass_kernel_spmd(nc, maps, core_ids=list(range(NCORES))).results

    maps = fused_maps(L0_attn_x, L0_attn_w_project, L0_attn_w_out,
                      L0_ffn_router_w, L0_ffn_router_b, L0_ffn_expert_w, L0_ffn_expert_b,
                      L1_attn_w_qkv, L1_attn_w_proj,
                      L1_ffn_gate_W_packed, L1_ffn_gate_scales,
                      L1_ffn_up_W_packed, L1_ffn_up_scales,
                      L1_ffn_down_W_packed, L1_ffn_down_scales)
    res = run(build_fused(), maps)
    x4 = unshard_tokens([np.asarray(r["x4"]) for r in res])
    return np.ascontiguousarray(x4, dtype=np.float32)




# version 10
# speedup vs baseline: 1.0432x; 1.0059x over previous
import numpy as np
import ml_dtypes
from contextlib import ExitStack

import concourse.bass as bass
import concourse.mybir as mybir
from concourse.bass_utils import run_bass_kernel_spmd
from concourse.aot_env import allow_ici_all_to_all

F32 = mybir.dt.float32
BF16 = mybir.dt.bfloat16
I32 = mybir.dt.int32
AF = mybir.ActivationFunctionType
OP = mybir.AluOpType
AX = mybir.AxisListType.X

BF16NP = ml_dtypes.bfloat16

NCORES = 8
D = 2048
B = 4
S = 4096
SC = S // NCORES
TOK = B * SC
KC = D // 128


class Prog:

    def __init__(self, nc, stack):
        self.nc = nc
        self.stack = stack
        self.sems = {}
        self.cnt = {}

    def sem(self, name):
        if name not in self.sems:
            self.sems[name] = self.stack.enter_context(self.nc.semaphore(name))
            self.cnt[name] = 0
        return self.sems[name]

    def inc(self, inst, name, by=1):
        inst.then_inc(self.sem(name), by)
        self.cnt[name] += by
        return self.cnt[name]

    def wait(self, eng, name, val=None):
        eng.wait_ge(self.sem(name), self.cnt[name] if val is None else val)

    def wait_all(self, eng, names):
        for n in names:
            if self.cnt.get(n):
                eng.wait_ge(self.sems[n], self.cnt[n])

    def chain(self, eng, name, fn, *a, **k):
        if self.cnt[name]:
            eng.wait_ge(self.sems[name], self.cnt[name])
        inst = fn(*a, **k)
        return self.inc(inst, name)


def emit_ln(p, blk, x_t, out_bf, sums, mean, rstd, scratch, n=D):
    nc = p.nc
    with nc.Block() as b1:
        V0 = p.cnt["v"]
        S0 = p.cnt["s"]

        @b1.vector
        def _(v):
            p.chain(v, "v", v.reduce_sum, sums[:, 0:1], x_t[:, :], axis=AX)
            p.chain(v, "v", v.tensor_scalar, mean[:, :], sums[:, 0:1], 1.0 / n, None, op0=OP.mult)
            p.chain(v, "v", v.tensor_tensor, rstd[:, :], mean[:, :], mean[:, :], op=OP.mult)
            p.wait(v, "s", S0 + 1)
            p.chain(v, "v", v.scalar_tensor_tensor, rstd[:, :], sums[:, 1:2], 1.0 / n,
                    rstd[:, :], op0=OP.mult, op1=OP.subtract)
            p.chain(v, "v", v.tensor_scalar, rstd[:, :], rstd[:, :], 1e-5, None, op0=OP.add)
            p.chain(v, "v", v.reciprocal, rstd[:, :], rstd[:, :])
            p.chain(v, "v", v.tensor_scalar, mean[:, :], mean[:, :], -1.0, None, op0=OP.mult)
            p.wait(v, "s", S0 + 2)
            p.chain(v, "v", v.tensor_scalar, out_bf[:, :], x_t[:, :], mean[:, 0:1],
                    rstd[:, 0:1], op0=OP.add, op1=OP.mult)

        @b1.scalar
        def _(s):
            p.chain(s, "s", s.activation, scratch[:, 0:n], x_t[:, :], AF.Square,
                    accum_out=sums[:, 1:2])
            p.wait(s, "v", V0 + 6)
            p.chain(s, "s", s.activation, rstd[:, :], rstd[:, :], AF.Sqrt)


def emit_ident(p, ident, iota_i, iota_f, rowid):
    nc = p.nc
    with nc.Block() as blk:
        @blk.gpsimd
        def _(g):
            p.chain(g, "g", g.iota, iota_i[:, :], [[1, 128]], channel_multiplier=0)
            p.chain(g, "g", g.iota, iota_f[:, 0:1].bitcast(I32), [[1, 1]], channel_multiplier=1)
    with nc.Block() as blk:
        @blk.vector
        def _(v):
            p.chain(v, "v", v.tensor_copy, rowid[:, :], iota_f[:, 0:1].bitcast(I32))
            p.chain(v, "v", v.tensor_copy, iota_f[:, :], iota_i[:, :])
    with nc.Block() as blk:
        @blk.vector
        def _(v):
            p.chain(v, "v", v.tensor_scalar, ident[:, :], iota_f[:, :], rowid[:, 0:1],
                    None, op0=OP.is_equal)






def emit_lnT_store(p, ident, ln_bf, lnT, r0, ps_tr, stage):
    nc = p.nc
    with nc.Block() as blk:
        M0 = p.cnt["tr_mm"]
        E0 = p.cnt["tr_ev"]
        P0 = {0: p.cnt["dOa"], 1: p.cnt["dOb"]}

        @blk.tensor
        def _(t):
            for c in range(KC):
                if E0 + c - 1 > 0:
                    p.wait(t, "tr_ev", E0 + c - 1)
                inst = t.transpose(ps_tr[c % 2][:, :], ln_bf[:, c * 128:(c + 1) * 128],
                                   ident[:, :])
                p.inc(inst, "tr_mm")

        @blk.vector
        def _(v):
            for c in range(KC):
                p.wait(v, "tr_mm", M0 + c + 1)
                pw = P0[c % 2] + 16 * (c // 2)
                if pw > 0:
                    p.wait(v, "dOa" if c % 2 == 0 else "dOb", pw)
                inst = v.tensor_copy(stage[c % 2][:, 0:128], ps_tr[c % 2][:, :])
                p.inc(inst, "tr_ev")

        @blk.sync
        def _(sy):
            for c in range(KC):
                p.wait(sy, "tr_ev", E0 + c + 1)
                inst = sy.dma_start(lnT[c * 128:(c + 1) * 128, r0:r0 + 128],
                                    stage[c % 2][:, 0:128])
                p.inc(inst, "dOa" if c % 2 == 0 else "dOb", 16)










def emit_transpose_to_slab2(p, ident, src_bf, slab, t0, T, psB, nchunks=KC):
    nc = p.nc
    with nc.Block() as blk:
        E0 = p.cnt["tr_ev"]
        M0 = p.cnt["tr_mm"]

        @blk.tensor
        def _(t):
            for c in range(nchunks):
                if E0 + c - 1 > 0:
                    p.wait(t, "tr_ev", E0 + c - 1)
                inst = t.transpose(psB[c % 2][:, 0:128], src_bf[:, c * 128:(c + 1) * 128],
                                   ident[:, :])
                p.inc(inst, "tr_mm")

        @blk.vector
        def _(v):
            for c in range(nchunks):
                p.wait(v, "tr_mm", M0 + c + 1)
                inst = v.tensor_copy(slab[:, c * T + t0: c * T + t0 + 128], psB[c % 2][:, 0:128])
                p.inc(inst, "tr_ev")


def emit_moe(nc, p, pool, d):
    ln2T, x1s, wrTe, weT = d["ln2T"], d["x1s"], d["wrTe"], d["weT"]
    x2, lnT, c_accD = d["x2"], d["lnT"], d["c_accD"]
    brh_sb, eb_sb, ones_bf, cT_sb, c_col = d["brh_sb"], d["eb_sb"], d["ones_bf"], d["cT_sb"], d["c_col"]
    identF, ident, sums, mean, rstd = d["identF"], d["ident"], d["sums"], d["mean"], d["rstd"]
    cstage, psA, psB, psC = d["cstage"], d["psA"], d["psB"], d["psC"]
    ps_c = psC[0:1, 0:512]
    ps_ct = psC[:, 0:1]
    p.sem("es")

    pool.reset()
    ln2T_sb = pool.take(KC * TOK, BF16)
    base = pool.mark()
    lg4 = [pool.take(8 * 512, BF16) for _ in range(2)]
    tmp = pool.take(11 * 512, BF16)
    bc = pool.take(8 * 512, BF16)
    aw = pool.take(8 * 512, BF16)
    bg = pool.take(8 * 512, BF16)
    wslab = [pool.take(KC * 1024, BF16) for _ in range(2)]
    ps_cb = [psB[i].bitcast(F32)[0:1, 0:512] for i in range(4)]
    p.sem("cs")
    p.sem("gg")

    p.sem("dGa")
    p.sem("dGb")
    p.sem("dCa")
    p.sem("dCb")

    p.sem("dRa")
    p.sem("dRb")
    p.sem("dTa")
    p.sem("dTb")
    SPL = {"g": range(0, 16), "s": range(0, 0), "c": range(0, 0)}
    SLQ = {"g": ("dGa", "dGb"), "s": ("dRa", "dRb"), "c": ("dTa", "dTb")}
    G0 = {k: {b: p.cnt[SLQ[k][b]] for b in range(2)} for k in SLQ}

    def load_router_slab(eng, kind, oc):
        buf = oc % 2
        sm = SLQ[kind][buf]
        for c in SPL[kind]:
            inst = eng.dma_start(wslab[buf][:, c * 1024:(c + 1) * 1024],
                                 wrTe[c * 128:(c + 1) * 128, oc * 1024:(oc + 1) * 1024])
            p.inc(inst, sm, 16)

    def rslab_cnt(kind, oc):
        return G0[kind][oc % 2] + 16 * len(SPL[kind]) * (oc // 2 + 1)

    with nc.Block() as blk:
        @blk.sync
        def _(sy):
            p.wait_all(sy, ["dOa", "dOb", "dOx"])
            for c in range(KC):
                inst = sy.dma_start(ln2T_sb[:, c * TOK:(c + 1) * TOK],
                                    ln2T[c * 128:(c + 1) * 128, :])
                p.inc(inst, "dS", 16)
            load_router_slab(sy, "s", 0)
            p.wait(sy, "dS")

        @blk.scalar
        def _(sc):
            load_router_slab(sc, "c", 0)

        @blk.vector
        def _(v):
            p.chain(v, "v", v.memset, tmp[:, 0:1024], 0.0)

        @blk.gpsimd
        def _(g):
            load_router_slab(g, "g", 0)

    with nc.Block() as blk:
        @blk.gpsimd
        def _(g):
            inst = g.dma_start(c_accD.rearrange("a b -> (a b)").rearrange("(p f) -> p f", p=32),
                               tmp[0:32, 0:1024].bitcast(F32))
            p.inc(inst, "dG", 16)
            p.wait(g, "dG")

    NOC = 16
    NGT = 4 * NOC
    with nc.Block() as blk:
        M0 = p.cnt["mm"]
        EV0 = p.cnt["ev"]
        ES0 = p.cnt["es"]
        SG0 = p.cnt["sg"]
        CM0 = p.cnt["cm"]
        CS0 = p.cnt["cs"]
        GCa = {0: p.cnt["dCa"], 1: p.cnt["dCb"]}
        GG0 = p.cnt["gg"]
        va = {}
        cva = {}
        vtt = {}

        def evac_v_cnt(gt, e):
            return EV0 + 4 * gt + e // 2 + 1

        def evac_s_cnt(gt, e):
            return ES0 + 4 * gt + (e + 1) // 2

        @blk.sync
        def _(sy):
            if SPL["s"]:
                for oc in range(1, NOC):
                    p.wait(sy, "mm", M0 + 32 * (oc - 1))
                    load_router_slab(sy, "s", oc)


        @blk.vector
        def _(v):
            for gt in range(NGT):
                oc = gt // 4
                lgc = lg4[gt % 2]
                LG = lgc[:, 0:4096]
                BC = bc[:, 0:4096]
                AW = aw[:, 0:4096]

                def L(e_):
                    return lgc[:, e_ * 512:(e_ + 1) * 512]

                def T(i_):
                    return tmp[:, i_ * 512:(i_ + 1) * 512]

                for e in range(0, 8, 2):
                    p.wait(v, "mm", M0 + 8 * gt + e + 1)
                    if e == 0 and gt >= 2:
                        p.wait(v, "cm", CM0 + 8 * (gt - 2) + 8)
                    if p.cnt["ev"]:
                        v.wait_ge(p.sem("ev"), p.cnt["ev"])
                    inst = v.tensor_scalar(
                        L(e), psA[(8 * gt + e) % 2][:, :],
                        brh_sb[:, e * 16 + oc: e * 16 + oc + 1],
                        None, op0=OP.add)
                    p.inc(inst, "ev")
                p.wait(v, "es", evac_s_cnt(gt, 7))
                for pr in range(4):
                    p.chain(v, "v", v.tensor_tensor, T(pr), L(2 * pr), L(2 * pr + 1), op=OP.max)
                    p.chain(v, "v", v.tensor_tensor, T(4 + pr), L(2 * pr), L(2 * pr + 1), op=OP.min)
                for q in range(2):
                    p.chain(v, "v", v.tensor_tensor, T(8), T(2 * q), T(2 * q + 1), op=OP.min)
                    p.chain(v, "v", v.tensor_tensor, T(2 * q), T(2 * q), T(2 * q + 1), op=OP.max)
                    p.chain(v, "v", v.tensor_tensor, T(2 * q + 1), T(4 + 2 * q), T(5 + 2 * q), op=OP.max)
                    p.chain(v, "v", v.tensor_tensor, T(2 * q + 1), T(2 * q + 1), T(8), op=OP.max)
                p.chain(v, "v", v.tensor_tensor, T(8), T(0), T(2), op=OP.min)
                p.chain(v, "v", v.tensor_tensor, T(9), T(0), T(2), op=OP.max)
                p.chain(v, "v", v.tensor_tensor, T(10), T(1), T(3), op=OP.max)
                p.chain(v, "v", v.tensor_tensor, T(10), T(10), T(8), op=OP.max)
                p.chain(v, "v", v.tensor_tensor, T(8), T(9), T(10), op=OP.add)
                vtt[gt] = p.cnt["v"]
                p.chain(v, "v", v.tensor_copy, bc[:, 0:512], T(8))
                p.chain(v, "v", v.tensor_copy, bc[:, 512:1024], bc[:, 0:512])
                p.chain(v, "v", v.tensor_copy, bc[:, 1024:2048], bc[:, 0:1024])
                p.chain(v, "v", v.tensor_copy, bc[:, 2048:4096], bc[:, 0:2048])
                p.chain(v, "v", v.scalar_tensor_tensor, AW, LG, 2.0, BC,
                        op0=OP.mult, op1=OP.subtract)
                va[gt] = p.cnt["v"]
                p.chain(v, "v", v.tensor_copy, bg[:, 0:512], T(10))
                p.chain(v, "v", v.tensor_copy, bg[:, 512:1024], bg[:, 0:512])
                p.chain(v, "v", v.tensor_copy, bg[:, 1024:2048], bg[:, 0:1024])
                p.chain(v, "v", v.tensor_copy, bg[:, 2048:4096], bg[:, 0:2048])
                p.chain(v, "v", v.tensor_tensor, bg[:, 0:4096], LG, bg[:, 0:4096],
                        op=OP.is_ge)
                p.wait(v, "sg", SG0 + gt + 1)
                p.chain(v, "v", v.tensor_tensor, LG, bg[:, 0:4096], AW, op=OP.mult)
                cva[gt] = p.cnt["v"]

        @blk.gpsimd
        def _(g):
            def accums(tb):
                for e in range(8):
                    p.wait(g, "cs", CS0 + 8 * tb + e + 1)
                    inst = g.dma_start(
                        c_accD[e:e + 1, (tb % 4) * 512:(tb % 4 + 1) * 512],
                        cstage[e % 2][:, :], accum_op=OP.add)
                    p.inc(inst, "dCa" if e % 2 == 0 else "dCb", 16)
            for gt in range(NGT):
                oc = gt // 4
                if gt % 4 == 0 and oc + 1 < NOC:
                    p.wait(g, "mm", M0 + 32 * oc)
                    load_router_slab(g, "g", oc + 1)
                if gt >= 1:
                    accums(gt - 1)
            accums(NGT - 1)

        @blk.scalar
        def _(sc):
            def cstage_evacs(tb):
                for e in range(8):
                    gdx = 8 * tb + e
                    p.wait(sc, "cm", CM0 + gdx + 1)
                    sm = "dCa" if e % 2 == 0 else "dCb"
                    tgt = GCa[e % 2] + (16 * ((gdx - 2) // 2 + 1) if gdx >= 2 else 0)
                    if tgt:
                        p.wait(sc, sm, tgt)
                    inst = sc.copy(cstage[e % 2][:, :], ps_cb[gdx % 4])
                    p.inc(inst, "cs")
            for gt in range(NGT):
                oc = gt // 4
                lgc = lg4[gt % 2]
                for e in range(1, 8, 2):
                    p.wait(sc, "mm", M0 + 8 * gt + e + 1)
                    if e == 1 and gt >= 2:
                        p.wait(sc, "cm", CM0 + 8 * (gt - 2) + 8)
                    inst = sc.add(
                        lgc[:, e * 512:(e + 1) * 512],
                        psA[(8 * gt + e) % 2][:, :],
                        brh_sb[:, e * 16 + oc: e * 16 + oc + 1])
                    p.inc(inst, "es")
                    if e == 1 and gt % 4 == 0 and oc + 1 < NOC and SPL["c"]:
                        load_router_slab(sc, "c", oc + 1)
                if gt >= 1:
                    cstage_evacs(gt - 1)
                p.wait(sc, "v", va[gt])
                inst = sc.activation(aw[:, 0:4096], aw[:, 0:4096], AF.Sigmoid)
                p.inc(inst, "sg")
            cstage_evacs(NGT - 1)

        @blk.tensor
        def _(t):
            def ones(tb):
                lgc = lg4[tb % 2]
                p.wait(t, "v", cva[tb])
                for e in range(8):
                    gdx = 8 * tb + e
                    csw = CS0 + gdx - 3 if gdx >= 4 else CS0
                    if csw > 0:
                        p.wait(t, "cs", csw)
                    inst = t.matmul(ps_cb[gdx % 4], ones_bf[:, :],
                                    lgc[:, e * 512:(e + 1) * 512],
                                    start=True, stop=True)
                    p.inc(inst, "cm")
            for gt in range(NGT):
                oc = gt // 4
                if gt % 4 == 0:
                    for kind in ("g", "s", "c"):
                        if SPL[kind]:
                            t.wait_ge(p.sem(SLQ[kind][oc % 2]), rslab_cnt(kind, oc))
                for e in range(8):
                    j8 = 8 * gt + e
                    if j8 >= 2:
                        pe = j8 - 2
                        pt, pee = pe // 8, pe % 8
                        if pee % 2 == 0:
                            p.wait(t, "ev", evac_v_cnt(pt, pee))
                        else:
                            p.wait(t, "es", evac_s_cnt(pt, pee))
                    for c in range(KC):
                        inst = t.matmul(
                            psA[j8 % 2][:, :],
                            wslab[oc % 2][:, c * 1024 + e * 128: c * 1024 + (e + 1) * 128],
                            ln2T_sb[:, c * TOK + (gt % 4) * 512: c * TOK + (gt % 4 + 1) * 512],
                            start=(c == 0), stop=(c == KC - 1))
                    p.inc(inst, "mm")
                if gt >= 1:
                    ones(gt - 1)
            ones(NGT - 1)

    pool.reset(base)
    arena = pool.take(KC * 512, F32)
    we_sb = [pool.take(KC * 512, BF16) for _ in range(2)]
    c_row = pool.take(TOK, F32)[0:1, :]
    aslot2 = [pool.take(512, F32) for _ in range(2)]
    sslot2 = [pool.take(512, F32) for _ in range(2)]

    with nc.Block() as blk:
        @blk.sync
        def _(sy):
            p.wait_all(sy, ["dCa", "dCb"])
    for e in range(8):
        with nc.Block() as blk:
            @blk.sync
            def _(sy):
                inst = sy.dma_start(c_row[:, :], c_accD[e:e + 1, :])
                p.inc(inst, "dS", 16)
                p.wait(sy, "dS")

            @blk.gpsimd
            def _(g):
                inst = g.dma_start(cT_sb[e:e + 1, :], c_accD[e:e + 1, :])
                p.inc(inst, "dG", 16)
                p.wait(g, "dG")
        with nc.Block() as blk:
            M0 = p.cnt["tr_mm"]
            E0 = p.cnt["tr_ev"]

            @blk.tensor
            def _(t):
                for tc in range(16):
                    if E0 + tc > 0:
                        p.wait(t, "tr_ev", E0 + tc)
                    inst = t.transpose(ps_ct, c_row[0:1, tc * 128:(tc + 1) * 128],
                                       identF[0:1, 0:1])
                    p.inc(inst, "tr_mm")

            @blk.vector
            def _(v):
                for tc in range(16):
                    p.wait(v, "tr_mm", M0 + tc + 1)
                    inst = v.tensor_copy(c_col[:, tc * 8 + e: tc * 8 + e + 1], ps_ct)
                    p.inc(inst, "tr_ev")

    E0Q = {k: {b: p.cnt[SLQ[k][b]] for b in range(2)} for k in SLQ}

    def load_we(eng, kind, k):
        oc4, e = k // 8, k % 8
        sm = SLQ[kind][k % 2]
        for c in SPL[kind]:
            inst = eng.dma_start(
                we_sb[k % 2][:, c * 512:(c + 1) * 512],
                weT[c * 128:(c + 1) * 128, e * D + oc4 * 512: e * D + (oc4 + 1) * 512])
            p.inc(inst, sm, 16)

    def weslab_cnt(kind, k):
        return E0Q[kind][k % 2] + 16 * len(SPL[kind]) * (k // 2 + 1)

    with nc.Block() as blk:
        @blk.gpsimd
        def _(g):
            load_we(g, "g", 0)

        @blk.sync
        def _(sy):
            load_we(sy, "s", 0)

        @blk.scalar
        def _(sc):
            load_we(sc, "c", 0)

    for oc4 in range(4):
        with nc.Block() as blk:
            M0 = p.cnt["mm"]
            E0 = p.cnt["ev"]

            @blk.tensor
            def _(t):
                for tt in range(16):
                    if E0 + tt - 1 > 0:
                        p.wait(t, "ev", E0 + tt - 1)
                    inst = t.matmul(psA[tt % 2][:, :], cT_sb[:, tt * 128:(tt + 1) * 128],
                                    eb_sb[:, oc4 * 512:(oc4 + 1) * 512],
                                    start=True, stop=True)
                    p.inc(inst, "mm")

            @blk.vector
            def _(v):
                for tt in range(16):
                    p.wait(v, "mm", M0 + tt + 1)
                    inst = v.tensor_copy(arena[:, tt * 512:(tt + 1) * 512], psA[tt % 2][:, :])
                    p.inc(inst, "ev")
        for e in range(8):
            with nc.Block() as blk:
                M0 = p.cnt["mm"]
                E0 = p.cnt["ev"]
                kslab = 8 * oc4 + e

                @blk.gpsimd
                def _(g):
                    if kslab + 1 < 32:
                        load_we(g, "g", kslab + 1)

                @blk.sync
                def _(sy):
                    if kslab + 1 < 32:
                        load_we(sy, "s", kslab + 1)

                @blk.scalar
                def _(sc):
                    if kslab + 1 < 32:
                        load_we(sc, "c", kslab + 1)

                @blk.tensor
                def _(t):
                    for kind in ("g", "s", "c"):
                        if SPL[kind]:
                            t.wait_ge(p.sem(SLQ[kind][kslab % 2]), weslab_cnt(kind, kslab))
                    for tt in range(16):
                        if E0 + tt - 1 > 0:
                            p.wait(t, "ev", E0 + tt - 1)
                        for c in range(KC):
                            inst = t.matmul(
                                psA[tt % 2][:, :],
                                ln2T_sb[:, c * TOK + tt * 128: c * TOK + (tt + 1) * 128],
                                we_sb[e % 2][:, c * 512:(c + 1) * 512],
                                start=(c == 0), stop=(c == KC - 1))
                        p.inc(inst, "mm")

                @blk.vector
                def _(v):
                    for tt in range(16):
                        p.wait(v, "mm", M0 + tt + 1)
                        inst = v.scalar_tensor_tensor(
                            arena[:, tt * 512:(tt + 1) * 512], psA[tt % 2][:, :],
                            c_col[:, tt * 8 + e: tt * 8 + e + 1],
                            arena[:, tt * 512:(tt + 1) * 512],
                            op0=OP.mult, op1=OP.add)
                        p.inc(inst, "ev")
        p.sem("dXa")
        p.sem("dXb")
        with nc.Block() as blk:
            DX0 = {0: p.cnt["dXa"], 1: p.cnt["dXb"]}
            EV0 = p.cnt["ev"]
            PA0 = {0: p.cnt["dAa"], 1: p.cnt["dAb"]}

            @blk.sync
            def _(sy):
                for tt in range(16):
                    if tt >= 2:
                        p.wait(sy, "ev", EV0 + tt - 1)
                    inst = sy.dma_start(aslot2[tt % 2][:, :],
                                        x1s[tt * 128:(tt + 1) * 128, oc4 * 512:(oc4 + 1) * 512])
                    p.inc(inst, "dXa" if tt % 2 == 0 else "dXb", 16)

            @blk.vector
            def _(v):
                for tt in range(16):
                    p.wait(v, "dXa" if tt % 2 == 0 else "dXb",
                           DX0[tt % 2] + 16 * (tt // 2 + 1))
                    pw = PA0[tt % 2] + 16 * (tt // 2)
                    if pw > 0:
                        p.wait(v, "dAa" if tt % 2 == 0 else "dAb", pw)
                    inst = v.tensor_tensor(sslot2[tt % 2][:, :],
                                           arena[:, tt * 512:(tt + 1) * 512],
                                           aslot2[tt % 2][:, :], op=OP.add)
                    p.inc(inst, "ev")

            @blk.scalar
            def _(s):
                for tt in range(16):
                    p.wait(s, "ev", EV0 + tt + 1)
                    inst = s.dma_start(x2[tt * 128:(tt + 1) * 128, oc4 * 512:(oc4 + 1) * 512],
                                       sslot2[tt % 2][:, :])
                    p.inc(inst, "dAa" if tt % 2 == 0 else "dAb", 16)

    pool.reset(base)
    x2_t2 = [pool.take(D, F32) for _ in range(2)]
    scratch = pool.take(D, F32)
    ln_bf = pool.take(D, BF16)
    stage3 = [pool.take(128, BF16) for _ in range(2)]
    p.sem("dLa")
    p.sem("dLb")
    DL0 = {0: p.cnt["dLa"], 1: p.cnt["dLb"]}
    with nc.Block() as blk:
        @blk.sync
        def _(sy):
            p.wait_all(sy, ["dAa", "dAb"])
            inst = sy.dma_start(x2_t2[0][:, :], x2[0:128, :])
            p.inc(inst, "dLa", 16)
    NTT = TOK // 128
    for tt in range(NTT):
        r0 = tt * 128
        x2_t = x2_t2[tt % 2]
        dl = ("dLa" if tt % 2 == 0 else "dLb", DL0[tt % 2] + 16 * (tt // 2 + 1))
        with nc.Block() as b1:
            V0 = p.cnt["v"]
            S0 = p.cnt["s"]

            @b1.sync
            def _(sy):
                if tt + 1 < NTT:
                    inst = sy.dma_start(x2_t2[(tt + 1) % 2][:, :],
                                        x2[r0 + 128:r0 + 256, :])
                    p.inc(inst, "dLa" if (tt + 1) % 2 == 0 else "dLb", 16)

            @b1.vector
            def _(v):
                v.wait_ge(p.sem(dl[0]), dl[1])
                p.chain(v, "v", v.reduce_sum, sums[:, 0:1], x2_t[:, :], axis=AX)
                p.chain(v, "v", v.tensor_scalar, mean[:, :], sums[:, 0:1], 1.0 / D, None, op0=OP.mult)
                p.chain(v, "v", v.tensor_tensor, rstd[:, :], mean[:, :], mean[:, :], op=OP.mult)
                p.wait(v, "s", S0 + 1)
                p.chain(v, "v", v.scalar_tensor_tensor, rstd[:, :], sums[:, 1:2], 1.0 / D,
                        rstd[:, :], op0=OP.mult, op1=OP.subtract)
                p.chain(v, "v", v.tensor_scalar, rstd[:, :], rstd[:, :], 1e-5, None, op0=OP.add)
                p.chain(v, "v", v.reciprocal, rstd[:, :], rstd[:, :])
                p.chain(v, "v", v.tensor_scalar, mean[:, :], mean[:, :], -1.0, None, op0=OP.mult)
                p.wait(v, "s", S0 + 2)
                p.chain(v, "v", v.tensor_scalar, ln_bf[:, :], x2_t[:, :], mean[:, 0:1],
                        rstd[:, 0:1], op0=OP.add, op1=OP.mult)

            @b1.scalar
            def _(sc):
                sc.wait_ge(p.sem(dl[0]), dl[1])
                p.chain(sc, "s", sc.activation, scratch[:, 0:D], x2_t[:, :], AF.Square,
                        accum_out=sums[:, 1:2])
                p.wait(sc, "v", V0 + 6)
                p.chain(sc, "s", sc.activation, rstd[:, :], rstd[:, :], AF.Sqrt)
        emit_lnT_store(p, ident, ln_bf, lnT, r0, [psB[0][:, 0:128], psB[1][:, 0:128]], stage3)


def emit_l1attn(nc, p, pool, d):
    ln3T, wqkvT, ao1 = d["ln3T"], d["wqkvT"], d["ao1"]
    bdmask, ident = d["bdmask"], d["ident"]
    psA, psB, psO, psC = d["psA"], d["psB"], d["psO"], d["psC"]
    ps_s = [psA[0][:, 0:128], psA[1][:, 0:128]]
    ps_tr = [psB[0][:, 0:128], psB[1][:, 0:128]]
    ps_o = [psO[:, 0:128], psC[:, 0:128]]

    pool.reset()
    ln3_sb = pool.take(KC * TOK, BF16)
    ltmp = pool.take(TOK, BF16)
    w_sb = pool.take(KC * 768, BF16)
    qkT_sb = pool.take(5 * TOK, BF16)
    v_sb = pool.take(16 * 128, BF16)
    s_t = [pool.take(128, F32) for _ in range(2)]
    p_t = [pool.take(128, BF16) for _ in range(2)]
    pn_t = [pool.take(128, BF16) for _ in range(2)]
    pT_t = [pool.take(128, BF16) for _ in range(2)]
    negmax = pool.take(1, F32)
    rowsum = pool.take(1, F32)
    rinv = pool.take(1, F32)
    stage4 = [pool.take(128, BF16) for _ in range(2)]

    ao1p = ao1.rearrange("(b t) d -> t b d", b=B)

    for c in range(KC):
        with nc.Block() as blk:
            @blk.sync
            def _(sy):
                if c == 0:
                    p.wait_all(sy, ["dOa", "dOb", "dOx"])
                inst = sy.dma_start(ltmp[:, :], ln3T[c * 128:(c + 1) * 128, :])
                p.inc(inst, "dS", 16)
                p.wait(sy, "dS")
        with nc.Block() as blk:
            @blk.vector
            def _(v):
                p.chain(v, "v", v.tensor_copy,
                        ln3_sb[:, c * TOK:(c + 1) * TOK].rearrange("p (t b) -> p t b", b=B),
                        ltmp[:, :].rearrange("p (b t) -> p t b", b=B))

    def rhs_jb(c, t4):
        return ln3_sb[:, c * TOK + t4 * 512: c * TOK + (t4 + 1) * 512]

    def lhs_jb(c, tt):
        return ln3_sb[:, c * TOK + tt * 128: c * TOK + (tt + 1) * 128]

    SCALE = 1.0 / np.sqrt(128.0)
    for g_ in range(4):
        with nc.Block() as blk:
            @blk.gpsimd
            def _(g):
                for c in range(KC):
                    inst = g.dma_start(w_sb[:, c * 768:(c + 1) * 768],
                                       wqkvT[c * 128:(c + 1) * 128, g_ * 768:(g_ + 1) * 768])
                    p.inc(inst, "dG", 16)
                p.wait(g, "dG")
        with nc.Block() as blk:
            M0 = p.cnt["mm"]
            E0 = p.cnt["ev"]

            @blk.tensor
            def _(t):
                for j, (ch, t4) in enumerate([(c_, t_) for c_ in range(5) for t_ in range(4)]):
                    if E0 + j - 1 > 0:
                        p.wait(t, "ev", E0 + j - 1)
                    for c in range(KC):
                        inst = t.matmul(
                            psA[j % 2][:, :],
                            w_sb[:, c * 768 + ch * 128: c * 768 + (ch + 1) * 128],
                            rhs_jb(c, t4),
                            start=(c == 0), stop=(c == KC - 1))
                    p.inc(inst, "mm")

            @blk.vector
            def _(v):
                for j, (ch, t4) in enumerate([(c_, t_) for c_ in range(5) for t_ in range(4)]):
                    p.wait(v, "mm", M0 + j + 1)
                    inst = v.tensor_copy(
                        qkT_sb[:, ch * TOK + t4 * 512: ch * TOK + (t4 + 1) * 512],
                        psA[j % 2][:, :])
                    p.inc(inst, "ev")
        with nc.Block() as blk:
            M0 = p.cnt["mm"]
            E0 = p.cnt["ev"]

            @blk.tensor
            def _(t):
                for tt in range(16):
                    if E0 + tt - 1 > 0:
                        p.wait(t, "ev", E0 + tt - 1)
                    for c in range(KC):
                        inst = t.matmul(
                            psA[tt % 2][:, 0:128],
                            lhs_jb(c, tt),
                            w_sb[:, c * 768 + 640: c * 768 + 768],
                            start=(c == 0), stop=(c == KC - 1))
                    p.inc(inst, "mm")

            @blk.vector
            def _(v):
                for tt in range(16):
                    p.wait(v, "mm", M0 + tt + 1)
                    inst = v.tensor_copy(v_sb[:, tt * 128:(tt + 1) * 128],
                                         psA[tt % 2][:, 0:128])
                    p.inc(inst, "ev")
        for tt in range(16):
            with nc.Block() as blk:
                V0 = p.cnt["v"]
                S0 = p.cnt["s"]
                T0 = p.cnt["t"]
                P0E = {0: p.cnt["dOa"], 1: p.cnt["dOb"]}

                @blk.tensor
                def _(t):
                    for hh in range(4):
                        if hh >= 2:
                            p.wait(t, "v", V0 + 6 * (hh - 2) + 1)
                        inst = t.matmul(ps_s[hh % 2],
                                        qkT_sb[:, hh * TOK + tt * 128: hh * TOK + (tt + 1) * 128],
                                        qkT_sb[:, 4 * TOK + tt * 128: 4 * TOK + (tt + 1) * 128],
                                        start=True, stop=True)
                        p.inc(inst, "t")
                        p.wait(t, "v", V0 + 6 * hh + 4)
                        if hh >= 2:
                            p.wait(t, "v", V0 + 6 * (hh - 2) + 5)
                        inst = t.transpose(ps_tr[hh % 2], pn_t[hh % 2][:, :], ident[:, :])
                        p.inc(inst, "t")
                        p.wait(t, "v", V0 + 6 * hh + 5)
                        if hh >= 2:
                            p.wait(t, "v", V0 + 6 * (hh - 2) + 6)
                        inst = t.matmul(ps_o[hh % 2], pT_t[hh % 2][:, :],
                                        v_sb[:, tt * 128:(tt + 1) * 128],
                                        start=True, stop=True)
                        p.inc(inst, "t")

                @blk.vector
                def _(v):
                    for hh in range(4):
                        p.wait(v, "t", T0 + 3 * hh + 1)
                        if p.cnt["v"]:
                            v.wait_ge(p.sem("v"), p.cnt["v"])
                        inst = v.scalar_tensor_tensor(s_t[hh % 2][:, :], ps_s[hh % 2],
                                                      SCALE, bdmask[:, :],
                                                      op0=OP.mult, op1=OP.add)
                        p.inc(inst, "v")
                        if hh >= 1:
                            p.wait(v, "s", S0 + hh)
                        v.wait_ge(p.sem("v"), p.cnt["v"])
                        inst = v.reduce_max(negmax[:, :], s_t[hh % 2][:, :], axis=AX,
                                            negate=True)
                        p.inc(inst, "v")
                        p.wait(v, "s", S0 + hh + 1)
                        v.wait_ge(p.sem("v"), p.cnt["v"])
                        inst = v.reciprocal(rinv[:, :], rowsum[:, :])
                        p.inc(inst, "v")
                        v.wait_ge(p.sem("v"), p.cnt["v"])
                        inst = v.tensor_scalar(pn_t[hh % 2][:, :], p_t[hh % 2][:, :],
                                               rinv[:, 0:1], None, op0=OP.mult)
                        p.inc(inst, "v")
                        p.wait(v, "t", T0 + 3 * hh + 2)
                        v.wait_ge(p.sem("v"), p.cnt["v"])
                        inst = v.tensor_copy(pT_t[hh % 2][:, :], ps_tr[hh % 2])
                        p.inc(inst, "v")
                        p.wait(v, "t", T0 + 3 * hh + 3)
                        pw = P0E[hh % 2] + 16 * (hh // 2)
                        if pw > 0:
                            p.wait(v, "dOa" if hh % 2 == 0 else "dOb", pw)
                        v.wait_ge(p.sem("v"), p.cnt["v"])
                        inst = v.tensor_copy(stage4[hh % 2][:, :], ps_o[hh % 2])
                        p.inc(inst, "v")

                @blk.scalar
                def _(s):
                    for hh in range(4):
                        p.wait(s, "v", V0 + 6 * hh + 2)
                        if hh >= 2:
                            p.wait(s, "v", V0 + 6 * (hh - 2) + 4)
                        inst = s.activation(p_t[hh % 2][:, :], s_t[hh % 2][:, :], AF.Exp,
                                            bias=negmax[:, 0:1], accum_out=rowsum[:, :])
                        p.inc(inst, "s")

                @blk.sync
                def _(sy):
                    for hh in range(4):
                        p.wait(sy, "v", V0 + 6 * hh + 6)
                        h = 4 * g_ + hh
                        inst = sy.dma_start(
                            ao1p[tt * 32:(tt + 1) * 32, :, h * 128:(h + 1) * 128],
                            stage4[hh % 2][:, :])
                        p.inc(inst, "dOa" if hh % 2 == 0 else "dOb", 16)


def emit_l1ffn(nc, p, pool, d):
    ln4T, x3s, wgG, wuG, wdG, x4 = (d["ln4T"], d["x3s"], d["wgG"], d["wuG"],
                                    d["wdG"], d["x4"])
    psA, psB = d["psA"], d["psB"]
    psH = [psA[0][:, :], psA[1][:, :],
           psB[2][:, :].bitcast(F32), psB[3][:, :].bitcast(F32)]
    TH = 1024
    pool.reset()
    ln4h = pool.take(KC * TH, BF16)
    y_sb = pool.take(8 * D, F32)
    gu_sb = [pool.take(KC * 512, BF16) for _ in range(2)]
    wd_sb = pool.take(4 * D, BF16)
    g_sb = pool.take(4 * TH, F32)
    u_sb = pool.take(4 * TH, F32)
    tmp = pool.take(4 * TH, F32)
    inter = u_sb[:, 0:2 * TH].bitcast(BF16)

    p.sem("es")
    bank_last = [None] * 4

    p.sem("dSg")
    p.sem("dSu")
    p.sem("dSw")

    def load_gu(sy, wsrc, nb, buf):
        sm = "dSg" if buf == 0 else "dSu"
        rb = (nb // 4) * D
        nl0 = (nb % 4) * 512
        for c in range(KC):
            inst = sy.dma_start(gu_sb[buf][:, c * 512:(c + 1) * 512],
                                wsrc[rb + c * 128:rb + (c + 1) * 128, nl0:nl0 + 512])
            p.inc(inst, sm, 16)
        return p.cnt[sm]

    def load_wd(sy, nb):
        n0 = nb * 512
        for nc4 in range(4):
            inst = sy.dma_start(wd_sb[:, nc4 * D:(nc4 + 1) * D],
                                wdG[n0 + nc4 * 128:n0 + (nc4 + 1) * 128, :])
            p.inc(inst, "dSw", 16)
        return p.cnt["dSw"]

    with nc.Block() as blk:
        @blk.gpsimd
        def _(g):
            p.wait(g, "cc", 3)
            p.wait_all(g, ["dOa", "dOb", "dOx"])

    gate_dS = [None]
    for th in range(2):
        t0 = th * TH
        with nc.Block() as blk:
            @blk.sync
            def _(sy):
                for c in range(KC):
                    inst = sy.dma_start(ln4h[:, c * TH:(c + 1) * TH],
                                        ln4T[c * 128:(c + 1) * 128, t0:t0 + TH])
                    p.inc(inst, "dS", 16)
                if th == 0:
                    gate_dS[0] = load_gu(sy, wgG, 0, 0)
                p.wait(sy, "dS")
        for nb in range(32):
            jobs8 = [(a, b_) for a in range(4) for b_ in range(2)]
            for i in range(4):
                bank_last[i] = None
            with nc.Block() as blk:
                M0 = p.cnt["mm"]
                EV0 = p.cnt["ev"]
                ES0 = p.cnt["es"]
                G0 = p.cnt["g"]
                up_dS = [None]

                @blk.sync
                def _(sy):
                    up_dS[0] = load_gu(sy, wuG, nb, 1)

                @blk.tensor
                def _(t):
                    t.wait_ge(p.sem("dSg"), gate_dS[0])
                    for j, (nc4, t4) in enumerate(jobs8):
                        if bank_last[j % 4] is not None:
                            p.wait(t, *bank_last[j % 4])
                        for c in range(KC):
                            inst = t.matmul(
                                psH[j % 4],
                                gu_sb[0][:, c * 512 + nc4 * 128: c * 512 + (nc4 + 1) * 128],
                                ln4h[:, c * TH + t4 * 512: c * TH + (t4 + 1) * 512],
                                start=(c == 0), stop=(c == KC - 1))
                        p.inc(inst, "mm")
                        bank_last[j % 4] = ("ev" if j % 2 == 0 else "es",
                                            (EV0 if j % 2 == 0 else ES0) + j // 2 + 1)

                @blk.vector
                def _(v):
                    for j, (nc4, t4) in enumerate(jobs8):
                        if j % 2 == 0:
                            p.wait(v, "mm", M0 + j + 1)
                            inst = v.tensor_copy(
                                g_sb[:, nc4 * TH + t4 * 512: nc4 * TH + (t4 + 1) * 512],
                                psH[j % 4])
                            p.inc(inst, "ev")
                    p.wait(v, "g", G0 + 2)
                    p.chain(v, "v", v.tensor_tensor, tmp[:, :], tmp[:, :], g_sb[:, :], op=OP.mult)

                @blk.scalar
                def _(sc):
                    for j, (nc4, t4) in enumerate(jobs8):
                        if j % 2 == 1:
                            p.wait(sc, "mm", M0 + j + 1)
                            inst = sc.copy(
                                g_sb[:, nc4 * TH + t4 * 512: nc4 * TH + (t4 + 1) * 512],
                                psH[j % 4])
                            p.inc(inst, "es")

                @blk.gpsimd
                def _(g):
                    p.wait(g, "ev", EV0 + 4)
                    p.wait(g, "es", ES0 + 4)
                    p.chain(g, "g", g.tensor_tensor, tmp[:, :], g_sb[:, :], g_sb[:, :], op=OP.mult)
                    p.chain(g, "g", g.tensor_scalar, tmp[:, :], tmp[:, :], 0.044715, 1.0,
                            op0=OP.mult, op1=OP.add)
            for i in range(4):
                bank_last[i] = None
            with nc.Block() as blk:
                M0 = p.cnt["mm"]
                EV0 = p.cnt["ev"]
                ES0 = p.cnt["es"]
                V0 = p.cnt["v"]
                S0 = p.cnt["s"]
                upl = up_dS[0]
                wd_dS = [None]
                jobs32 = [(a, b_) for a in range(8) for b_ in range(4)]
                sched = {}
                nev, nes = 0, 0
                for j in range(8):
                    if j % 2 == 0:
                        nev += 1
                        sched[j] = ("ev", EV0 + nev)
                    else:
                        nes += 1
                        sched[j] = ("es", ES0 + nes)
                for j32 in range(32):
                    nev += 1
                    sched[8 + j32] = ("ev", EV0 + nev)

                @blk.sync
                def _(sy):
                    wd_dS[0] = load_wd(sy, nb)
                    if nb + 1 < 32:
                        gate_dS[0] = load_gu(sy, wgG, nb + 1, 0)
                    elif th == 0:
                        gate_dS[0] = load_gu(sy, wgG, 0, 0)

                @blk.scalar
                def _(sc):
                    inst = sc.activation(tmp[:, :], tmp[:, :], AF.Tanh,
                                         scale=0.7978845608028654)
                    p.inc(inst, "s")
                    p.wait(sc, "s", S0 + 1)
                    inst = sc.add(tmp[:, :], tmp[:, :], 1.0)
                    p.inc(inst, "s")
                    for j, (nc4, t4) in enumerate(jobs8):
                        if j % 2 == 1:
                            p.wait(sc, "mm", M0 + j + 1)
                            inst = sc.copy(
                                u_sb[:, nc4 * TH + t4 * 512: nc4 * TH + (t4 + 1) * 512],
                                psH[j % 4])
                            p.inc(inst, "es")

                @blk.tensor
                def _(t):
                    t.wait_ge(p.sem("dSu"), upl)
                    for j, (nc4, t4) in enumerate(jobs8):
                        if bank_last[j % 4] is not None:
                            p.wait(t, *bank_last[j % 4])
                        for c in range(KC):
                            inst = t.matmul(
                                psH[j % 4],
                                gu_sb[1][:, c * 512 + nc4 * 128: c * 512 + (nc4 + 1) * 128],
                                ln4h[:, c * TH + t4 * 512: c * TH + (t4 + 1) * 512],
                                start=(c == 0), stop=(c == KC - 1))
                        p.inc(inst, "mm")
                        bank_last[j % 4] = sched[j]
                    p.wait(t, "v", V0 + 2)
                    t.wait_ge(p.sem("dSw"), wd_dS[0])
                    for j32, (tt, hc) in enumerate(jobs32):
                        J = 8 + j32
                        if bank_last[J % 4] is not None:
                            p.wait(t, *bank_last[J % 4])
                        for nc4 in range(4):
                            inst = t.matmul(
                                psH[J % 4],
                                inter[:, nc4 * TH + tt * 128: nc4 * TH + (tt + 1) * 128],
                                wd_sb[:, nc4 * D + hc * 512: nc4 * D + (hc + 1) * 512],
                                start=(nc4 == 0), stop=(nc4 == 3))
                        p.inc(inst, "mm")
                        bank_last[J % 4] = sched[J]

                @blk.vector
                def _(v):
                    for j, (nc4, t4) in enumerate(jobs8):
                        if j % 2 == 0:
                            p.wait(v, "mm", M0 + j + 1)
                            inst = v.tensor_copy(
                                u_sb[:, nc4 * TH + t4 * 512: nc4 * TH + (t4 + 1) * 512],
                                psH[j % 4])
                            p.inc(inst, "ev")
                    p.wait(v, "es", ES0 + 4)
                    p.chain(v, "v", v.tensor_tensor, g_sb[:, :], g_sb[:, :], u_sb[:, :], op=OP.mult)
                    p.wait(v, "s", S0 + 2)
                    p.chain(v, "v", v.tensor_tensor, inter[:, :], g_sb[:, :], tmp[:, :], op=OP.mult)
                    for j32, (tt, hc) in enumerate(jobs32):
                        J = 8 + j32
                        ysl = y_sb[:, tt * D + hc * 512: tt * D + (hc + 1) * 512]
                        p.wait(v, "mm", M0 + J + 1)
                        if nb == 0:
                            inst = v.tensor_copy(ysl, psH[J % 4])
                        else:
                            inst = v.tensor_tensor(ysl, ysl, psH[J % 4], op=OP.add)
                        p.inc(inst, "ev")
        for tt in range(8):
            r0 = t0 + tt * 128
            with nc.Block() as blk:
                @blk.sync
                def _(sy):
                    inst = sy.dma_start(tmp[:, 0:2048], x3s[r0:r0 + 128, :])
                    p.inc(inst, "dS", 16)
                    p.wait(sy, "dS")
            with nc.Block() as blk:
                @blk.vector
                def _(v):
                    p.chain(v, "v", v.tensor_tensor, g_sb[:, 0:2048],
                            y_sb[:, tt * D:(tt + 1) * D], tmp[:, 0:2048], op=OP.add)
            with nc.Block() as blk:
                @blk.sync
                def _(sy):
                    inst = sy.dma_start(x4[r0:r0 + 128, :], g_sb[:, 0:2048])
                    p.inc(inst, "dO", 16)
                    p.wait(sy, "dO")


def emit_attn_phase(nc, p, pool, d, npairs=8):
    qkR, vR, aoS = d["qkR"], d["vR"], d["aoS"]
    ident, rowid = d["ident"], d["rowid"]
    psA, psB, psO, psC = d["psA"], d["psB"], d["psO"], d["psC"]
    pool.reset()
    MW = 4800
    band = pool.take(MW, F32)
    bandw = pool.take(MW, F32)
    k_sb = [pool.take(S, BF16) for _ in range(2)]
    v_sb = [pool.take(S, BF16) for _ in range(2)]
    q_sb = [pool.take(S, BF16) for _ in range(2)]
    s_sb = [pool.take(S, F32) for _ in range(2)]
    p_sb = [pool.take(S, BF16) for _ in range(2)]
    pT_sb = [pool.take(S, BF16) for _ in range(2)]
    negmax = [pool.take(1, F32) for _ in range(2)]
    rowsum = [pool.take(1, F32) for _ in range(2)]
    rinv = [pool.take(1, F32) for _ in range(2)]
    stage1 = [pool.take(128, BF16) for _ in range(2)]
    p.sem("trs")

    with nc.Block() as blk:
        @blk.gpsimd
        def _(g):
            p.chain(g, "g", g.iota, band[:, :].bitcast(I32), [[1, MW]], channel_multiplier=0)
    with nc.Block() as blk:
        @blk.vector
        def _(v):
            p.chain(v, "v", v.tensor_copy, bandw[:, :], band[:, :].bitcast(I32))
    with nc.Block() as blk:
        @blk.vector
        def _(v):
            p.chain(v, "v", v.tensor_scalar, bandw[:, :], bandw[:, :], rowid[:, 0:1],
                    -4096.0, op0=OP.subtract, op1=OP.add)
    with nc.Block() as blk:
        @blk.vector
        def _(v):
            p.chain(v, "v", v.tensor_scalar, bandw[:, :], bandw[:, :], 0.0, None, op0=OP.is_gt)
    with nc.Block() as blk:
        @blk.vector
        def _(v):
            p.chain(v, "v", v.tensor_scalar, band[:, :], bandw[:, :], -1e9, None, op0=OP.mult)

    SCALE = 1.0 / np.sqrt(128.0)
    NKB = [qt // 4 + 1 for qt in range(32)]
    NSUB = [n * 4 for n in NKB]
    CKB = [sum(NKB[:q]) for q in range(33)]
    CSUB = [sum(NSUB[:q]) for q in range(33)]

    p.sem("dSa")
    p.sem("dSb")

    def load_pair(sy, pj, buf):
        sm = "dSa" if buf == 0 else "dSb"
        inst = sy.dma_start(
            k_sb[buf][:, :].rearrange("p (r t) -> p r t", r=8),
            bass.AP(qkR.tensor, (pj * 256 + 128) * 512,
                    [[512, 128], [2048 * 512, 8], [1, 512]]))
        p.inc(inst, sm, 16)
        inst = sy.dma_start(
            q_sb[buf][:, :].rearrange("p (r t) -> p r t", r=8),
            bass.AP(qkR.tensor, (pj * 256) * 512,
                    [[512, 128], [2048 * 512, 8], [1, 512]]))
        p.inc(inst, sm, 16)
        for r in range(8):
            inst = sy.dma_start(
                v_sb[buf][:, r * 512:(r + 1) * 512].rearrange("p (j d) -> p j d", j=4),
                bass.AP(vR.tensor, (r * 4096 + pj * 512) * 128,
                        [[128, 128], [128 * 128, 4], [1, 128]]))
            p.inc(inst, sm, 16)
        return p.cnt[sm]

    pair_dS = [None, None]
    with nc.Block() as blk:
        @blk.sync
        def _(sy):
            p.wait(sy, "cc", 5)
            pair_dS[0] = load_pair(sy, 0, 0)

    for pj in range(npairs):
        kb_buf = pj % 2
        with nc.Block() as blk:
            SS0 = p.cnt["sS"]
            SE0 = p.cnt["sE"]
            V0 = p.cnt["v"]
            S0 = p.cnt["s"]
            TRM0 = p.cnt["tr_mm"]
            TRE0 = p.cnt["tr_ev"]
            TRS0 = p.cnt["trs"]
            OM0 = p.cnt["o_mm"]
            OE0 = p.cnt["o_ev"]
            P0 = {0: p.cnt["dOa"], 1: p.cnt["dOb"]}
            my_dS = pair_dS[pj % 2]
            ps_o = [psO[:, 0:128], psC[:, 0:128]]

            @blk.sync
            def _(sy):
                if pj + 1 < npairs:
                    pair_dS[(pj + 1) % 2] = load_pair(sy, pj + 1, (pj + 1) % 2)
                for qt in range(32):
                    p.wait(sy, "o_ev", OE0 + qt + 1)
                    r0o = (qt // 4) * 4096 + pj * 512 + (qt % 4) * 128
                    inst = sy.dma_start(aoS[r0o:r0o + 128, :], stage1[qt % 2][:, :])
                    p.inc(inst, "dOa" if qt % 2 == 0 else "dOb", 16)

            @blk.tensor
            def _(t):
                bankB = {}
                t.wait_ge(p.sem("dSa" if kb_buf == 0 else "dSb"), my_dS)

                def t1(qt):
                    for kb in range(NKB[qt]):
                        g_ = CKB[qt] + kb
                        if SE0 + g_ - 1 > SE0 - 2 and g_ >= 2:
                            p.wait(t, "sE", SE0 + g_ - 1)
                        inst = t.matmul(psA[g_ % 2][:, :],
                                        q_sb[kb_buf][:, qt * 128:(qt + 1) * 128],
                                        k_sb[kb_buf][:, kb * 512:(kb + 1) * 512],
                                        start=True, stop=True)
                        p.inc(inst, "sS")

                def t2(qt):
                    nsub = NSUB[qt]

                    def emit_tr(j):
                        gs = CSUB[qt] + j
                        if j == 0:
                            p.wait(t, "s", S0 + qt + 1)
                        if bankB.get(gs % 4) is not None:
                            p.wait(t, *bankB[gs % 4])
                        inst = t.transpose(psB[gs % 4][:, 0:128],
                                           p_sb[qt % 2][:, j * 128:(j + 1) * 128],
                                           ident[:, :])
                        p.inc(inst, "tr_mm")
                        if j % 2 == 0:
                            bankB[gs % 4] = ("tr_ev", TRE0 + CSUB[qt] // 2 + j // 2 + 1)
                        else:
                            bankB[gs % 4] = ("trs", TRS0 + CSUB[qt] // 2 + (j + 1) // 2)

                    def emit_mm(j):
                        if j % 2 == 0:
                            p.wait(t, "tr_ev", TRE0 + CSUB[qt] // 2 + j // 2 + 1)
                        else:
                            p.wait(t, "trs", TRS0 + CSUB[qt] // 2 + (j + 1) // 2)
                        if j == 0 and OE0 + qt - 1 > 0:
                            p.wait(t, "o_ev", OE0 + qt - 1)
                        inst = t.matmul(ps_o[qt % 2], pT_sb[qt % 2][:, j * 128:(j + 1) * 128],
                                        v_sb[kb_buf][:, j * 128:(j + 1) * 128],
                                        start=(j == 0), stop=(j == nsub - 1))
                        if j == nsub - 1:
                            p.inc(inst, "o_mm")

                    for j in range(nsub):
                        emit_tr(j)
                        if j >= 3:
                            emit_mm(j - 3)
                    for j in range(max(0, nsub - 3), nsub):
                        emit_mm(j)

                for qt in range(32):
                    t1(qt)
                    if qt >= 1:
                        t2(qt - 1)
                t2(31)

            @blk.vector
            def _(v):
                for it in range(33):
                    if it < 32:
                        qt = it
                        if qt >= 2:
                            p.wait(v, "s", S0 + qt - 1)
                        for kb in range(NKB[qt]):
                            g_ = CKB[qt] + kb
                            p.wait(v, "sS", SS0 + g_ + 1)
                            mstart = 4096 - qt * 128 + kb * 512
                            inst = v.scalar_tensor_tensor(
                                s_sb[qt % 2][:, kb * 512:(kb + 1) * 512], psA[g_ % 2][:, :],
                                SCALE, band[:, mstart:mstart + 512],
                                op0=OP.mult, op1=OP.add)
                            p.inc(inst, "sE")
                        p.wait(v, "sE", SE0 + CKB[qt] + NKB[qt])
                        if p.cnt["v"]:
                            v.wait_ge(p.sem("v"), p.cnt["v"])
                        inst = v.reduce_max(negmax[qt % 2][:, :],
                                            s_sb[qt % 2][:, 0:NKB[qt] * 512],
                                            axis=AX, negate=True)
                        p.inc(inst, "v")
                    if it >= 1:
                        qt = it - 1
                        for j in range(NSUB[qt]):
                            if j % 2 == 0:
                                p.wait(v, "tr_mm", TRM0 + CSUB[qt] + j + 1)
                                inst = v.tensor_copy(
                                    pT_sb[qt % 2][:, j * 128:(j + 1) * 128],
                                    psB[(CSUB[qt] + j) % 4][:, 0:128])
                                p.inc(inst, "tr_ev")
                        p.wait(v, "s", S0 + qt + 1)
                        v.wait_ge(p.sem("v"), p.cnt["v"])
                        inst = v.reciprocal(rinv[qt % 2][:, :], rowsum[qt % 2][:, :])
                        p.inc(inst, "v")
                        p.wait(v, "o_mm", OM0 + qt + 1)
                        pw = P0[qt % 2] + 16 * (qt // 2)
                        if pw > 0:
                            p.wait(v, "dOa" if qt % 2 == 0 else "dOb", pw)
                        v.wait_ge(p.sem("v"), p.cnt["v"])
                        inst = v.tensor_scalar(stage1[qt % 2][:, :], ps_o[qt % 2],
                                               rinv[qt % 2][:, 0:1], None, op0=OP.mult)
                        p.inc(inst, "o_ev")

            @blk.scalar
            def _(sc):
                for it in range(33):
                    if it < 32:
                        qt = it
                        p.wait(sc, "v", V0 + (1 if qt == 0 else 2 * qt))
                        p.wait(sc, "sE", SE0 + CKB[qt] + NKB[qt])
                        if qt >= 2:
                            p.wait(sc, "tr_mm", TRM0 + CSUB[qt - 1])
                        inst = sc.activation(p_sb[qt % 2][:, 0:NKB[qt] * 512],
                                             s_sb[qt % 2][:, 0:NKB[qt] * 512], AF.Exp,
                                             bias=negmax[qt % 2][:, 0:1],
                                             accum_out=rowsum[qt % 2][:, :])
                        p.inc(inst, "s")
                    if it >= 1:
                        qt = it - 1
                        for j in range(NSUB[qt]):
                            if j % 2 == 1:
                                p.wait(sc, "tr_mm", TRM0 + CSUB[qt] + j + 1)
                                inst = sc.copy(
                                    pT_sb[qt % 2][:, j * 128:(j + 1) * 128],
                                    psB[(CSUB[qt] + j) % 4][:, 0:128])
                                p.inc(inst, "trs")




def emit_dequant(nc, p, pool, d):
    gWp, gSc, uWp, uSc, dWp, dSc = d["gWp"], d["gSc"], d["uWp"], d["uSc"], d["dWp"], d["dSc"]
    wgL, wuL, wdL = d["wgL"], d["wuL"], d["wdL"]
    ident, psB = d["ident"], d["psB"]
    pool.reset()
    wp_sb = [pool.take(1024, I32) for _ in range(2)]
    sc_sb = [pool.take(64, F32) for _ in range(2)]
    nib = [pool.take(2048, I32) for _ in range(2)]
    m7 = [pool.take(2048, I32) for _ in range(2)]
    bits = [pool.take(2048, I32) for _ in range(2)]
    sgn = [pool.take(2048, I32) for _ in range(2)]
    alt = [pool.take(2048, I32) for _ in range(2)]
    msk = [pool.take(2048, I32) for _ in range(2)]
    outi = [pool.take(2048, I32) for _ in range(2)]
    deq = [pool.take(2048, BF16) for _ in range(2)]
    stg = [pool.take(128, BF16) for _ in range(2)]
    p.sem("dq0")
    p.sem("dq1")
    p.sem("dSq0")
    p.sem("dSq1")
    p.sem("dQa")
    p.sem("dQb")
    p.sem("trs")

    gWp_, gSc_, uWp_, uSc_, dWp_, dSc_ = gWp, gSc, uWp, uSc, dWp, dSc
    tiles = [(nm, rt) for nm in range(3) for rt in range(16)]
    mats = [(gWp_, gSc_, wgL, 0x3F000000, 252), (uWp_, uSc_, wuL, 0x3F000000, 252),
            (dWp_, dSc_, wdL, 0x3E800000, 250)]

    def chain_front(v, b, ALT, OFF, dS_val):
        c = lambda fn, *a, **k: p.chain(v, "dq0", fn, *a, **k)
        v.wait_ge(p.sem("dSq0" if b == 0 else "dSq1"), dS_val)
        c(v.tensor_scalar, nib[b][:, 0:1024], wp_sb[b][:, :], 15, None, op0=OP.bitwise_and)
        c(v.tensor_scalar, nib[b][:, 1024:2048], wp_sb[b][:, :], 4, 15,
          op0=OP.logical_shift_right, op1=OP.bitwise_and)
        c(v.tensor_scalar, m7[b][:, :], nib[b][:, :], 7, None, op0=OP.bitwise_and)
        c(v.tensor_scalar, bits[b][:, :], m7[b][:, :], OFF, None, op0=OP.add)
        c(v.tensor_scalar, bits[b][:, :], bits[b][:, :], 22, None, op0=OP.logical_shift_left)
        c(v.tensor_scalar, sgn[b][:, :], nib[b][:, :], 8, 28,
          op0=OP.bitwise_and, op1=OP.logical_shift_left)
        c(v.tensor_tensor, bits[b][:, :], bits[b][:, :], sgn[b][:, :], op=OP.bitwise_or)
        c(v.tensor_scalar, alt[b][:, :], nib[b][:, :], 1, None, op0=OP.bitwise_and)
        c(v.tensor_scalar, alt[b][:, :], alt[b][:, :], ALT, None, op0=OP.mult)
        c(v.tensor_tensor, alt[b][:, :], alt[b][:, :], sgn[b][:, :], op=OP.bitwise_or)
        c(v.tensor_scalar, msk[b][:, :], m7[b][:, :], 2, None, op0=OP.is_ge)

    def chain_tail(g, b, front_cnt, tr_need):
        c = lambda fn, *a, **k: p.chain(g, "dq1", fn, *a, **k)
        g.wait_ge(p.sem("dq0"), front_cnt)
        if tr_need > 0:
            g.wait_ge(p.sem("tr_mm"), tr_need)
        c(g.tensor_tensor, outi[b][:, :], bits[b][:, :], alt[b][:, :], op=OP.subtract)
        c(g.tensor_tensor, outi[b][:, :], outi[b][:, :], msk[b][:, :], op=OP.mult)
        c(g.tensor_tensor, outi[b][:, :], outi[b][:, :], alt[b][:, :], op=OP.add)
        of = outi[b][:, :].bitcast(F32)
        for par in range(2):
            src = of[:, par * 1024:(par + 1) * 1024].rearrange("p (g j) -> p g j", j=16)
            dst = deq[b][:, :].rearrange("p (g j) -> p g j", j=32)[:, :, par::2]
            scb = sc_sb[b][:, :].unsqueeze(-1).broadcast_to([128, 64, 16])
            c(g.tensor_tensor, dst, src, scb, op=OP.mult)

    NFR = 11
    NTL = 5
    ld_dS = []
    ld_dS = []
    with nc.Block() as blk:
        CH0v = p.cnt["dq0"]
        CH1 = p.cnt["dq1"]
        TRM0 = p.cnt["tr_mm"]
        TRS0 = p.cnt["trs"]
        P0 = {0: p.cnt["dQa"], 1: p.cnt["dQb"]}

        @blk.sync
        def _(sy):
            for ti, (nm, rt) in enumerate(tiles):
                wp_d, sc_d = mats[nm][0], mats[nm][1]
                b = ti % 2
                if ti >= 2:
                    p.wait(sy, "dq1", CH1 + NTL * (ti - 1))
                smq = "dSq0" if b == 0 else "dSq1"
                inst = sy.dma_start(wp_sb[b][:, :], wp_d[rt * 128:(rt + 1) * 128, :])
                p.inc(inst, smq, 16)
                inst = sy.dma_start(sc_sb[b][:, :], sc_d[rt * 128:(rt + 1) * 128, :])
                p.inc(inst, smq, 16)
                ld_dS.append(p.cnt[smq])

        @blk.vector
        def _(v):
            for ti, (nm, rt) in enumerate(tiles):
                ALT, OFF = mats[nm][3], mats[nm][4]
                if ti >= 2:
                    p.wait(v, "dq1", CH1 + NTL * (ti - 1))
                chain_front(v, ti % 2, ALT, OFF, ld_dS[ti])

        @blk.gpsimd
        def _(g):
            for ti, (nm, rt) in enumerate(tiles):
                tr_need = TRM0 + 16 * (ti - 1) if ti >= 2 else 0
                chain_tail(g, ti % 2, CH0v + NFR * (ti + 1), tr_need)

        @blk.tensor
        def _(t):
            for ti, (nm, rt) in enumerate(tiles):
                b = ti % 2
                t.wait_ge(p.sem("dq1"), CH1 + NTL * (ti + 1))
                for cc_ in range(16):
                    gs = 16 * ti + cc_
                    if gs >= 4:
                        p.wait(t, "trs", TRS0 + gs - 3)
                    inst = t.transpose(psB[gs % 4][:, 0:128],
                                       deq[b][:, cc_ * 128:(cc_ + 1) * 128], ident[:, :])
                    p.inc(inst, "tr_mm")

        @blk.scalar
        def _(sc):
            for ti, (nm, rt) in enumerate(tiles):
                out_d = mats[nm][2]
                for cc_ in range(16):
                    gs = 16 * ti + cc_
                    p.wait(sc, "tr_mm", TRM0 + gs + 1)
                    pw = P0[gs % 2] + 16 * (gs // 2)
                    if pw > 0:
                        p.wait(sc, "dQa" if gs % 2 == 0 else "dQb", pw)
                    inst = sc.copy(stg[gs % 2][:, :], psB[gs % 4][:, 0:128])
                    p.inc(inst, "trs")
                    p.wait(sc, "trs", TRS0 + gs + 1)
                    inst = sc.dma_start(
                        out_d[cc_ * 128:(cc_ + 1) * 128, rt * 128:(rt + 1) * 128],
                        stg[gs % 2][:, :])
                    p.inc(inst, "dQa" if gs % 2 == 0 else "dQb", 16)



class Pool:

    def __init__(self, ap, words):
        self.ap = ap
        self.words = words
        self.off = 0

    def take(self, nelem, dt):
        esz = {F32: 4, I32: 4, BF16: 2}[dt]
        w = (nelem * esz + 31) // 32 * 8
        assert self.off + w <= self.words, f"pool overflow: {self.off}+{w} > {self.words}"
        view = self.ap[:, self.off:self.off + w]
        self.off += w
        if dt is not F32:
            view = view.bitcast(dt)
        return view[:, 0:nelem]

    def mark(self):
        return self.off

    def reset(self, off=0):
        self.off = off


def build_fused(dev_out=(), phases=None):
    MLP = 16384
    nc = bass.Bass("TRN2", target_bir_lowering=False)
    ein = lambda n, sh, dt: nc.dram_tensor(n, sh, dt, kind="ExternalInput").ap()
    xs = ein("xs", [TOK, D], F32)
    wpT = ein("wpT", [D, 3 * D], F32)
    woT = ein("woT", [D, D], F32)
    wrTe = ein("wrTe", [D, 8 * D], F32)
    brh = ein("brh", [128, 128], F32)
    weT = ein("weT", [D, 8 * D], F32)
    ebb = ein("ebb", [8, D], F32)
    wqkvT = ein("wqkvT", [D, 3072], F32)
    wprojT = ein("wprojT", [D, D], F32)
    gWp = ein("gWp", [2048, 1024], I32)
    gSc = ein("gSc", [2048, 64], F32)
    uWp = ein("uWp", [2048, 1024], I32)
    uSc = ein("uSc", [2048, 64], F32)
    dWp = ein("dWp", [2048, 1024], I32)
    dSc = ein("dSc", [2048, 64], F32)
    x4 = nc.dram_tensor("x4", [TOK, D], F32, kind="ExternalOutput").ap()

    def internal(n, sh, dt, shared=False):
        kind = "ExternalOutput" if n in dev_out else "Internal"
        kw = dict(addr_space="Shared") if (shared and n not in dev_out) else {}
        return nc.dram_tensor(n, sh, dt, kind=kind, **kw).ap()

    qkS = internal("qkS", [8 * 2048, 512], BF16)
    qkR = internal("qkR", [8 * 2048, 512], BF16)
    vS = internal("vS", [8 * 4096, 128], BF16)
    vR = internal("vR", [8 * 4096, 128], BF16)
    aoS = internal("aoS", [8 * 4096, 128], BF16)
    aoR = internal("aoR", [8 * 4096, 128], BF16)
    wgL = internal("wgL", [D, 2048], BF16)
    wuL = internal("wuL", [D, 2048], BF16)
    wdL = internal("wdL", [2048, D], BF16)
    wgG = internal("wgG", [8 * D, 2048], BF16, shared=True)
    wuG = internal("wuG", [8 * D, 2048], BF16, shared=True)
    wdG = internal("wdG", [MLP, D], BF16, shared=True)
    x1D = internal("x1D", [TOK, D], F32)
    x2D = internal("x2D", [TOK, D], F32)
    x3D = internal("x3D", [TOK, D], F32)
    ln2TD = internal("ln2TD", [D, TOK], BF16)
    ln3TD = internal("ln3TD", [D, TOK], BF16)
    ln4TD = internal("ln4TD", [D, TOK], BF16)
    ao1D = internal("ao1D", [TOK, D], BF16)
    c_accD = internal("c_accD", [8, TOK], F32)

    if phases is None:
        phases = {"dequant", "A", "attn", "projC", "moe", "l1attn", "projF", "l1ffn",
                  "cc"}
    with ExitStack() as st:
        p = Prog(nc, st)
        sb = lambda name, shape, dt: st.enter_context(nc.sbuf_tensor(name, shape, dt))
        psa = lambda name, shape, dt: st.enter_context(nc.psum_tensor(name, shape, dt))

        ident = sb("ident", [128, 128], BF16)
        identF = sb("identF", [128, 128], F32)
        iota_i = sb("iota_i", [128, 128], I32)
        iota_f = sb("iota_f", [128, 128], F32)
        rowid = sb("rowid", [128, 1], F32)
        bdmask = sb("bdmask", [128, 128], F32)
        ones_bf = sb("ones_bf", [128, 1], BF16)
        brh_sb = sb("brh_sb", [128, 128], F32)
        eb_sb = sb("eb_sb", [8, D], BF16)
        cT_sb = sb("cT_sb", [8, TOK], BF16)
        c_col = sb("c_col", [128, 128], F32)
        sums = sb("sums", [128, 2], F32)
        mean = sb("mean", [128, 1], F32)
        rstd = sb("rstd", [128, 1], F32)
        cstage = [sb(f"cstage{i}", [1, 512], F32) for i in range(2)]
        PW = 49152
        pool = Pool(sb("pool", [128, PW], F32), PW)

        psA = [psa(f"psA{i}", [128, 512], F32) for i in range(2)]
        psB = [psa(f"psB{i}", [128, 1024], BF16) for i in range(4)]
        psO = psa("psO", [128, 512], F32)
        psC = psa("psC", [128, 512], F32)

        for s_ in ["v", "s", "t", "g", "dS", "dG", "dO", "dA", "dOa", "dOb", "dOx",
                   "dAa", "dAb", "tr_mm", "tr_ev",
                   "mm", "ev", "cm", "cv", "sg", "cc", "o_mm", "o_ev", "sS", "sE"]:
            p.sem(s_)

        with nc.Block() as blk:
            @blk.sync
            def _(sy):
                inst = sy.dma_start(brh_sb[:, :], brh)
                p.inc(inst, "dS", 16)
                p.wait(sy, "dS")

            @blk.gpsimd
            def _(g):
                inst = g.dma_start(eb_sb[:, :], ebb)
                p.inc(inst, "dG", 16)
                p.wait(g, "dG")
        emit_ident(p, ident, iota_i, iota_f, rowid)
        with nc.Block() as blk:
            @blk.vector
            def _(v):
                p.chain(v, "v", v.tensor_scalar, identF[:, :], iota_f[:, :], rowid[:, 0:1],
                        None, op0=OP.is_equal)
                p.chain(v, "v", v.memset, ones_bf[:, :], 1.0)
        with nc.Block() as blk:
            @blk.vector
            def _(v):
                p.chain(v, "v", v.tensor_scalar, bdmask[:, :].bitcast(I32), iota_i[:, :],
                        2, None, op0=OP.logical_shift_right)
                p.chain(v, "v", v.tensor_copy, sums[:, 0:1].bitcast(I32),
                        iota_i[:, 0:1])
        with nc.Block() as blk:
            @blk.gpsimd
            def _(g):
                p.chain(g, "g", g.iota, sums[:, 0:1].bitcast(I32), [[1, 1]], channel_multiplier=1)
        with nc.Block() as blk:
            @blk.vector
            def _(v):
                p.chain(v, "v", v.tensor_scalar, sums[:, 0:1].bitcast(I32),
                        sums[:, 0:1].bitcast(I32), 2, None, op0=OP.logical_shift_right)
        with nc.Block() as blk:
            @blk.vector
            def _(v):
                p.chain(v, "v", v.tensor_copy, mean[:, 0:1], sums[:, 0:1].bitcast(I32))
                p.chain(v, "v", v.tensor_copy, rstd[:, :].bitcast(I32)[:, 0:1],
                        sums[:, 0:1].bitcast(I32))
        with nc.Block() as blk:
            @blk.vector
            def _(v):
                p.chain(v, "v", v.tensor_copy, identF[:, :], bdmask[:, :].bitcast(I32))
        with nc.Block() as blk:
            @blk.vector
            def _(v):
                p.chain(v, "v", v.tensor_scalar, bdmask[:, :], identF[:, :], mean[:, 0:1],
                        None, op0=OP.is_equal)
        with nc.Block() as blk:
            @blk.vector
            def _(v):
                p.chain(v, "v", v.tensor_scalar, bdmask[:, :], bdmask[:, :], -1.0, 1e9,
                        op0=OP.add, op1=OP.mult)
        with nc.Block() as blk:
            @blk.vector
            def _(v):
                p.chain(v, "v", v.tensor_scalar, identF[:, :], iota_f[:, :], rowid[:, 0:1],
                        None, op0=OP.is_equal)

        emit_dequant(nc, p, pool, dict(
            gWp=gWp, gSc=gSc, uWp=uWp, uSc=uSc, dWp=dWp, dSc=dSc,
            wgL=wgL, wuL=wuL, wdL=wdL, ident=ident, psB=psB))
        with nc.Block() as blk:
            @blk.gpsimd
            def _(g):
                p.wait_all(g, ["dQa", "dQb"])
                for src, dst in [(wgL, wgG), (wuL, wuG), (wdL, wdG)]:
                    inst = g.collective_compute(
                        "AllGather", OP.bypass,
                        replica_groups=[list(range(NCORES))],
                        ins=[src.opt()], outs=[dst.opt()])
                    p.inc(inst, "cc")

        pool.reset()
        x_t = pool.take(D, F32)
        scratch = pool.take(D, F32)
        ln_bf = pool.take(D, BF16)
        ln1T = pool.take(KC * TOK, BF16)
        wslab = pool.take(KC * 512, BF16)
        stage5 = [pool.take(512, BF16) for _ in range(2)]

        for tt in range(TOK // 128):
            with nc.Block() as blk:
                @blk.sync
                def _(sy):
                    inst = sy.dma_start(x_t[:, :], xs[tt * 128:(tt + 1) * 128, :])
                    p.inc(inst, "dS", 16)
                    p.wait(sy, "dS")
            emit_ln(p, None, x_t, ln_bf, sums, mean, rstd, scratch)
            emit_transpose_to_slab2(p, ident, ln_bf, ln1T, tt * 128, TOK, psB)

        def mm_evac_store(jobs):
            with nc.Block() as blk:
                M0 = p.cnt["mm"]
                E0 = p.cnt["ev"]
                P0 = {0: p.cnt["dOa"], 1: p.cnt["dOb"]}

                @blk.tensor
                def _(t):
                    for j, (lhs_l, rhs_l, _o) in enumerate(jobs):
                        if E0 + j - 1 > 0:
                            p.wait(t, "ev", E0 + j - 1)
                        for c in range(len(lhs_l)):
                            inst = t.matmul(psA[j % 2][:, :], lhs_l[c], rhs_l[c],
                                            start=(c == 0), stop=(c == len(lhs_l) - 1))
                        p.inc(inst, "mm")

                @blk.vector
                def _(v):
                    for j in range(len(jobs)):
                        p.wait(v, "mm", M0 + j + 1)
                        pw = P0[j % 2] + 16 * (j // 2)
                        if pw > 0:
                            p.wait(v, "dOa" if j % 2 == 0 else "dOb", pw)
                        inst = v.tensor_copy(stage5[j % 2][:, :], psA[j % 2][:, :])
                        p.inc(inst, "ev")

                @blk.sync
                def _(sy):
                    for j, (_l, _r, o_ap) in enumerate(jobs):
                        p.wait(sy, "ev", E0 + j + 1)
                        inst = sy.dma_start(o_ap, stage5[j % 2][:, :])
                        p.inc(inst, "dOa" if j % 2 == 0 else "dOb", 16)

        def load_wslab(src_ap, c0, col0, ncols=512):
            with nc.Block() as blk:
                @blk.gpsimd
                def _(g):
                    for c in range(KC):
                        inst = g.dma_start(wslab[:, c * 512:c * 512 + ncols],
                                           src_ap[c * 128:(c + 1) * 128, col0:col0 + ncols])
                        p.inc(inst, "dG", 16)
                    p.wait(g, "dG")

        for mb in range(8):
            load_wslab(wpT, 0, mb * 512)
            jobs = []
            for m4 in range(4):
                row0 = mb * 512 + m4 * 128
                is_k = row0 >= D
                h = (row0 - D) // 128 if is_k else row0 // 128
                for t4 in range(4):
                    b = t4
                    j = b * 2 + h // 8
                    r0 = j * 2048 + (h % 8) * 256 + (128 if is_k else 0)
                    lhs = [wslab[:, c * 512 + m4 * 128: c * 512 + (m4 + 1) * 128] for c in range(KC)]
                    rhs = [ln1T[:, c * TOK + t4 * 512: c * TOK + (t4 + 1) * 512] for c in range(KC)]
                    jobs.append((lhs, rhs, qkS[r0:r0 + 128, :]))
            mm_evac_store(jobs)
        for vb in range(4):
            load_wslab(wpT, 0, 2 * D + vb * 512)
            h0 = vb * 4
            jv = (h0 // 8)
            jobs = []
            for tt in range(TOK // 128):
                b = tt // 4
                j = b * 2 + jv
                lhs = [ln1T[:, c * TOK + tt * 128: c * TOK + (tt + 1) * 128] for c in range(KC)]
                rhs = [wslab[:, c * 512:(c + 1) * 512] for c in range(KC)]
                off = (j * 4096 + (h0 % 8) * 512 + (tt % 4) * 128) * 128
                dst = bass.AP(vS.tensor, off, [[128, 128], [512 * 128, 4], [1, 128]])
                jobs.append((lhs, rhs, dst))
            mm_evac_store(jobs)
        with nc.Block() as blk:
            @blk.gpsimd
            def _(g):
                p.wait_all(g, ["dOa", "dOb"])
                with allow_ici_all_to_all():
                    inst = g.collective_compute(
                        "AllToAll", OP.bypass,
                        replica_groups=[list(range(NCORES))],
                        ins=[qkS.opt()], outs=[qkR.opt()])
                    p.inc(inst, "cc")
                    inst = g.collective_compute(
                        "AllToAll", OP.bypass,
                        replica_groups=[list(range(NCORES))],
                        ins=[vS.opt()], outs=[vR.opt()])
                    p.inc(inst, "cc")

        emit_attn_phase(nc, p, pool, dict(
            qkR=qkR, vR=vR, aoS=aoS, ident=ident, rowid=rowid,
            psA=psA, psB=psB, psO=psO, psC=psC))

        with nc.Block() as blk:
            @blk.gpsimd
            def _(g):
                p.wait_all(g, ["dOa", "dOb"])
                with allow_ici_all_to_all():
                    inst = g.collective_compute(
                        "AllToAll", OP.bypass,
                        replica_groups=[list(range(NCORES))],
                        ins=[aoS.opt()], outs=[aoR.opt()])
                    p.inc(inst, "cc")

        def emit_proj(xs_in, ao_rd, wT_in, x1_out, lnT_out, wait_cc=None):
            pool.reset()
            w_sb = pool.take(KC * D, BF16)
            ao_t = pool.take(D, BF16)
            aoT_t = pool.take(KC * 128, BF16)
            x_t = pool.take(D, F32)
            x1_t = pool.take(D, F32)
            scratch = pool.take(D, F32)
            ln_bf = pool.take(D, BF16)
            stage2 = [pool.take(128, BF16) for _ in range(2)]
            with nc.Block() as blk:
                @blk.gpsimd
                def _(g):
                    if wait_cc is not None:
                        p.wait(g, "cc", wait_cc)
                    p.wait_all(g, ["dOa", "dOb", "dOx", "dAa", "dAb"])
                    for c in range(KC):
                        inst = g.dma_start(w_sb[:, c * D:(c + 1) * D], wT_in[c * 128:(c + 1) * 128, :])
                        p.inc(inst, "dG", 16)
                    p.wait(g, "dG")
            for tt in range(TOK // 128):
                r0 = tt * 128
                with nc.Block() as blk:
                    @blk.sync
                    def _(sy):
                        inst = sy.dma_start(x_t[:, :], xs_in[r0:r0 + 128, :])
                        p.inc(inst, "dS", 16)

                    @blk.gpsimd
                    def _(g):
                        ao_rd(g, tt, ao_t)
                        p.wait(g, "dG")
                with nc.Block() as blk:
                    @blk.sync
                    def _(sy):
                        p.wait(sy, "dS")
                emit_transpose_to_slab2(p, ident, ao_t, aoT_t, 0, 128, psB)
                with nc.Block() as blk:
                    M0 = p.cnt["mm"]
                    E0 = p.cnt["ev"]

                    @blk.tensor
                    def _(t):
                        for j in range(4):
                            if E0 + j - 1 > 0:
                                p.wait(t, "ev", E0 + j - 1)
                            for c in range(KC):
                                inst = t.matmul(psA[j % 2][:, :],
                                                aoT_t[:, c * 128:(c + 1) * 128],
                                                w_sb[:, c * D + j * 512: c * D + (j + 1) * 512],
                                                start=(c == 0), stop=(c == KC - 1))
                            p.inc(inst, "mm")

                    @blk.vector
                    def _(v):
                        for j in range(4):
                            p.wait(v, "mm", M0 + j + 1)
                            inst = v.tensor_tensor(x1_t[:, j * 512:(j + 1) * 512],
                                                   psA[j % 2][:, :],
                                                   x_t[:, j * 512:(j + 1) * 512], op=OP.add)
                            p.inc(inst, "ev")
                with nc.Block() as blk:
                    @blk.sync
                    def _(sy):
                        inst = sy.dma_start(x1_out[r0:r0 + 128, :], x1_t[:, :])
                        p.inc(inst, "dO", 16)
                emit_ln(p, None, x1_t, ln_bf, sums, mean, rstd, scratch)
                emit_lnT_store(p, ident, ln_bf, lnT_out, r0, [psB[0][:, 0:128], psB[1][:, 0:128]],
                               stage2)

        def ao_from_aoR(g, tt, ao_t):
            b = tt // 4
            for hb in range(2):
                off = ((b * 2 + hb) * 4096 + (tt % 4) * 128) * 128
                inst = g.dma_start(
                    ao_t[:, hb * 1024:(hb + 1) * 1024].rearrange("p (h d) -> p h d", h=8),
                    bass.AP(aoR.tensor, off, [[128, 128], [512 * 128, 8], [1, 128]]))
                p.inc(inst, "dG", 16)

        emit_proj(xs, ao_from_aoR, woT, x1D, ln2TD, wait_cc=6)

        emit_moe(nc, p, pool, dict(
            ln2T=ln2TD, x1s=x1D, wrTe=wrTe, weT=weT, x2=x2D, lnT=ln3TD, c_accD=c_accD,
            brh_sb=brh_sb, eb_sb=eb_sb, ones_bf=ones_bf, cT_sb=cT_sb, c_col=c_col,
            identF=identF, ident=ident, sums=sums, mean=mean, rstd=rstd, cstage=cstage,
            psA=psA, psB=psB, psC=psC))

        emit_l1attn(nc, p, pool, dict(
            ln3T=ln3TD, wqkvT=wqkvT, ao1=ao1D, bdmask=bdmask, ident=ident,
            psA=psA, psB=psB, psO=psO, psC=psC))

        def ao_from_ao1(g, tt, ao_t):
            inst = g.dma_start(ao_t[:, :], ao1D[tt * 128:(tt + 1) * 128, :])
            p.inc(inst, "dG", 16)

        emit_proj(x2D, ao_from_ao1, wprojT, x3D, ln4TD)

        emit_l1ffn(nc, p, pool, dict(
            ln4T=ln4TD, x3s=x3D, wgG=wgG, wuG=wuG, wdG=wdG, x4=x4, psA=psA, psB=psB))

        with nc.Block() as blk:
            @blk.sync
            def _(sy):
                p.wait_all(sy, ["dO", "dOa", "dOb", "dOx", "dAa", "dAb", "dS", "dG", "dA",
                                "dSa", "dSb", "dSg", "dSu", "dSw", "dGa", "dGb",
                                "dCa", "dCb", "dQa", "dQb", "dSq0", "dSq1", "dXa", "dXb",
                                "dRa", "dRb", "dTa", "dTb", "dLa", "dLb"])
    return nc


def shard_tokens(full):
    out = []
    for i in range(NCORES):
        sl = full[:, i * SC:(i + 1) * SC]
        out.append(np.ascontiguousarray(sl.reshape((TOK,) + full.shape[2:])))
    return out


def unshard_tokens(shards):
    rest = shards[0].shape[1:]
    full = np.empty((B, S) + rest, dtype=shards[0].dtype)
    for i, sh in enumerate(shards):
        full[:, i * SC:(i + 1) * SC] = sh.reshape((B, SC) + rest)
    return full


def fused_maps(L0_attn_x, L0_attn_w_project, L0_attn_w_out,
               L0_ffn_router_w, L0_ffn_router_b, L0_ffn_expert_w, L0_ffn_expert_b,
               L1_attn_w_qkv, L1_attn_w_proj,
               L1_ffn_gate_W_packed, L1_ffn_gate_scales,
               L1_ffn_up_W_packed, L1_ffn_up_scales,
               L1_ffn_down_W_packed, L1_ffn_down_scales):
    x = np.asarray(L0_attn_x, dtype=np.float32)
    xs = shard_tokens(x)
    wpT = np.ascontiguousarray(np.asarray(L0_attn_w_project, dtype=np.float32).T)
    woT = np.ascontiguousarray(np.asarray(L0_attn_w_out, dtype=np.float32).T)
    rw = np.asarray(L0_ffn_router_w, dtype=np.float32)
    rb = np.asarray(L0_ffn_router_b, dtype=np.float32)
    ew = np.asarray(L0_ffn_expert_w, dtype=np.float32)
    ebias = np.ascontiguousarray(np.asarray(L0_ffn_expert_b, dtype=np.float32))
    wrTe = np.ascontiguousarray(
        rw.reshape(16, 128, 8, D).transpose(0, 2, 1, 3).reshape(8 * D, D).T)
    brh = np.ascontiguousarray(rb.reshape(D, 8).T.reshape(8 * D).reshape(128, 128).T)
    weT = np.ascontiguousarray(ew.transpose(2, 0, 1).reshape(D, 8 * D))
    wqkvT = np.ascontiguousarray(np.asarray(L1_attn_w_qkv, dtype=np.float32).T)
    wprojT = np.ascontiguousarray(np.asarray(L1_attn_w_proj, dtype=np.float32).T)
    gWp = np.asarray(L1_ffn_gate_W_packed)
    gSc = np.asarray(L1_ffn_gate_scales, dtype=np.float32)
    uWp = np.asarray(L1_ffn_up_W_packed)
    uSc = np.asarray(L1_ffn_up_scales, dtype=np.float32)
    dWp = np.asarray(L1_ffn_down_W_packed)
    dSc = np.asarray(L1_ffn_down_scales, dtype=np.float32)
    maps = []
    for i in range(NCORES):
        maps.append(dict(
            xs=xs[i], wpT=wpT, woT=woT, wrTe=wrTe, brh=brh, weT=weT, ebb=ebias,
            wqkvT=wqkvT, wprojT=wprojT,
            gWp=np.ascontiguousarray(gWp[i * 2048:(i + 1) * 2048]),
            gSc=np.ascontiguousarray(gSc[i * 2048:(i + 1) * 2048]),
            uWp=np.ascontiguousarray(uWp[i * 2048:(i + 1) * 2048]),
            uSc=np.ascontiguousarray(uSc[i * 2048:(i + 1) * 2048]),
            dWp=np.ascontiguousarray(dWp[:, i * 1024:(i + 1) * 1024]),
            dSc=np.ascontiguousarray(dSc[:, i * 64:(i + 1) * 64])))
    return maps


def kernel(L0_attn_x, L0_attn_w_project, L0_attn_w_out,
           L0_ffn_router_w, L0_ffn_router_b, L0_ffn_expert_w, L0_ffn_expert_b,
           L1_attn_w_qkv, L1_attn_w_proj,
           L1_ffn_gate_W_packed, L1_ffn_gate_scales,
           L1_ffn_up_W_packed, L1_ffn_up_scales,
           L1_ffn_down_W_packed, L1_ffn_down_scales):
    run = lambda nc, maps: run_bass_kernel_spmd(nc, maps, core_ids=list(range(NCORES))).results

    maps = fused_maps(L0_attn_x, L0_attn_w_project, L0_attn_w_out,
                      L0_ffn_router_w, L0_ffn_router_b, L0_ffn_expert_w, L0_ffn_expert_b,
                      L1_attn_w_qkv, L1_attn_w_proj,
                      L1_ffn_gate_W_packed, L1_ffn_gate_scales,
                      L1_ffn_up_W_packed, L1_ffn_up_scales,
                      L1_ffn_down_W_packed, L1_ffn_down_scales)
    res = run(build_fused(), maps)
    x4 = unshard_tokens([np.asarray(r["x4"]) for r in res])
    return np.ascontiguousarray(x4, dtype=np.float32)


